# version 01
# baseline (speedup 1.0000x reference)
import numpy as np
import ml_dtypes
from contextlib import ExitStack

import concourse.bass as bass
import concourse.mybir as mybir
from concourse.bass_utils import run_bass_kernel_spmd

F32 = mybir.dt.float32
BF16 = mybir.dt.bfloat16
AF = mybir.ActivationFunctionType
ALU = mybir.AluOpType
AX = mybir.AxisListType
BF = ml_dtypes.bfloat16

D = 4096
T = 1024
TH = 1027
HN = 128
PD = 64
G = 8
NCH = 8
INTER = 8192
FA = 10368
NKT = 32
NEG = -1.0e30
EPS_LN = 1e-5
EPS_RMS = 1e-5


class Ph:
    pass


def bc2(ap, n0, n1):
    return ap.unsqueeze(2).broadcast_to([ap.shape[0], n0, n1])


def build_nc(taps=(), upto=99):
    nc = bass.Bass("TRN2", target_bir_lowering=False)
    p = Ph()
    p.taps = set(taps)

    io = {}

    def inp(name, shape, dt=F32):
        io[name] = nc.dram_tensor(name, shape, dt, kind="ExternalInput").ap()

    inp("x_sl", [TH, D])
    inp("wiT_a", [D, FA])
    inp("wiT_g", [D, INTER])
    inp("woT", [INTER, D])
    inp("wfT", [D, D])
    inp("amat", [D, 32])
    inp("bmat", [32, D])
    inp("limes", [8, 32])
    inp("limT", [32, 8])
    inp("cwt", [128, 320])
    inp("cbt", [128, 80])
    inp("nwt", [128, 64])
    inp("alogc", [128, 1])
    inp("dtbc", [128, 1])
    inp("dvecc", [128, 1])
    inp("maskq", [128, 4])
    inp("qsel", [128, 4])
    inp("b8", [8, 1])
    inp("brow", [1, D])
    inp("mnegT", [128, 128])
    inp("idbf", [128, 128], BF16)
    inp("idf32", [128, 128])
    io["out"] = nc.dram_tensor("out", [T, D], F32, kind="ExternalOutput").ap()
    p.io = io

    p.xch_sp = nc.dram_tensor("xch_sp", [NCH, 128, INTER], BF16).ap()
    p.yfm_sp = nc.dram_tensor("yfm_sp", [64, 128, T], BF16).ap()
    p.gate_sp = nc.dram_tensor("gate_sp", [64, 128, T], BF16).ap()
    p.x1_sp = nc.dram_tensor("x1_sp", [T, D], F32).ap()
    p.x2p_sp = nc.dram_tensor("x2p_sp", [T, D], F32).ap()
    p.carry_dram = nc.dram_tensor("carry_dram", [128, INTER + 1], F32).ap()
    p.gath_dram = nc.dram_tensor("gath_dram", [512, INTER + 1], F32).ap()
    p.msc_dram = nc.dram_tensor("msc_dram", [2, 1], F32).ap()
    p.msc_out = nc.dram_tensor("msc_out", [2, 1], F32).ap()

    p.tap = {}

    def mktap(name, shape, dt):
        if name in p.taps:
            p.tap[name] = nc.dram_tensor(name, shape, dt, kind="ExternalOutput").ap()

    mktap("tap_ln1", [128, NKT * TH], BF16)
    mktap("tap_dt", [128, TH], F32)
    mktap("tap_acg", [128, T], F32)
    mktap("tap_bc", [128, 16 * T], BF16)
    mktap("tap_btok", [128, 64 * 128], BF16)
    mktap("tap_carry", [128, INTER], F32)
    mktap("tap_xch", [NCH, 128, INTER], BF16)
    mktap("tap_R", [128, INTER], F32)
    mktap("tap_y", [64, 128, T], BF16)
    mktap("tap_g3", [64, 128, T], BF16)
    mktap("tap_x1", [T, D], F32)
    mktap("tap_x2p", [T, D], F32)
    mktap("tap_hsl", [8, 512], F32)
    mktap("tap_dsl", [8, 512], F32)
    mktap("tap_z", [32, T], F32)
    mktap("tap_msc", [2, 1], F32)

    with ExitStack() as ctx:
        p.ctx = ctx
        p.nc = nc
        p.mksb = lambda n, s, d: ctx.enter_context(nc.sbuf_tensor(n, s, d))
        p.phase_sems = []

        def mksem(n):
            h = nc.alloc_semaphore(name=n)
            p.phase_sems.append(h)
            return h

        def free_sems():
            if p.phase_sems:
                nc.clear_and_free_semaphores(p.phase_sems)
                p.phase_sems = []
        p.mksem = mksem
        p.free_sems = free_sems
        p.vs = nc.alloc_semaphore(name="vself")
        p.ps_ = nc.alloc_semaphore(name="pself")
        p.vsn = 0
        p.psn = 0

        def vf(v, inst, sem=None, k=1):
            p.vsn += 1
            inst.then_inc(p.vs, 1)
            v.wait_ge(p.vs, p.vsn)
            if sem is not None:
                v.engine_nop().then_inc(sem, k)
            return inst

        def pf(g, inst, sem=None, k=1):
            p.psn += 1
            inst.then_inc(p.ps_, 1)
            g.wait_ge(p.ps_, p.psn)
            if sem is not None:
                g.engine_nop().then_inc(sem, k)
            return inst
        p.ss = nc.alloc_semaphore(name="sself")
        p.ssn = 0

        def sf(sc, inst, sem=None, k=1):
            p.ssn += 1
            inst.then_inc(p.ss, 1)
            sc.wait_ge(p.ss, p.ssn)
            if sem is not None:
                sc.nop().then_inc(sem, k)
            return inst
        p.vf = vf
        p.pf = pf
        p.sf = sf

        p.ones1 = p.mksb("ones1", [1, 128], F32)
        p.onescbf = p.mksb("onescbf", [128, 1], BF16)
        p.ones128 = p.mksb("ones128", [128, 128], F32)
        p.mneg = p.mksb("mneg_s", [128, 128], F32)
        p.idbf = p.mksb("idbf_s", [128, 128], BF16)
        p.idf32 = p.mksb("idf32_s", [128, 128], F32)
        p.cwt = p.mksb("cwt_s", [128, 320], F32)
        p.cbt = p.mksb("cbt_s", [128, 80], F32)
        p.nwt = p.mksb("nwt_s", [128, 64], F32)
        p.alog = p.mksb("alog_s", [128, 1], F32)
        p.aneg = p.mksb("aneg_s", [128, 1], F32)
        p.dtb = p.mksb("dtb_s", [128, 1], F32)
        p.dvec = p.mksb("dvec_s", [128, 1], F32)
        p.maskq = p.mksb("maskq_s", [128, 4], F32)
        p.qsel = p.mksb("qsel_s", [128, 4], F32)
        p.b8 = p.mksb("b8_s", [8, 1], F32)
        p.cst = {}
        for cv in (1e-5, -1.0, 2.0):
            ct = p.mksb(f"cst_{abs(cv):.0e}_{'n' if cv < 0 else 'p'}", [128, 1], F32)
            nc.const_aps.register(F32, cv, ct[:, 0:1])
            p.cst[cv] = ct

        p.scopeA = ExitStack()
        sbA = lambda n, s, d: p.scopeA.enter_context(nc.sbuf_tensor(n, s, d))
        p.scopeLn = ExitStack()
        p.ln1 = p.scopeLn.enter_context(
            nc.sbuf_tensor("ln1", [128, NKT * TH], BF16, side="right"))
        p.dtf = sbA("dtf", [128, TH], F32)
        p.acg = sbA("acg", [128, T], F32)
        p.bcfm = sbA("bcfm", [128, 16 * T], BF16)
        p.btok = sbA("btok", [128, 64 * 128], BF16)
        p.rbuf = sbA("rbuf", [128, INTER], F32)

        phase0(nc, p)
        p.free_sems()
        if upto >= 1:
            phase1(nc, p)
            tapblk(nc, p, [("tap_ln1", p.ln1)])
            p.free_sems()
        if upto >= 2:
            p.scope23 = ExitStack()
            p.dttok = p.scope23.enter_context(
                nc.sbuf_tensor("dttok", [128, NCH * 128], F32))
            p.wftok = p.scope23.enter_context(
                nc.sbuf_tensor("wftok", [128, NCH * 128], F32))
            phase2(nc, p)
            tapblk(nc, p, [("tap_dt", p.dtf), ("tap_acg", p.acg)])
            p.free_sems()
        if upto >= 3:
            phase3(nc, p)
            tapblk(nc, p, [("tap_bc", p.bcfm), ("tap_btok", p.btok),
                           ("tap_carry", p.rbuf)])
            p.free_sems()
        if upto >= 2:
            p.scope23.close()
        p.scopeLn.close()
        if upto >= 4:
            p.scopeB = ExitStack()
            p.rb = p.scopeB.enter_context(
                nc.sbuf_tensor("rb", [128, INTER], BF16))
            phase4(nc, p)
            tapblk(nc, p, [("tap_R", p.rbuf)])
            p.free_sems()
        if upto >= 5:
            phase5(nc, p)
            p.free_sems()
        if upto >= 4:
            p.scopeB.close()
        p.scopeA.close()
        if upto >= 6:
            p.scope67 = ExitStack()
            p.g3buf = p.scope67.enter_context(
                nc.sbuf_tensor("g3buf", [128, 64 * T], BF16))
            phase6(nc, p)
            tapblk(nc, p, [("tap_g3", p.g3buf)])
            p.free_sems()
        if upto >= 7:
            phase7(nc, p)
            p.free_sems()
        if upto >= 6:
            p.scope67.close()
        if upto >= 8:
            phase8(nc, p)
        finalphase(nc, p)
    return nc


def tapblk(nc, p, items):
    items = [(n, t) for (n, t) in items if n in p.tap]
    if not items:
        return
    sem = p.mksem("tap_" + items[0][0])
    with nc.Block() as blk:
        @blk.sync
        def _(s):
            n = 0
            for name, src in items:
                s.dma_start(out=p.tap[name], in_=src[:]).then_inc(sem, 16)
                n += 16
            s.wait_ge(sem, n)


def phase0(nc, p):
    io = p.io
    s_ld = p.mksem("p0_ld")
    with nc.Block() as blk:
        @blk.gpsimd
        def _(g):
            g.memset(p.ones1[:], 1.0)
            g.memset(p.onescbf[:], 1.0)
            g.memset(p.ones128[:], 1.0)
            for cv, ct in p.cst.items():
                g.memset(ct[:], cv)

        @blk.sync
        def _(s):
            n = 0
            for dst, src in [(p.mneg, "mnegT"), (p.idbf, "idbf"), (p.idf32, "idf32"),
                             (p.cwt, "cwt"), (p.cbt, "cbt"), (p.nwt, "nwt"),
                             (p.alog, "alogc"), (p.dtb, "dtbc"), (p.dvec, "dvecc"),
                             (p.maskq, "maskq"), (p.qsel, "qsel"),
                             (p.b8, "b8")]:
                s.dma_start(out=dst[:], in_=io[src]).then_inc(s_ld, 16)
                n += 16
            s.wait_ge(s_ld, n)
    with nc.Block() as blk:
        @blk.scalar
        def _(sc):
            p.sf(sc, sc.activation(p.aneg[:], p.alog[:], AF.Exp))
            sc.mul(p.aneg[:], p.aneg[:], -1.0)


def phase1(nc, p):
    io = p.io
    NT = 9
    with ExitStack() as c1:
        sb = lambda n, s, d: c1.enter_context(nc.sbuf_tensor(n, s, d))
        x0 = [sb(f"x0_{i}", [128, D], F32) for i in range(2)]
        sqs = p.bcfm[:, 0:2 * D].bitcast(F32)
        rbv = p.rbuf[:, 0:D].bitcast(BF16)
        xn = [rbv[:, 0:D], rbv[:, D:2 * D]]
        st = sb("lnst", [128, 8], F32)
        tp = [c1.enter_context(nc.psum_tensor(f"lntp_{i}", [128, 128], BF16))
              for i in range(2)]
        s_ld = [p.mksem("p1_lda"), p.mksem("p1_ldb")]
        s_sq = p.mksem("p1_sq")
        s_var = p.mksem("p1_var")
        s_sd = p.mksem("p1_sd")
        s_xn = p.mksem("p1_xn")
        s_t = p.mksem("p1_t")
        s_te = p.mksem("p1_te")

        with nc.Block() as blk:
            @blk.sync
            def _(s):
                for ti in range(NT):
                    rows = 128 if ti < 8 else 3
                    if ti >= 2:
                        s.wait_ge(s_xn, ti - 1)
                    s.dma_start(out=x0[ti % 2][0:rows, :],
                                in_=io["x_sl"][ti * 128: ti * 128 + rows, :]
                                ).then_inc(s_ld[ti % 2], 16)

            @blk.scalar
            def _(sc):
                for ti in range(NT):
                    sc.wait_ge(s_ld[ti % 2], (ti // 2 + 1) * 16)
                    sc.wait_ge(s_var, ti)
                    sc.activation(sqs, x0[ti % 2][:], AF.Square,
                                  accum_out=st[:, 1:2]).then_inc(s_sq, 1)
                    sc.wait_ge(s_var, ti + 1)
                    sc.activation(st[:, 6:7], st[:, 5:6], AF.Sqrt,
                                  bias=EPS_LN).then_inc(s_sd, 1)

            @blk.vector
            def _(v):
                for ti in range(NT):
                    v.wait_ge(s_ld[ti % 2], (ti // 2 + 1) * 16)
                    p.vf(v, v.reduce_sum(st[:, 0:1], x0[ti % 2][:], axis=AX.X))
                    v.wait_ge(s_sq, ti + 1)
                    v.tensor_scalar(st[:, 2:3], st[:, 0:1], 1.0 / D, None, ALU.mult)
                    p.vf(v, v.tensor_scalar(st[:, 3:4], st[:, 1:2], 1.0 / D,
                                            None, ALU.mult))
                    p.vf(v, v.tensor_mul(st[:, 4:5], st[:, 2:3], st[:, 2:3]))
                    v.tensor_sub(st[:, 5:6], st[:, 3:4], st[:, 4:5]).then_inc(s_var, 1)
                    v.wait_ge(s_sd, ti + 1)
                    p.vf(v, v.reciprocal(st[:, 7:8], st[:, 6:7]))
                    v.tensor_scalar(xn[ti % 2], x0[ti % 2][:], st[:, 2:3],
                                    st[:, 7:8], ALU.subtract,
                                    ALU.mult).then_inc(s_xn, 1)
                    cols = 128 if ti < 8 else 3
                    for k in range(NKT):
                        v.wait_ge(s_t, ti * NKT + k + 1)
                        v.tensor_copy(
                            p.ln1[:, k * TH + ti * 128: k * TH + ti * 128 + cols],
                            tp[k % 2][:, 0:cols]).then_inc(s_te, 1)

            @blk.tensor
            def _(t):
                for ti in range(NT):
                    t.wait_ge(s_xn, ti + 1)
                    for k in range(NKT):
                        idx = ti * NKT + k
                        if idx >= 2:
                            t.wait_ge(s_te, idx - 1)
                        t.transpose(tp[k % 2][:], xn[ti % 2][:, k * 128:(k + 1) * 128],
                                    p.idbf[:]).then_inc(s_t, 1)


def phase2(nc, p):
    io = p.io
    with ExitStack() as c2:
        sb = lambda n, s, d: c2.enter_context(nc.sbuf_tensor(n, s, d))
        wdt = [sb(f"wdt_{i}", [128, 4 * 128], BF16) for i in range(2)]
        dtraw = sb("dtraw", [128, TH], F32)
        t1 = sb("p2t1", [128, TH], F32)
        t2 = sb("p2t2", [128, TH], F32)
        wff = sb("wff", [128, T], F32)
        psdt = c2.enter_context(nc.psum_tensor("psdt", [128, 1536], F32))
        pst = c2.enter_context(nc.psum_tensor("p2pst", [128, 128], F32))
        s_w = [p.mksem("p2_wa"), p.mksem("p2_wb")]
        s_mm = p.mksem("p2_mm")
        s_dv = p.mksem("p2_dv")
        s_sc = p.mksem("p2_sc")
        s_t = p.mksem("p2_t")
        s_te = p.mksem("p2_te")

        with nc.Block() as blk:
            @blk.gpsimd
            def _(g):
                for kg in range(8):
                    if kg >= 2:
                        g.wait_ge(s_mm, kg - 1)
                    g.dma_start(
                        out=wdt[kg % 2][:],
                        in_=io["wiT_a"][kg * 512:(kg + 1) * 512, 10240:10368]
                        .rearrange("(k q) f -> q k f", q=128)
                        ).then_inc(s_w[kg % 2], 16)

            @blk.tensor
            def _(t):
                for kg in range(8):
                    t.wait_ge(s_w[kg % 2], (kg // 2 + 1) * 16)
                    for k4 in range(4):
                        k = kg * 4 + k4
                        lhs = wdt[kg % 2][:, k4 * 128:(k4 + 1) * 128]
                        st_, sp_ = (k == 0), (k == 31)
                        t.matmul(psdt[:, 0:512], lhs,
                                 p.ln1[:, k * TH: k * TH + 512], start=st_, stop=sp_)
                        t.matmul(psdt[:, 512:1024], lhs,
                                 p.ln1[:, k * TH + 512: k * TH + 1024],
                                 start=st_, stop=sp_)
                        mm = t.matmul(psdt[:, 1024:1027], lhs,
                                      p.ln1[:, k * TH + 1024: k * TH + 1027],
                                      start=st_, stop=sp_)
                    mm.then_inc(s_mm, 1)
                for c in range(NCH):
                    m = 2 * c
                    t.wait_ge(s_dv, 3)
                    if m >= 1:
                        t.wait_ge(s_te, m)
                    t.transpose(pst[:], p.dtf[:, 3 + c * 128: 3 + (c + 1) * 128],
                                p.idf32[:]).then_inc(s_t, 1)
                    t.wait_ge(s_sc, 5)
                    t.wait_ge(s_te, m + 1)
                    t.transpose(pst[:], wff[:, c * 128:(c + 1) * 128],
                                p.idf32[:]).then_inc(s_t, 1)

            @blk.scalar
            def _(sc):
                sc.wait_ge(s_mm, 8)
                p.sf(sc, sc.copy(dtraw[:], psdt[:, 0:1027]), s_sc, 1)
                p.sf(sc, sc.activation(t1[:], dtraw[:], AF.Abs, bias=p.dtb[:]))
                p.sf(sc, sc.activation(t1[:], t1[:], AF.Exp, scale=-1.0))
                sc.activation(t1[:], t1[:], AF.Ln, bias=1.0).then_inc(s_sc, 1)
                sc.wait_ge(s_dv, 4)
                sc.activation(wff[:], p.acg[:], AF.Exp, scale=-1.0,
                              bias=p.acg[:, T - 1:T]).then_inc(s_sc, 3)

            @blk.vector
            def _(v):
                v.wait_ge(s_sc, 2)
                p.vf(v, v.tensor_scalar(t2[:], dtraw[:], p.dtb[:], 0.0,
                                        ALU.add, ALU.max))
                p.vf(v, v.tensor_add(t2[:], t2[:], t1[:]))
                p.vf(v, v.tensor_scalar(p.dtf[:], t2[:], 100.0, None,
                                        ALU.min), s_dv, 3)
                p.vf(v, v.tensor_scalar(t1[:, 0:T], p.dtf[:, 3:TH], p.aneg[:],
                                        None, ALU.mult))
                v.tensor_tensor_scan(p.acg[:], t1[:, 0:T],
                                     p.ones128[:, 0:1].broadcast_to([128, T]),
                                     0.0, ALU.add, ALU.mult).then_inc(s_dv, 1)
                for c in range(NCH):
                    m = 2 * c
                    v.wait_ge(s_t, m + 1)
                    v.tensor_copy(p.dttok[:, c * 128:(c + 1) * 128],
                                  pst[:]).then_inc(s_te, 1)
                    v.wait_ge(s_t, m + 2)
                    v.tensor_copy(p.wftok[:, c * 128:(c + 1) * 128],
                                  pst[:]).then_inc(s_te, 1)


def phase3(nc, p):
    io = p.io
    with ExitStack() as c3:
        sb = lambda n, s, d: c3.enter_context(nc.sbuf_tensor(n, s, d))
        wbuf = [sb(f"p3w_{i}", [128, 4 * 256], BF16) for i in range(2)]
        raw = [sb(f"p3raw_{i}", [128, TH], F32) for i in range(2)]
        cacc = [sb(f"p3cv_{i}", [128, T], F32) for i in range(2)]
        hsb = [sb(f"p3hs_{i}", [128, T], BF16) for i in range(2)]
        xst = [sb(f"p3xs_{i}", [128, T], BF16) for i in range(2)]
        xds = [sb(f"p3xd_{i}", [128, 128], BF16) for i in range(2)]
        gst = [sb(f"p3gs_{i}", [128, T], BF16) for i in range(2)]
        psA = [c3.enter_context(nc.psum_tensor(f"p3psA_{i}", [128, 1536], F32))
               for i in range(2)]
        psT = c3.enter_context(nc.psum_tensor("p3psT", [128, 128], BF16))
        psC = c3.enter_context(nc.psum_tensor("p3psC", [128, 128], F32))
        s_w = [p.mksem("p3_wa"), p.mksem("p3_wb")]
        s_pc = p.mksem("p3_pc")
        s_mm = p.mksem("p3_mm")
        s_raw = p.mksem("p3_raw")
        s_cv = p.mksem("p3_cv")
        s_si = p.mksem("p3_si")
        s_t = p.mksem("p3_t")
        s_te = p.mksem("p3_te")
        s_xd = p.mksem("p3_xd")
        s_cy = p.mksem("p3_cy")
        s_ce = p.mksem("p3_ce")
        s_sp = [p.mksem("p3_spa"), p.mksem("p3_spb")]
        s_ge = p.mksem("p3_ge")
        s_gs = [p.mksem("p3_gsa"), p.mksem("p3_gsb")]

        tiles = ([("bc", i) for i in range(16)] + [("hs", i) for i in range(64)]
                 + [("gt", i) for i in range(64)])
        npair = 72

        def wsrc(kind, i):
            if kind == "gt":
                return io["wiT_g"], i * 128
            return io["wiT_a"], (8192 + i * 128) if kind == "bc" else i * 128

        def psum_free_wait(eng, ti):
            kind, i = tiles[ti]
            if kind == "gt":
                eng.wait_ge(s_ge, i + 1)
            else:
                eng.wait_ge(s_raw, ti + 1)

        with nc.Block() as blk:
            @blk.gpsimd
            def _(g):
                for pr in range(npair):
                    wt, base = wsrc(*tiles[2 * pr])
                    for kg in range(8):
                        pi = pr * 8 + kg
                        if pi >= 2:
                            g.wait_ge(s_pc, pi - 1)
                        g.dma_start(out=wbuf[pi % 2][:],
                                    in_=wt[kg * 512:(kg + 1) * 512,
                                           base:base + 256]
                                    .rearrange("(k q) f -> q k f", q=128)
                                    ).then_inc(s_w[pi % 2], 16)

            @blk.tensor
            def _(t):
                for pr in range(npair):
                    kind0 = tiles[2 * pr][0]
                    for kg in range(8):
                        pi = pr * 8 + kg
                        t.wait_ge(s_w[pi % 2], (pi // 2 + 1) * 16)
                        if kg == 0 and pr >= 1:
                            psum_free_wait(t, 2 * (pr - 1))
                            psum_free_wait(t, 2 * (pr - 1) + 1)
                        for k4 in range(4):
                            k = kg * 4 + k4
                            st_, sp_ = (k == 0), (k == 31)
                            for sub in range(2):
                                lhs = wbuf[pi % 2][:, k4 * 256 + sub * 128:
                                                   k4 * 256 + sub * 128 + 128]
                                if kind0 == "gt":
                                    t.matmul(psA[sub][:, 0:512], lhs,
                                             p.ln1[:, k * TH + 3: k * TH + 515],
                                             start=st_, stop=sp_)
                                    mm = t.matmul(psA[sub][:, 512:1024], lhs,
                                                  p.ln1[:, k * TH + 515: k * TH + 1027],
                                                  start=st_, stop=sp_)
                                else:
                                    t.matmul(psA[sub][:, 0:512], lhs,
                                             p.ln1[:, k * TH: k * TH + 512],
                                             start=st_, stop=sp_)
                                    t.matmul(psA[sub][:, 512:1024], lhs,
                                             p.ln1[:, k * TH + 512: k * TH + 1024],
                                             start=st_, stop=sp_)
                                    mm = t.matmul(psA[sub][:, 1024:1027], lhs,
                                                  p.ln1[:, k * TH + 1024: k * TH + 1027],
                                                  start=st_, stop=sp_)
                        mm.then_inc(s_pc, 1)
                    if pr == 7:
                        for c in range(NCH):
                            for gg in range(G):
                                j = c * 8 + gg
                                t.wait_ge(s_si, gg + 1)
                                if j >= 1:
                                    t.wait_ge(s_te, j)
                                t.transpose(
                                    psT[:],
                                    p.bcfm[:, gg * T + c * 128: gg * T + (c + 1) * 128],
                                    p.idbf[:]).then_inc(s_t, 1)
                    if 9 <= pr <= 40:
                        for ti in (2 * (pr - 1), 2 * (pr - 1) + 1):
                            kind, i = tiles[ti]
                            assert kind == "hs"
                            t.wait_ge(s_si, ti + 1)
                            for c in range(NCH):
                                j = 64 + i * 8 + c
                                t.wait_ge(s_te, j)
                                t.transpose(psT[:], hsb[i % 2][:, c * 128:(c + 1) * 128],
                                            p.idbf[:]).then_inc(s_t, 1)
                            for c in range(NCH):
                                t.wait_ge(s_xd, i * 8 + c + 1)
                                if c == 0:
                                    t.wait_ge(s_ce, i)
                                t.matmul(psC[:],
                                         p.btok[:, (c * 8 + i // 8) * 128:
                                                (c * 8 + i // 8 + 1) * 128],
                                         xds[(i * 8 + c) % 2][:],
                                         start=(c == 0), stop=(c == 7)
                                         ).then_inc(s_cy, 1)

            @blk.scalar
            def _(sc):
                for ti, (kind, i) in enumerate(tiles):
                    pr = ti // 2
                    sc.wait_ge(s_pc, (pr + 1) * 8)
                    if kind == "gt":
                        if i >= 2:
                            sc.wait_ge(s_gs[i % 2], (i // 2) * 16)
                        sc.copy(gst[i % 2][:], psA[ti % 2][:, 0:1024]
                                ).then_inc(s_ge, 1)
                        continue
                    if pr >= 1:
                        sc.wait_ge(s_cv, min(2 * pr, 80))
                    sc.copy(raw[ti % 2][:], psA[ti % 2][:, 0:1027]).then_inc(s_raw, 1)
                    sc.wait_ge(s_cv, ti + 1)
                    tci = (64 + i) if kind == "bc" else i
                    dst = (p.bcfm[:, i * T:(i + 1) * T] if kind == "bc"
                           else hsb[i % 2][:])
                    if kind == "hs" and i >= 2:
                        sc.wait_ge(s_t, 64 + (i - 1) * 8)
                    sc.activation(dst, cacc[ti % 2][:], AF.Silu,
                                  bias=p.cbt[:, tci: tci + 1]).then_inc(s_si, 1)

            @blk.vector
            def _(v):
                def conv(ti):
                    kind, i = tiles[ti]
                    v.wait_ge(s_raw, ti + 1)
                    if ti >= 2:
                        v.wait_ge(s_si, ti - 1)
                    tci = (64 + i) if kind == "bc" else i
                    r = raw[ti % 2]
                    ca = cacc[ti % 2]
                    p.vf(v, v.tensor_scalar(ca[:], r[:, 0:T],
                                            p.cwt[:, tci * 4: tci * 4 + 1],
                                            None, ALU.mult))
                    for k in range(1, 3):
                        p.vf(v, v.scalar_tensor_tensor(
                            ca[:], r[:, k:T + k],
                            p.cwt[:, tci * 4 + k: tci * 4 + k + 1],
                            ca[:], ALU.mult, ALU.add))
                    v.scalar_tensor_tensor(
                        ca[:], r[:, 3:T + 3],
                        p.cwt[:, tci * 4 + 3: tci * 4 + 4],
                        ca[:], ALU.mult, ALU.add).then_inc(s_cv, 1)

                for ti in range(16):
                    conv(ti)
                for c in range(NCH):
                    for gg in range(G):
                        j = c * 8 + gg
                        v.wait_ge(s_t, j + 1)
                        v.tensor_copy(p.btok[:, j * 128:(j + 1) * 128],
                                      psT[:]).then_inc(s_te, 1)
                conv(16)
                conv(17)
                for i in range(64):
                    if i + 2 < 64:
                        conv(16 + i + 2)
                    for c in range(NCH):
                        j = 64 + i * 8 + c
                        v.wait_ge(s_t, j + 1)
                        if i >= 2:
                            v.wait_ge(s_sp[i % 2], (i // 2) * 16)
                        v.tensor_tensor(
                            xst[i % 2][:, c * 128:(c + 1) * 128], psT[:],
                            bc2(p.dttok[:, c * 128 + 2 * i: c * 128 + 2 * i + 2], 2, 64),
                            ALU.mult).then_inc(s_te, 1)
                    for c in range(NCH):
                        idx = i * 8 + c
                        if idx >= 2:
                            v.wait_ge(s_cy, idx - 1)
                        v.tensor_tensor(
                            xds[idx % 2][:],
                            xst[i % 2][:, c * 128:(c + 1) * 128],
                            bc2(p.wftok[:, c * 128 + 2 * i: c * 128 + 2 * i + 2], 2, 64),
                            ALU.mult).then_inc(s_xd, 1)
                    v.wait_ge(s_cy, (i + 1) * 8)
                    v.tensor_copy(p.rbuf[:, i * 128:(i + 1) * 128],
                                  psC[:]).then_inc(s_ce, 1)

            @blk.sync
            def _(s):
                for i in range(64):
                    s.wait_ge(s_te, 64 + (i + 1) * 8)
                    s.dma_start(
                        out=p.xch_sp.rearrange("c q f -> q c f")
                        [:, :, i * 128:(i + 1) * 128],
                        in_=xst[i % 2][:].rearrange("q (c l) -> q c l", c=NCH)
                    ).then_inc(s_sp[i % 2], 16)
                for i in range(64):
                    s.wait_ge(s_ge, i + 1)
                    s.dma_start(out=p.gate_sp[i].opt(),
                                in_=gst[i % 2][:]).then_inc(s_gs[i % 2], 16)
                for sl in range(2):
                    s.wait_ge(s_sp[sl], 32 * 16)
                    s.wait_ge(s_gs[sl], 32 * 16)
                s.wait_ge(s_ce, 64)


def phase4(nc, p):
    s_a = p.mksem("p4_a")
    s_cc = p.mksem("p4_cc")
    s_g = p.mksem("p4_g")
    s_ga = p.mksem("p4_ga")
    s_m = p.mksem("p4_m")
    s_t = p.mksem("p4_t")
    s_e = p.mksem("p4_e")
    s_wb = p.mksem("p4_wb")
    with ExitStack() as c4:
        sb = lambda n, s, d: c4.enter_context(nc.sbuf_tensor(n, s, d))
        gbuf = sb("p4_gbuf", [128, INTER], F32)
        am = sb("p4_am", [128, 4], F32)
        sml = sb("p4_sml", [128, 24], F32)
        wmt = sb("p4_wmt", [128, 128], F32)
        pst = c4.enter_context(nc.psum_tensor("p4_pst", [128, 128], F32))

        with nc.Block() as blk:
            @blk.sync
            def _(s):
                s.dma_start(out=p.carry_dram[:, 0:INTER],
                            in_=p.rbuf[:]).then_inc(s_a, 16)
                s.dma_start(out=p.carry_dram[:, INTER:INTER + 1],
                            in_=p.acg[:, T - 1:T],
                            allow_slow_non_contiguous=True).then_inc(s_a, 16)
                s.wait_ge(s_a, 32)

        with nc.Block() as blk:
            @blk.gpsimd
            def _(g):
                g.collective_compute(
                    "AllGather", ALU.bypass,
                    replica_groups=[[0, 1, 2, 3], [4, 5, 6, 7]],
                    ins=[p.carry_dram.opt()], outs=[p.gath_dram.opt()],
                ).then_inc(s_cc)
                g.wait_ge(s_cc, 1)

        with nc.Block() as blk:
            @blk.sync
            def _(s):
                s.dma_start(out=am[:],
                            in_=p.gath_dram[:, INTER:INTER + 1]
                            .rearrange("(q h) o -> h (q o)", h=128),
                            allow_slow_non_contiguous=True).then_inc(s_ga, 16)
                for qp in range(4):
                    if qp >= 1:
                        s.wait_ge(s_m, qp)
                    s.dma_start(out=gbuf[:],
                                in_=p.gath_dram[qp * 128:(qp + 1) * 128, 0:INTER]
                                ).then_inc(s_g, 16)

            @blk.vector
            def _(v):
                v.wait_ge(s_ga, 16)
                p.vf(v, v.memset(sml[:, 0:1], 0.0))
                p.vf(v, v.tensor_copy(sml[:, 1:2], am[:, 0:1]))
                p.vf(v, v.tensor_add(sml[:, 2:3], sml[:, 1:2], am[:, 1:2]))
                p.vf(v, v.tensor_add(sml[:, 3:4], sml[:, 2:3], am[:, 2:3]))
                p.vf(v, v.tensor_add(sml[:, 4:8], sml[:, 0:4], am[:, 0:4]))
                p.vf(v, v.tensor_mul(sml[:, 12:16], p.qsel[:], sml[:, 0:4]))
                p.vf(v, v.reduce_sum(sml[:, 8:9], sml[:, 12:16], axis=AX.X))
                p.vf(v, v.tensor_scalar(sml[:, 12:16], sml[:, 4:8], sml[:, 8:9],
                                        None, ALU.subtract))
                v.tensor_scalar(sml[:, 16:20], sml[:, 12:16], -1.0, 0.0,
                                ALU.mult, ALU.min).then_inc(s_e, 1)
                v.wait_ge(s_e, 2)
                p.vf(v, v.tensor_mul(sml[:, 12:16], sml[:, 20:24], p.maskq[:]))
                v.memset(p.rbuf[:], 0.0)
                for qp in range(4):
                    v.tensor_scalar(wmt[:], p.ones128[:], sml[:, 12 + qp:13 + qp],
                                    None, ALU.mult).then_inc(s_wb, 1)
                    v.wait_ge(s_t, qp + 1)
                    v.wait_ge(s_g, (qp + 1) * 16)
                    p.vf(v, v.tensor_tensor(gbuf[:], gbuf[:], bc2(pst[:], 128, 64),
                                            ALU.mult))
                    v.tensor_add(p.rbuf[:], p.rbuf[:], gbuf[:]).then_inc(s_m, 1)

            @blk.scalar
            def _(sc):
                sc.wait_ge(s_e, 1)
                sc.activation(sml[:, 20:24], sml[:, 16:20], AF.Exp).then_inc(s_e, 1)

            @blk.tensor
            def _(t):
                for qp in range(4):
                    t.wait_ge(s_wb, qp + 1)
                    if qp >= 1:
                        t.wait_ge(s_m, qp)
                    t.transpose(pst[:], wmt[:], p.idf32[:]).then_inc(s_t, 1)


def phase5(nc, p):
    with ExitStack() as c5:
        sb = lambda n, s, d: c5.enter_context(nc.sbuf_tensor(n, s, d))
        xch = [sb(f"p5x_{i}", [128, INTER], BF16) for i in range(2)]
        pall = [sb(f"p5pa_{i}", [128, 2048], F32) for i in range(2)]
        lt = [sb(f"p5lt_{i}", [128, 128], F32) for i in range(16)]
        gt = [sb(f"p5gt_{i}", [128, 128], BF16) for i in range(16)]
        gts = [sb(f"p5gts_{i}", [128, 128], F32) for i in range(2)]
        tmp1 = sb("p5tmp1", [128, 512], F32)
        tmp2 = [sb(f"p5tmp2_{i}", [128, 512], F32) for i in range(2)]
        hsd = [sb(f"p5hsd_{i}", [128, 512], F32) for i in range(2)]
        ytk = [sb(f"p5ytk_{i}", [128, 1024], BF16) for i in range(2)]
        yst = [sb(f"p5yst_{i}", [128, 1024], BF16) for i in range(2)]
        xds = [sb(f"p5xd_{i}", [128, 1024], BF16) for i in range(2)]
        afm = sb("p5afm", [128, 128], F32)
        acum = sb("p5acum", [128, 128], F32)
        nacum = sb("p5nacum", [128, 128], F32)
        eex = sb("p5eex", [128, 128], F32)
        dfm = sb("p5dfm", [128, 128], F32)
        dtok = sb("p5dtok", [128, 128], F32)
        ezb = sb("p5ezb", [128, 128], F32)
        ezm = sb("p5ezm", [128, 128], F32)
        dtrec = sb("p5dtrec", [128, 128], F32)
        ddtf = sb("p5ddtf", [128, 128], F32)
        ddtm = sb("p5ddtm", [128, 128], F32)
        smlc = sb("p5smlc", [128, 4], F32)
        psb = [c5.enter_context(nc.psum_tensor(f"p5psb_{i}", [128, 128], F32))
               for i in range(2)]
        psa = psb[0]
        psbt = c5.enter_context(nc.psum_tensor("p5psbt", [128, 128], BF16))
        psg = c5.enter_context(nc.psum_tensor("p5psg", [128, 128], F32))
        psyd = [c5.enter_context(nc.psum_tensor(f"p5psyd_{i}", [128, 512], F32))
                for i in range(2)]
        psyo = [c5.enter_context(nc.psum_tensor(f"p5psyo_{i}", [128, 512], F32))
                for i in range(2)]
        s_xl = [p.mksem("p5_xla"), p.mksem("p5_xlb")]
        s_pv = p.mksem("p5_pv")
        s_pa5 = p.mksem("p5_pa5")
        s_pt = p.mksem("p5_pt")
        s_pp = p.mksem("p5_pp")
        s_bt = p.mksem("p5_bt")
        s_lt = p.mksem("p5_lt")
        s_m1 = p.mksem("p5_m1")
        s_ge = p.mksem("p5_ge")
        s_gt = p.mksem("p5_gt")
        s_ymm = p.mksem("p5_ymm")
        s_p1 = p.mksem("p5_p1")
        s_p2 = p.mksem("p5_p2")
        s_hd = p.mksem("p5_hd")
        s_ye = p.mksem("p5_ye")
        s_yt = p.mksem("p5_yt")
        s_ys = p.mksem("p5_ys")
        s_sp = [p.mksem("p5_spa"), p.mksem("p5_spb")]
        s_rb = p.mksem("p5_rb")
        s_rz = p.mksem("p5_rz")
        s_xd = p.mksem("p5_xd")
        s_sm = p.mksem("p5_sm")
        s_se = p.mksem("p5_se")

        NPV = 10

        with nc.Block() as blk:
            @blk.sync
            def _(s):
                for c in range(NCH):
                    if c >= 2:
                        s.wait_ge(s_se, (c - 1) * 8)
                    s.dma_start(out=xch[c % 2][:],
                                in_=p.xch_sp[c].opt()).then_inc(s_xl[c % 2], 16)
                    for gg in range(G):
                        cg = c * 8 + gg
                        s.wait_ge(s_ys, (cg + 1) * 8)
                        s.dma_start(
                            out=p.yfm_sp.rearrange("j q t -> q j t")
                            [:, gg * 8:(gg + 1) * 8, c * 128:(c + 1) * 128],
                            in_=yst[cg % 2][:].rearrange("q (j l) -> q j l", j=8)
                        ).then_inc(s_sp[cg % 2], 16)
                for sl in range(2):
                    s.wait_ge(s_sp[sl], 32 * 16)

            @blk.vector
            def _(v):
                for c in range(NCH):
                    cs, ce = c * 128, (c + 1) * 128
                    b = c * NPV
                    if c > 0:
                        v.wait_ge(s_se, c * 8)
                        v.wait_ge(s_ymm, c * 128)
                        v.wait_ge(s_xd, c * 8)
                        v.wait_ge(s_ye, c * 16)
                        v.wait_ge(s_pt, c * 4)
                    v.tensor_copy(p.rb[:], p.rbuf[:]).then_inc(s_rb, 1)
                    if c == 0:
                        p.vf(v, v.memset(smlc[:, 0:1], 0.0))
                    else:
                        p.vf(v, v.tensor_copy(smlc[:, 0:1], p.acg[:, cs - 1:cs]))
                    v.tensor_scalar(afm[:], p.acg[:, cs:ce], smlc[:, 0:1],
                                    None, ALU.subtract).then_inc(s_pv, 1)
                    v.wait_ge(s_pt, c * 4 + 1)
                    p.vf(v, v.tensor_copy(acum[:], psa[:]), s_pv, 1)
                    v.tensor_scalar(nacum[:], acum[:], -1.0, None,
                                    ALU.mult).then_inc(s_pv, 1)
                    v.tensor_scalar(smlc[:, 1:2], p.acg[:, ce - 1:ce], smlc[:, 0:1],
                                    None, ALU.subtract).then_inc(s_pv, 1)
                    v.wait_ge(s_pa5, c * 3 + 2)
                    v.tensor_scalar(ezb[:], p.ones128[:], smlc[:, 2:3],
                                    None, ALU.mult).then_inc(s_pv, 1)
                    v.wait_ge(s_pt, c * 4 + 2)
                    v.tensor_copy(dtok[:], psa[:]).then_inc(s_pv, 1)
                    v.wait_ge(s_pt, c * 4 + 3)
                    v.tensor_copy(ezm[:], psa[:]).then_inc(s_pv, 1)
                    p.vf(v, v.reciprocal(dtrec[:],
                                         p.dtf[:, 3 + cs: 3 + ce]), s_pv, 1)
                    v.tensor_scalar(ddtf[:], dtrec[:], p.dvec[:], None,
                                    ALU.mult).then_inc(s_pv, 1)
                    v.wait_ge(s_pt, c * 4 + 4)
                    v.tensor_copy(ddtm[:], psa[:]).then_inc(s_pv, 1)
                    def yst_evac(cg2):
                        for j in range(8):
                            v.wait_ge(s_yt, cg2 * 8 + j + 1)
                            if cg2 >= 2:
                                v.wait_ge(s_sp[cg2 % 2], ((cg2 - 2) // 2 + 1) * 16)
                            v.tensor_copy(
                                yst[cg2 % 2][:, j * 128:(j + 1) * 128],
                                psbt[:]).then_inc(s_ys, 1)

                    for gg in range(G):
                        cg = c * 8 + gg
                        if gg >= 1:
                            yst_evac(cg - 1)
                        for half in range(2):
                            hg = cg * 2 + half
                            v.wait_ge(s_ymm, (hg + 1) * 8)
                            if hg >= 2:
                                v.wait_ge(s_ye, hg - 1)
                            for hh8 in range(8):
                                h = gg * 16 + half * 8 + hh8
                                inst = v.tensor_scalar(
                                    tmp1[:, hh8 * 64:(hh8 + 1) * 64],
                                    psyo[hg % 2][:, hh8 * 64:(hh8 + 1) * 64],
                                    eex[:, h:h + 1], None, ALU.mult)
                            p.vf(v, inst)
                            p.vf(v, v.tensor_add(tmp2[hg % 2][:], tmp1[:],
                                                 psyd[hg % 2][:]), s_p2, 1)
                    yst_evac(c * 8 + 7)
                    v.wait_ge(s_rb, c + 1)
                    p.vf(v, v.tensor_tensor(p.rbuf[:], p.rbuf[:],
                                            bc2(ezm[:], 128, 64),
                                            ALU.mult), s_rz, 1)
                    for gg in range(G):
                        cg = c * 8 + gg
                        v.wait_ge(s_sm, (cg + 1) * 16)
                        v.tensor_add(p.rbuf[:, gg * 1024: gg * 1024 + 512],
                                     p.rbuf[:, gg * 1024: gg * 1024 + 512],
                                     psyd[gg % 2][:])
                        v.tensor_add(p.rbuf[:, gg * 1024 + 512:(gg + 1) * 1024],
                                     p.rbuf[:, gg * 1024 + 512:(gg + 1) * 1024],
                                     psyo[gg % 2][:]).then_inc(s_se, 1)

            @blk.scalar
            def _(sc):
                for c in range(NCH):
                    cs, ce = c * 128, (c + 1) * 128
                    b = c * NPV
                    if c > 0:
                        sc.wait_ge(s_pt, (c - 1) * 4 + 2)
                    sc.activation(dfm[:], p.acg[:, cs:ce], AF.Exp, scale=-1.0,
                                  bias=p.acg[:, ce - 1:ce]).then_inc(s_pa5, 1)
                    sc.wait_ge(s_pv, b + 4)
                    sc.activation(smlc[:, 2:3], smlc[:, 1:2],
                                  AF.Exp).then_inc(s_pa5, 1)
                    sc.wait_ge(s_pv, b + 2)
                    if c > 0:
                        sc.wait_ge(s_p2, c * 16)
                    sc.activation(eex[:], acum[:], AF.Exp).then_inc(s_pa5, 1)
                    for gg in range(G):
                        cg = c * 8 + gg
                        sc.wait_ge(s_m1, cg + 1)
                        if cg >= 2:
                            sc.wait_ge(s_gt, (cg - 1) * 16)
                        sc.copy(gts[cg % 2][:], psg[:]).then_inc(s_ge, 1)
                        for hh in range(16):
                            hidx = cg * 16 + hh
                            sc.wait_ge(s_bt, hidx + 1)
                            if cg >= 1:
                                sc.wait_ge(s_gt, (cg - 1) * 16 + hh + 1)
                            sc.activation(lt[hh][:], psb[hh % 2][:], AF.Exp,
                                          bias=nacum[:, gg * 16 + hh: gg * 16 + hh + 1]
                                          ).then_inc(s_lt, 1)

            @blk.gpsimd
            def _(g):
                for c in range(NCH):
                    b = c * NPV
                    for gg in range(G):
                        cg = c * 8 + gg
                        g.wait_ge(s_pv, b + 2)
                        if cg >= 2:
                            g.wait_ge(s_bt, (cg - 1) * 16)
                        g.tensor_tensor(
                            pall[cg % 2][:],
                            bc2(acum[:, gg * 16:(gg + 1) * 16], 16, 128),
                            p.mneg[:].unsqueeze(1).broadcast_to([128, 16, 128]),
                            ALU.add).then_inc(s_pp, 1)
                        for hh in range(16):
                            hidx = cg * 16 + hh
                            g.wait_ge(s_lt, hidx + 1)
                            g.wait_ge(s_ge, cg + 1)
                            if cg >= 1:
                                g.wait_ge(s_ymm, (cg - 1) * 16 + hh + 1)
                            g.tensor_tensor(gt[hh][:], gts[cg % 2][:],
                                            lt[hh][:], ALU.mult).then_inc(s_gt, 1)
                        for half in range(2):
                            hgp = cg * 2 + half
                            g.wait_ge(s_pv, b + 10)
                            g.wait_ge(s_xl[c % 2], (c // 2 + 1) * 16)
                            if hgp >= 2:
                                g.wait_ge(s_ye, hgp - 1)
                            sp = gg * 1024 + half * 512
                            h0 = gg * 16 + half * 8
                            p.pf(g, g.tensor_tensor(hsd[hgp % 2][:],
                                                    xch[c % 2][:, sp:sp + 512],
                                                    bc2(ddtm[:, h0:h0 + 8], 8, 64),
                                                    ALU.mult), s_hd, 1)
                            g.wait_ge(s_p2, hgp + 1)
                            if cg >= 2 and half == 0:
                                g.wait_ge(s_yt, (cg - 1) * 8)
                            g.tensor_tensor(ytk[cg % 2][:, half * 512:(half + 1) * 512],
                                            tmp2[hgp % 2][:], hsd[hgp % 2][:],
                                            ALU.add).then_inc(s_ye, 1)
                    for gg in range(G):
                        cg = c * 8 + gg
                        g.wait_ge(s_pv, b + 6)
                        if cg >= 2:
                            g.wait_ge(s_sm, (cg - 1) * 16)
                        g.tensor_tensor(xds[gg % 2][:],
                                        xch[c % 2][:, gg * 1024:(gg + 1) * 1024],
                                        bc2(dtok[:, gg * 16:(gg + 1) * 16], 16, 64),
                                        ALU.mult).then_inc(s_xd, 1)

            @blk.tensor
            def _(t):
                for c in range(NCH):
                    cs, ce = c * 128, (c + 1) * 128
                    b = c * NPV
                    t.wait_ge(s_pv, b + 1)
                    if c >= 1:
                        t.wait_ge(s_lt, c * 128)
                    t.transpose(psa[:], afm[:], p.idf32[:]).then_inc(s_pt, 1)
                    t.wait_ge(s_pa5, c * 3 + 1)
                    t.wait_ge(s_pv, b + 2)
                    t.transpose(psa[:], dfm[:], p.idf32[:]).then_inc(s_pt, 1)
                    t.wait_ge(s_pv, b + 6)
                    t.transpose(psa[:], ezb[:], p.idf32[:]).then_inc(s_pt, 1)
                    t.wait_ge(s_pv, b + 9)
                    t.transpose(psa[:], ddtf[:], p.idf32[:]).then_inc(s_pt, 1)
                    for gg in range(G):
                        cg = c * 8 + gg
                        if cg >= 1:
                            t.wait_ge(s_ge, cg)
                        t.matmul(psg[:], p.bcfm[:, gg * T + cs: gg * T + ce],
                                 p.bcfm[:, (8 + gg) * T + cs:(8 + gg) * T + ce],
                                 start=True, stop=True).then_inc(s_m1, 1)
                        for hh in range(16):
                            hidx = cg * 16 + hh
                            t.wait_ge(s_pp, cg + 1)
                            if gg == 0 and hh == 0:
                                t.wait_ge(s_pv, b + 10)
                            if hidx >= 2:
                                t.wait_ge(s_lt, hidx - 1)
                            t.transpose(psb[hh % 2][:],
                                        pall[cg % 2][:, hh * 128:(hh + 1) * 128],
                                        p.idf32[:]).then_inc(s_bt, 1)
                        for hh in range(16):
                            hidx = cg * 16 + hh
                            hg = cg * 2 + hh // 8
                            h = gg * 16 + hh
                            t.wait_ge(s_gt, hidx + 1)
                            t.wait_ge(s_xl[c % 2], (c // 2 + 1) * 16)
                            if hh % 8 == 0 and hg >= 2:
                                t.wait_ge(s_p2, hg - 1)
                            sl = slice((hh % 8) * 64, (hh % 8 + 1) * 64)
                            t.matmul(psyd[hg % 2][:, sl], gt[hh][:],
                                     xch[c % 2][:, h * 64:(h + 1) * 64],
                                     start=True, stop=True)
                            t.wait_ge(s_rb, c + 1)
                            t.matmul(psyo[hg % 2][:, sl],
                                     p.bcfm[:, (8 + gg) * T + cs:(8 + gg) * T + ce],
                                     p.rb[:, h * 64:(h + 1) * 64],
                                     start=True, stop=True).then_inc(s_ymm, 1)
                        for j in range(8):
                            cgj = cg * 8 + j
                            t.wait_ge(s_ye, cg * 2 + 1 + (1 if j >= 4 else 0))
                            if cgj >= 1:
                                t.wait_ge(s_ys, cgj)
                            t.transpose(psbt[:],
                                        ytk[cg % 2][:, j * 128:(j + 1) * 128],
                                        p.idbf[:]).then_inc(s_yt, 1)
                    for gg in range(G):
                        cg = c * 8 + gg
                        t.wait_ge(s_xd, cg + 1)
                        t.wait_ge(s_rz, c + 1)
                        t.wait_ge(s_p2, c * 16 + 16)
                        if cg >= 2:
                            t.wait_ge(s_se, cg - 1)
                        for hh in range(16):
                            dstp = psyd[gg % 2] if hh < 8 else psyo[gg % 2]
                            sl = slice((hh % 8) * 64, (hh % 8 + 1) * 64)
                            t.matmul(dstp[:, sl],
                                     p.btok[:, (c * 8 + gg) * 128:
                                            (c * 8 + gg + 1) * 128],
                                     xds[gg % 2][:, hh * 64:(hh + 1) * 64],
                                     start=True, stop=True).then_inc(s_sm, 1)

    if "tap_xch" in p.tap or "tap_y" in p.tap or "tap_R" in p.tap:
        sem = p.mksem("p5_tap")
        with nc.Block() as blk:
            @blk.sync
            def _(s):
                n = 0
                if "tap_xch" in p.tap:
                    s.dma_start(out=p.tap["tap_xch"], in_=p.xch_sp).then_inc(sem, 16)
                    n += 16
                if "tap_y" in p.tap:
                    s.dma_start(out=p.tap["tap_y"], in_=p.yfm_sp).then_inc(sem, 16)
                    n += 16
                if "tap_R" in p.tap:
                    s.dma_start(out=p.tap["tap_R"], in_=p.rbuf[:]).then_inc(sem, 16)
                    n += 16
                s.wait_ge(sem, n)


def phase6(nc, p):
    with ExitStack() as c6:
        sb = lambda n, s, d: c6.enter_context(nc.sbuf_tensor(n, s, d))
        gbx = [sb(f"p6gb_{i}", [128, T], BF16) for i in range(2)]
        sgb = [sb(f"p6sg_{i}", [128, T], BF16) for i in range(2)]
        yfm = [sb(f"p6yf_{i}", [128, T], BF16) for i in range(2)]
        sqb = [sb(f"p6sq_{i}", [128, T], BF16) for i in range(2)]
        rsm = sb("p6rs", [1, 2 * T], F32)
        psV = c6.enter_context(nc.psum_tensor("p6psV", [1, T], F32))
        psR = c6.enter_context(nc.psum_tensor("p6psR", [128, T], F32))
        s_gl = [p.mksem("p6_gla"), p.mksem("p6_glb")]
        s_yl = [p.mksem("p6_yla"), p.mksem("p6_ylb")]
        s_sg = p.mksem("p6_sg")
        s_g2 = p.mksem("p6_g2")
        s_sq = p.mksem("p6_sq")
        s_vm = p.mksem("p6_vm")
        s_vr = p.mksem("p6_vr")
        s_sd = p.mksem("p6_sd")
        s_rc = p.mksem("p6_rc")
        s_rm = p.mksem("p6_rm")
        s_sc6 = p.mksem("p6_sc")

        with nc.Block() as blk:
            @blk.sync
            def _(s):
                for ft in range(64):
                    if ft >= 2:
                        s.wait_ge(s_sg, ft - 1)
                    s.dma_start(out=gbx[ft % 2][:],
                                in_=p.gate_sp[ft].opt()).then_inc(s_gl[ft % 2], 16)
                    if ft >= 2:
                        s.wait_ge(s_g2, ft - 1)
                    s.dma_start(out=yfm[ft % 2][:],
                                in_=p.yfm_sp[ft].opt()).then_inc(s_yl[ft % 2], 16)

            @blk.scalar
            def _(sc):
                for ft in range(64):
                    sc.wait_ge(s_gl[ft % 2], (ft // 2 + 1) * 16)
                    if ft >= 2:
                        sc.wait_ge(s_g2, ft - 1)
                    sc.activation(sgb[ft % 2][:], gbx[ft % 2][:],
                                  AF.Silu).then_inc(s_sg, 1)
                    sc.wait_ge(s_g2, ft + 1)
                    if ft >= 2:
                        sc.wait_ge(s_vm, ft - 1)
                    sc.activation(sqb[ft % 2][:], p.g3buf[:, ft * T:(ft + 1) * T],
                                  AF.Square).then_inc(s_sq, 1)
                    if ft % 8 == 7:
                        grp = ft // 8
                        sc.wait_ge(s_vr, grp + 1)
                        sc.activation(rsm[:, 0:T], rsm[:, 0:T], AF.Sqrt,
                                      bias=EPS_RMS).then_inc(s_sd, 1)

            @blk.tensor
            def _(t):
                for ft in range(64):
                    st_, sp_ = (ft % 8 == 0), (ft % 8 == 7)
                    t.wait_ge(s_sq, ft + 1)
                    if st_:
                        t.wait_ge(s_vr, ft // 8)
                    t.matmul(psV[:, 0:512], p.onescbf[:], sqb[ft % 2][:, 0:512],
                             start=st_, stop=sp_)
                    t.matmul(psV[:, 512:1024], p.onescbf[:], sqb[ft % 2][:, 512:1024],
                             start=st_, stop=sp_).then_inc(s_vm, 1)
                    if sp_:
                        grp = ft // 8
                        t.wait_ge(s_rc, grp + 1)
                        if grp >= 1:
                            t.wait_ge(s_sc6, grp * 8)
                        t.matmul(psR[:, 0:512], p.ones1[:], rsm[:, T: T + 512],
                                 start=True, stop=True)
                        t.matmul(psR[:, 512:1024], p.ones1[:], rsm[:, T + 512: 2 * T],
                                 start=True, stop=True).then_inc(s_rm, 1)

            @blk.vector
            def _(v):
                for ft in range(64):
                    v.wait_ge(s_sg, ft + 1)
                    v.wait_ge(s_yl[ft % 2], (ft // 2 + 1) * 16)
                    v.tensor_tensor(p.g3buf[:, ft * T:(ft + 1) * T], yfm[ft % 2][:],
                                    sgb[ft % 2][:], ALU.mult).then_inc(s_g2, 1)
                    if ft % 8 == 7:
                        grp = ft // 8
                        v.wait_ge(s_vm, (grp + 1) * 8)
                        v.tensor_scalar(rsm[:, 0:T], psV[:], 1.0 / 1024.0, None,
                                        ALU.mult).then_inc(s_vr, 1)
                        v.wait_ge(s_sd, grp + 1)
                        v.reciprocal(rsm[:, T:2 * T], rsm[:, 0:T]).then_inc(s_rc, 1)
                        v.wait_ge(s_rm, grp + 1)
                        for ft2 in range(grp * 8, grp * 8 + 8):
                            v.scalar_tensor_tensor(
                                p.g3buf[:, ft2 * T:(ft2 + 1) * T],
                                p.g3buf[:, ft2 * T:(ft2 + 1) * T],
                                p.nwt[:, ft2:ft2 + 1], psR[:],
                                ALU.mult, ALU.mult).then_inc(s_sc6, 1)


def phase7(nc, p):
    io = p.io
    with ExitStack() as c7:
        sb = lambda n, s, d: c7.enter_context(nc.sbuf_tensor(n, s, d))
        wob = [sb(f"p7w_{i}", [128, 4 * 512], BF16) for i in range(2)]
        x0s = [sb(f"p7x0_{i}", [128, 512], F32) for i in range(2)]
        x1s = [sb(f"p7x1_{i}", [128, 512], F32) for i in range(2)]
        pso = [c7.enter_context(nc.psum_tensor(f"p7ps_{i}", [128, 512], F32))
               for i in range(8)]
        s_w = [p.mksem("p7_wa"), p.mksem("p7_wb")]
        s_pc = p.mksem("p7_pc")
        s_mm = p.mksem("p7_mm")
        s_x0 = [p.mksem("p7_x0a"), p.mksem("p7_x0b")]
        s_ev = p.mksem("p7_ev")
        s_st = [p.mksem("p7_sta"), p.mksem("p7_stb")]

        with nc.Block() as blk:
            @blk.gpsimd
            def _(g):
                for o in range(8):
                    for ig in range(16):
                        pi = o * 16 + ig
                        if pi >= 2:
                            g.wait_ge(s_pc, pi - 1)
                        g.dma_start(out=wob[pi % 2][:],
                                    in_=io["woT"][ig * 512:(ig + 1) * 512,
                                                  o * 512:(o + 1) * 512]
                                    .rearrange("(k q) f -> q k f", q=128)
                                    ).then_inc(s_w[pi % 2], 16)

            @blk.tensor
            def _(t):
                for o in range(8):
                    for ig in range(16):
                        pi = o * 16 + ig
                        t.wait_ge(s_w[pi % 2], (pi // 2 + 1) * 16)
                        if ig == 0 and o >= 1:
                            t.wait_ge(s_ev, o * 8)
                        for i4 in range(4):
                            i = ig * 4 + i4
                            st_, sp_ = (i == 0), (i == 63)
                            for tt in range(8):
                                mm = t.matmul(pso[tt][:],
                                              p.g3buf[:, i * T + tt * 128:
                                                      i * T + (tt + 1) * 128],
                                              wob[pi % 2][:, i4 * 512:(i4 + 1) * 512],
                                              start=st_, stop=sp_)
                        mm.then_inc(s_pc, 1)

            @blk.sync
            def _(s):
                for o in range(8):
                    for tt in range(8):
                        idx = o * 8 + tt
                        if idx >= 2:
                            s.wait_ge(s_ev, idx - 1)
                        s.dma_start(out=x0s[tt % 2][:],
                                    in_=io["x_sl"][3 + tt * 128: 3 + (tt + 1) * 128,
                                                   o * 512:(o + 1) * 512]
                                    ).then_inc(s_x0[idx % 2], 16)
                        s.wait_ge(s_ev, idx + 1)
                        s.dma_start(out=p.x1_sp[tt * 128:(tt + 1) * 128,
                                                o * 512:(o + 1) * 512],
                                    in_=x1s[tt % 2][:]).then_inc(s_st[idx % 2], 16)
                s.wait_ge(s_st[0], 32 * 16); s.wait_ge(s_st[1], 32 * 16)

            @blk.vector
            def _(v):
                for o in range(8):
                    v.wait_ge(s_pc, (o + 1) * 16)
                    for tt in range(8):
                        idx = o * 8 + tt
                        v.wait_ge(s_x0[idx % 2], (idx // 2 + 1) * 16)
                        if idx >= 2:
                            v.wait_ge(s_st[idx % 2], (idx // 2) * 16)
                        v.tensor_add(x1s[tt % 2][:], x0s[tt % 2][:],
                                     pso[tt][:]).then_inc(s_ev, 1)

    if "tap_x1" in p.tap:
        sem = p.mksem("p7_tap")
        with nc.Block() as blk:
            @blk.sync
            def _(s):
                s.dma_start(out=p.tap["tap_x1"], in_=p.x1_sp).then_inc(sem, 16)
                s.wait_ge(sem, 16)


def phase8(nc, p):
    io = p.io
    scope8 = ExitStack()
    sb8 = lambda n, s, d: scope8.enter_context(nc.sbuf_tensor(n, s, d))
    ln2 = sb8("ln2", [128, NKT * T], BF16)
    biasm = sb8("biasm", [128, D], F32)
    browb = sb8("browb", [1, D], F32)
    zsb = sb8("zsb", [32, T], F32)
    zbf = sb8("zbf", [32, T], BF16)
    hsl = sb8("hsl", [8, 512], F32)
    dsl = sb8("dsl", [8, 512], F32)

    with ExitStack() as c8:
        sb = lambda n, s, d: c8.enter_context(nc.sbuf_tensor(n, s, d))
        x1t = [sb(f"p8x1_{i}", [128, D], F32) for i in range(2)]
        sqs = sb("p8sq", [128, D], F32)
        xn = [sb(f"p8xn_{i}", [128, D], BF16) for i in range(2)]
        st = sb("p8st", [128, 8], F32)
        tp = [c8.enter_context(nc.psum_tensor(f"p8tp_{i}", [128, 128], BF16))
              for i in range(2)]
        s_ld = [p.mksem("p8_lda"), p.mksem("p8_ldb")]
        s_sq = p.mksem("p8_sq")
        s_var = p.mksem("p8_var")
        s_sd = p.mksem("p8_sd")
        s_xn = p.mksem("p8_xn")
        s_t = p.mksem("p8_t")
        s_te = p.mksem("p8_te")

        with nc.Block() as blk:
            @blk.sync
            def _(s):
                for ti in range(8):
                    if ti >= 2:
                        s.wait_ge(s_xn, ti - 1)
                    s.dma_start(out=x1t[ti % 2][:],
                                in_=p.x1_sp[ti * 128:(ti + 1) * 128, :]
                                ).then_inc(s_ld[ti % 2], 16)

            @blk.scalar
            def _(sc):
                for ti in range(8):
                    sc.wait_ge(s_ld[ti % 2], (ti // 2 + 1) * 16)
                    sc.wait_ge(s_var, ti)
                    sc.activation(sqs[:], x1t[ti % 2][:], AF.Square,
                                  accum_out=st[:, 1:2]).then_inc(s_sq, 1)
                    sc.wait_ge(s_var, ti + 1)
                    sc.activation(st[:, 6:7], st[:, 5:6], AF.Sqrt,
                                  bias=EPS_LN).then_inc(s_sd, 1)

            @blk.vector
            def _(v):
                for ti in range(8):
                    v.wait_ge(s_ld[ti % 2], (ti // 2 + 1) * 16)
                    p.vf(v, v.reduce_sum(st[:, 0:1], x1t[ti % 2][:], axis=AX.X))
                    v.wait_ge(s_sq, ti + 1)
                    v.tensor_scalar(st[:, 2:3], st[:, 0:1], 1.0 / D, None, ALU.mult)
                    p.vf(v, v.tensor_scalar(st[:, 3:4], st[:, 1:2], 1.0 / D,
                                            None, ALU.mult))
                    p.vf(v, v.tensor_mul(st[:, 4:5], st[:, 2:3], st[:, 2:3]))
                    v.tensor_sub(st[:, 5:6], st[:, 3:4], st[:, 4:5]).then_inc(s_var, 1)
                    v.wait_ge(s_sd, ti + 1)
                    p.vf(v, v.reciprocal(st[:, 7:8], st[:, 6:7]))
                    v.tensor_scalar(xn[ti % 2][:], x1t[ti % 2][:], st[:, 2:3],
                                    st[:, 7:8], ALU.subtract,
                                    ALU.mult).then_inc(s_xn, 1)
                    for k in range(NKT):
                        v.wait_ge(s_t, ti * NKT + k + 1)
                        v.tensor_copy(ln2[:, k * T + ti * 128: k * T + (ti + 1) * 128],
                                      tp[k % 2][:]).then_inc(s_te, 1)

            @blk.tensor
            def _(t):
                for ti in range(8):
                    t.wait_ge(s_xn, ti + 1)
                    for k in range(NKT):
                        idx = ti * NKT + k
                        if idx >= 2:
                            t.wait_ge(s_te, idx - 1)
                        t.transpose(tp[k % 2][:], xn[ti % 2][:, k * 128:(k + 1) * 128],
                                    p.idbf[:]).then_inc(s_t, 1)

    with ExitStack() as c8:
        sb = lambda n, s, d: c8.enter_context(nc.sbuf_tensor(n, s, d))
        abuf = sb("p8ab", [128, 32 * 32], BF16)
        wf8 = sb("p8wf8", [128, 32 * 8], BF16)
        limTb = sb("p8limT", [32, 8], BF16)
        mcol = sb("p8mcol", [8, 2], F32)
        msc = sb("p8msc", [2, 8], F32)
        psb = c8.enter_context(nc.psum_tensor("p8psb", [128, 512], F32))
        psz = c8.enter_context(nc.psum_tensor("p8psz", [32, T], F32))
        pshs = c8.enter_context(nc.psum_tensor("p8pshs", [8, 512], F32))
        psds = c8.enter_context(nc.psum_tensor("p8psds", [8, 512], F32))
        psmx = c8.enter_context(nc.psum_tensor("p8psmx", [2, 8], F32))
        s_ld = p.mksem("p8b_ld")
        s_bm = p.mksem("p8b_bm")
        s_be = p.mksem("p8b_be")
        s_z = p.mksem("p8b_z")
        s_ze = p.mksem("p8b_ze")
        s_h = p.mksem("p8b_h")
        s_he = p.mksem("p8b_he")
        s_ds = p.mksem("p8b_ds")
        s_mx = p.mksem("p8b_mx")
        s_st = p.mksem("p8b_st")

        with nc.Block() as blk:
            @blk.gpsimd
            def _(g):
                g.dma_start(out=browb[:], in_=io["brow"]).then_inc(s_ld, 16)
                g.dma_start(out=abuf[:],
                            in_=io["amat"].rearrange("(k q) r -> q k r", q=128)
                            ).then_inc(s_ld, 16)
                g.dma_start(out=wf8[:],
                            in_=io["wfT"][:, 0:8].rearrange("(k q) e -> q k e", q=128)
                            ).then_inc(s_ld, 16)
                g.dma_start(out=limTb[:], in_=io["limT"]).then_inc(s_ld, 16)

            @blk.tensor
            def _(t):
                t.wait_ge(s_ld, 64)
                for o in range(8):
                    if o >= 1:
                        t.wait_ge(s_be, o)
                    t.matmul(psb[:], p.ones1[:], browb[:, o * 512:(o + 1) * 512],
                             start=True, stop=True).then_inc(s_bm, 1)
                t.wait_ge(s_ld, 64)
                for k in range(NKT):
                    st_, sp_ = (k == 0), (k == NKT - 1)
                    t.matmul(psz[:, 0:512], abuf[:, k * 32:(k + 1) * 32],
                             ln2[:, k * T: k * T + 512], start=st_, stop=sp_)
                    t.matmul(psz[:, 512:1024], abuf[:, k * 32:(k + 1) * 32],
                             ln2[:, k * T + 512:(k + 1) * T],
                             start=st_, stop=sp_).then_inc(s_z, 1)
                t.wait_ge(s_ld, 64)
                for k in range(NKT):
                    st_, sp_ = (k == 0), (k == NKT - 1)
                    odd = ln2[:, k * T:(k + 1) * T].rearrange(
                        "q (a two) -> q two a", two=2)[:, 1, :]
                    t.matmul(pshs[:], wf8[:, k * 8:(k + 1) * 8], odd,
                             start=st_, stop=sp_).then_inc(s_h, 1)
                t.wait_ge(s_ld, 64)
                t.wait_ge(s_ze, 2)
                zodd = zbf[:].rearrange("q (a two) -> q two a", two=2)[:, 1, :]
                t.matmul(psds[:], limTb[:], zodd, start=True,
                         stop=True).then_inc(s_ds, 1)
                t.wait_ge(s_mx, 1)
                t.transpose(psmx[:], mcol[:], p.idf32[0:8, 0:8]).then_inc(s_mx, 1)

            @blk.vector
            def _(v):
                for o in range(8):
                    v.wait_ge(s_bm, o + 1)
                    v.tensor_copy(biasm[:, o * 512:(o + 1) * 512],
                                  psb[:]).then_inc(s_be, 1)
                v.wait_ge(s_z, NKT)
                v.tensor_copy(zsb[:], psz[:]).then_inc(s_ze, 1)
                v.tensor_copy(zbf[:], psz[:]).then_inc(s_ze, 1)
                v.wait_ge(s_h, NKT)
                p.vf(v, v.tensor_scalar(hsl[:], pshs[:], p.b8[:], None,
                                        ALU.add), s_he, 1)
                v.wait_ge(s_ds, 1)
                p.vf(v, v.tensor_copy(dsl[:], psds[:]))
                v.tensor_reduce(mcol[:, 0:1], hsl[:], AX.X, ALU.max,
                                apply_absolute_value=True)
                p.vf(v, v.tensor_reduce(mcol[:, 1:2], dsl[:], AX.X, ALU.max,
                                        apply_absolute_value=True), s_mx, 1)
                v.wait_ge(s_mx, 2)
                v.tensor_reduce(msc[:, 0:1], psmx[:], AX.X, ALU.max).then_inc(s_mx, 1)

            @blk.sync
            def _(s):
                s.wait_ge(s_mx, 3)
                s.dma_start(out=p.msc_dram, in_=msc[:, 0:1]).then_inc(s_st, 16)
                s.wait_ge(s_st, 16)

    with ExitStack() as c8:
        sb = lambda n, s, d: c8.enter_context(nc.sbuf_tensor(n, s, d))
        wfb = [sb(f"p8wf_{i}", [128, 4 * 512], BF16) for i in range(2)]
        x1c = [sb(f"p8x1c_{i}", [128, 512], F32) for i in range(2)]
        x2c = [sb(f"p8x2c_{i}", [128, 512], F32) for i in range(2)]
        tmpc = sb("p8tmpc", [128, 512], F32)
        psh = [c8.enter_context(nc.psum_tensor(f"p8psh_{i}", [128, 512], F32))
               for i in range(8)]
        s_cc = p.mksem("p8_cc")
        s_w = [p.mksem("p8d_wa"), p.mksem("p8d_wb")]
        s_pc = p.mksem("p8d_pc")
        s_mm = p.mksem("p8d_mm")
        s_x1 = [p.mksem("p8d_x1a"), p.mksem("p8d_x1b")]
        s_ev = p.mksem("p8d_ev")
        s_st = [p.mksem("p8d_sta"), p.mksem("p8d_stb")]
        p.s8_cc = s_cc

        with nc.Block() as blk:
            @blk.gpsimd
            def _(g):
                g.collective_compute(
                    "AllReduce", ALU.max,
                    replica_groups=[[0, 1, 2, 3, 4, 5, 6, 7]],
                    ins=[p.msc_dram.opt()], outs=[p.msc_out.opt()],
                ).then_inc(s_cc)
                for o in range(8):
                    for kg in range(8):
                        pi = o * 8 + kg
                        if pi >= 2:
                            g.wait_ge(s_pc, pi - 1)
                        g.dma_start(out=wfb[pi % 2][:],
                                    in_=io["wfT"][kg * 512:(kg + 1) * 512,
                                                  o * 512:(o + 1) * 512]
                                    .rearrange("(k q) f -> q k f", q=128)
                                    ).then_inc(s_w[pi % 2], 16)

            @blk.tensor
            def _(t):
                for o in range(8):
                    for kg in range(8):
                        pi = o * 8 + kg
                        t.wait_ge(s_w[pi % 2], (pi // 2 + 1) * 16)
                        if kg == 0 and o >= 1:
                            t.wait_ge(s_ev, o * 8)
                        for k4 in range(4):
                            k = kg * 4 + k4
                            st_, sp_ = (k == 0), (k == NKT - 1)
                            for tt in range(8):
                                mm = t.matmul(psh[tt][:],
                                              ln2[:, k * T + tt * 128:
                                                  k * T + (tt + 1) * 128],
                                              wfb[pi % 2][:, k4 * 512:(k4 + 1) * 512],
                                              start=st_, stop=sp_)
                        mm.then_inc(s_pc, 1)

            @blk.sync
            def _(s):
                for o in range(8):
                    for tt in range(8):
                        idx = o * 8 + tt
                        if idx >= 2:
                            s.wait_ge(s_ev, idx - 1)
                        s.dma_start(out=x1c[tt % 2][:],
                                    in_=p.x1_sp[tt * 128:(tt + 1) * 128,
                                                o * 512:(o + 1) * 512]
                                    ).then_inc(s_x1[idx % 2], 16)
                        s.wait_ge(s_ev, idx + 1)
                        s.dma_start(out=p.x2p_sp[tt * 128:(tt + 1) * 128,
                                                 o * 512:(o + 1) * 512],
                                    in_=x2c[tt % 2][:]).then_inc(s_st[idx % 2], 16)
                s.wait_ge(s_st[0], 32 * 16); s.wait_ge(s_st[1], 32 * 16)

            @blk.vector
            def _(v):
                for o in range(8):
                    v.wait_ge(s_pc, (o + 1) * 8)
                    for tt in range(8):
                        idx = o * 8 + tt
                        v.wait_ge(s_x1[idx % 2], (idx // 2 + 1) * 16)
                        if idx >= 2:
                            v.wait_ge(s_st[idx % 2], (idx // 2) * 16)
                        p.vf(v, v.tensor_add(tmpc[:], x1c[tt % 2][:],
                                             biasm[:, o * 512:(o + 1) * 512]))
                        v.tensor_add(x2c[tt % 2][:], tmpc[:],
                                     psh[tt][:]).then_inc(s_ev, 1)

    with ExitStack() as c8:
        sb = lambda n, s, d: c8.enter_context(nc.sbuf_tensor(n, s, d))
        mscs = sb("p8mscs", [2, 1], F32)
        rsc1 = sb("p8rsc1", [1, 2], F32)
        rscb = sb("p8rscb", [128, 2], F32)
        htok = sb("p8htok", [128, 4 * 8], F32)
        dtok8 = sb("p8dtok", [128, 4 * 8], F32)
        rtr = sb("p8rtr", [128, 8 * 8], F32)
        top8 = sb("p8top8", [128, 8], F32)
        wnb = sb("p8wnb", [128, 4 * 8], BF16)
        wTb = sb("p8wTb", [8, 4 * 128], BF16)
        limb = sb("p8limb", [8, 32], BF16)
        pmix = sb("p8pmix", [32, 512], F32)
        zpb = sb("p8zpb", [32, T], BF16)
        bmb = sb("p8bmb", [32, D], BF16)
        x2s = [sb(f"p8x2s_{i}", [128, 512], F32) for i in range(2)]
        outs = [sb(f"p8outs_{i}", [128, 512], F32) for i in range(2)]
        ps1 = c8.enter_context(nc.psum_tensor("p8e_ps1", [128, 512], F32))
        ps2 = c8.enter_context(nc.psum_tensor("p8e_ps2", [128, 512], F32))
        psw = c8.enter_context(nc.psum_tensor("p8e_psw", [8, 128], BF16))
        s_ld = p.mksem("p8e_ld")
        s_ldg = p.mksem("p8e_ldg")
        s_stp = p.mksem("p8e_stp")
        s_x2 = [p.mksem("p8e_x2a"), p.mksem("p8e_x2b")]
        s_ev = p.mksem("p8e_ev")
        s_mm = p.mksem("p8e_mm")
        s_st = [p.mksem("p8e_sta"), p.mksem("p8e_stb")]

        with nc.Block() as blk:
            @blk.sync
            def _(s):
                s.wait_ge(p.s8_cc, 1)
                s.dma_start(out=mscs[:], in_=p.msc_out).then_inc(s_ld, 16)
                s.wait_ge(s_ld, 16)
                s.wait_ge(s_ldg, 32)

            @blk.gpsimd
            def _(g):
                g.dma_start(out=limb[:], in_=io["limes"]).then_inc(s_ldg, 16)
                g.dma_start(out=bmb[:], in_=io["bmat"]).then_inc(s_ldg, 16)

        with nc.Block() as blk:
            @blk.vector
            def _(v):
                p.vf(v, v.tensor_scalar(mscs[:], mscs[:], 1e-6, None, ALU.max))
                p.vf(v, v.reciprocal(mscs[:], mscs[:]))
                v.tensor_scalar(mscs[:], mscs[:], 0.5, None, ALU.mult)

        with nc.Block() as blk:
            @blk.tensor
            def _(t):
                t.transpose(ps1[0:1, 0:2], mscs[:], p.idf32[0:2, 0:2]).then_inc(s_stp, 1)

            @blk.vector
            def _(v):
                v.wait_ge(s_stp, 1)
                v.tensor_copy(rsc1[:], ps1[0:1, 0:2])
        with nc.Block() as blk:
            @blk.tensor
            def _(t):
                t.matmul(ps1[:, 0:2], p.ones1[:], rsc1[:], start=True,
                         stop=True).then_inc(s_stp, 1)

            @blk.vector
            def _(v):
                v.wait_ge(s_stp, 2)
                v.tensor_copy(rscb[:], ps1[:, 0:2])

        s_t4 = p.mksem("p8e_t4")
        s_v4 = p.mksem("p8e_v4")
        with nc.Block() as blk:
            @blk.tensor
            def _(t):
                for j in range(4):
                    if j >= 1:
                        t.wait_ge(s_v4, j)
                    t.transpose(ps1[:, 0:8], hsl[:, j * 128:(j + 1) * 128],
                                p.idf32[0:8, 0:8])
                    t.transpose(ps2[:, 0:8], dsl[:, j * 128:(j + 1) * 128],
                                p.idf32[0:8, 0:8]).then_inc(s_t4, 1)

            @blk.vector
            def _(v):
                for j in range(4):
                    v.wait_ge(s_t4, j + 1)
                    v.tensor_copy(htok[:, j * 8:(j + 1) * 8], ps1[:, 0:8])
                    v.tensor_copy(dtok8[:, j * 8:(j + 1) * 8],
                                  ps2[:, 0:8]).then_inc(s_v4, 1)

        for j in range(4):
            with nc.Block() as blk:
                @blk.vector
                def _(v):
                    hj = htok[:, j * 8:(j + 1) * 8]
                    dj = dtok8[:, j * 8:(j + 1) * 8]
                    p.vf(v, v.tensor_scalar(rtr[:, 8:16], dj, rscb[:, 1:2],
                                            None, ALU.mult))
                    v.scalar_tensor_tensor(rtr[:, 0:8], hj, rscb[:, 0:1],
                                           rtr[:, 8:16], ALU.mult, ALU.add)
            with nc.Block() as blk:
                @blk.scalar
                def _(sc):
                    sc.activation(rtr[:, 16:24], rtr[:, 0:8], AF.Exp)
            with nc.Block() as blk:
                @blk.vector
                def _(v):
                    p.vf(v, v.reduce_sum(rtr[:, 24:25], rtr[:, 16:24], axis=AX.X))
                    p.vf(v, v.reciprocal(rtr[:, 25:26], rtr[:, 24:25]))
                    p.vf(v, v.tensor_scalar(rtr[:, 32:40], rtr[:, 16:24],
                                            rtr[:, 25:26], None, ALU.mult))
                    p.vf(v, v.max(top8[:], rtr[:, 32:40]))
                    v.tensor_scalar(rtr[:, 26:27], top8[:, 1:2], -2.0, None,
                                    ALU.mult)
            with nc.Block() as blk:
                @blk.scalar
                def _(sc):
                    sc.activation(rtr[:, 40:48], rtr[:, 32:40], AF.Sigmoid,
                                  scale=2.0, bias=rtr[:, 26:27])
            with nc.Block() as blk:
                @blk.vector
                def _(v):
                    p.vf(v, v.tensor_mul(rtr[:, 48:56], rtr[:, 32:40],
                                         rtr[:, 40:48]))
                    p.vf(v, v.reduce_sum(rtr[:, 24:25], rtr[:, 48:56], axis=AX.X))
                    p.vf(v, v.tensor_scalar(rtr[:, 24:25], rtr[:, 24:25], 1e-9,
                                            None, ALU.add))
                    p.vf(v, v.reciprocal(rtr[:, 25:26], rtr[:, 24:25]))
                    v.tensor_scalar(wnb[:, j * 8:(j + 1) * 8], rtr[:, 48:56],
                                    rtr[:, 25:26], None, ALU.mult)

        s_t6 = p.mksem("p8e_t6")
        s_v6 = p.mksem("p8e_v6")
        with nc.Block() as blk:
            @blk.tensor
            def _(t):
                for j in range(4):
                    if j >= 1:
                        t.wait_ge(s_v6, j)
                    t.transpose(psw[:], wnb[:, j * 8:(j + 1) * 8],
                                p.idbf[:]).then_inc(s_t6, 1)

            @blk.vector
            def _(v):
                for j in range(4):
                    v.wait_ge(s_t6, j + 1)
                    v.tensor_copy(wTb[:, j * 128:(j + 1) * 128],
                                  psw[:]).then_inc(s_v6, 1)
        with nc.Block() as blk:
            @blk.tensor
            def _(t):
                t.matmul(ps2[0:32, 0:512], limb[:], wTb[:], start=True,
                         stop=True).then_inc(s_t6, 1)

            @blk.vector
            def _(v):
                v.wait_ge(s_t6, 5)
                p.vf(v, v.tensor_copy(pmix[:], ps2[0:32, 0:512]))
                v.tensor_tensor(zpb[:], zbf[:], bc2(pmix[:], 512, 2), ALU.mult)

        with nc.Block() as blk:
            @blk.tensor
            def _(t):
                for tt in range(8):
                    for o in range(8):
                        idx = tt * 8 + o
                        dstp = ps1 if idx % 2 == 0 else ps2
                        if idx >= 2:
                            t.wait_ge(s_ev, idx - 1)
                        t.matmul(dstp[:], zpb[:, tt * 128:(tt + 1) * 128],
                                 bmb[:, o * 512:(o + 1) * 512],
                                 start=True, stop=True).then_inc(s_mm, 1)

            @blk.vector
            def _(v):
                for tt in range(8):
                    for o in range(8):
                        idx = tt * 8 + o
                        v.wait_ge(s_mm, idx + 1)
                        v.wait_ge(s_x2[idx % 2], (idx // 2 + 1) * 16)
                        if idx >= 2:
                            v.wait_ge(s_st[idx % 2], (idx // 2) * 16)
                        v.scalar_tensor_tensor(outs[idx % 2][:],
                                               (ps1 if idx % 2 == 0 else ps2)[:],
                                               0.5, x2s[idx % 2][:],
                                               ALU.mult, ALU.add).then_inc(s_ev, 1)

            @blk.sync
            def _(s):
                for tt in range(8):
                    for o in range(8):
                        idx = tt * 8 + o
                        if idx >= 2:
                            s.wait_ge(s_ev, idx - 1)
                        s.dma_start(out=x2s[idx % 2][:],
                                    in_=p.x2p_sp[tt * 128:(tt + 1) * 128,
                                                 o * 512:(o + 1) * 512]
                                    ).then_inc(s_x2[idx % 2], 16)
                        s.wait_ge(s_ev, idx + 1)
                        s.dma_start(out=io["out"][tt * 128:(tt + 1) * 128,
                                                  o * 512:(o + 1) * 512],
                                    in_=outs[idx % 2][:]).then_inc(s_st[idx % 2], 16)
                s.wait_ge(s_st[0], 32 * 16)
                s.wait_ge(s_st[1], 32 * 16)

    scope8.close()

    if "tap_x2p" in p.tap or "tap_z" in p.tap or "tap_hsl" in p.tap:
        sem = p.mksem("p8_tap")
        with nc.Block() as blk:
            @blk.sync
            def _(s):
                n = 0
                if "tap_x2p" in p.tap:
                    s.dma_start(out=p.tap["tap_x2p"], in_=p.x2p_sp).then_inc(sem, 16)
                    n += 16
                if "tap_msc" in p.tap:
                    s.dma_start(out=p.tap["tap_msc"], in_=p.msc_out).then_inc(sem, 16)
                    n += 16
                s.wait_ge(sem, n)


def finalphase(nc, p):
    pass


def prep_in_maps(inputs):
    f32 = np.float32
    x = np.ascontiguousarray(inputs["L0_attn_x"], dtype=f32)
    wiT = np.ascontiguousarray(inputs["L0_attn_in_proj_w"].T, dtype=f32)
    wiT_g = np.ascontiguousarray(wiT[:, 0:INTER])
    wiT_a = np.ascontiguousarray(wiT[:, INTER:])
    woT = np.ascontiguousarray(inputs["L0_attn_out_proj_w"].T, dtype=f32)
    wfT = np.ascontiguousarray(inputs["L0_ffn_W"].T, dtype=f32)
    conv_w = np.ascontiguousarray(inputs["L0_attn_conv_w"], dtype=f32).reshape(10240, 4)
    cwt = np.ascontiguousarray(conv_w.reshape(80, 128, 4).transpose(1, 0, 2)
                               .reshape(128, 320))
    cbt = np.ascontiguousarray(
        inputs["L0_attn_conv_b"].astype(f32).reshape(80, 128).T)
    nwt = np.ascontiguousarray(
        inputs["L0_attn_norm_w"].astype(f32).reshape(64, 128).T)
    alogc = np.ascontiguousarray(inputs["L0_attn_A_log"].astype(f32).reshape(128, 1))
    dtbc = np.ascontiguousarray(inputs["L0_attn_dt_bias"].astype(f32).reshape(128, 1))
    dvecc = np.ascontiguousarray(inputs["L0_attn_D"].astype(f32).reshape(128, 1))
    amat = np.ascontiguousarray(inputs["L0_ffn_A"], dtype=f32)
    bmat = np.ascontiguousarray(inputs["L0_ffn_Bmat"], dtype=f32)
    limes = np.ascontiguousarray(inputs["L0_ffn_LiMEs"], dtype=f32)
    limT = np.ascontiguousarray(limes.T)
    b = inputs["L0_ffn_b"].astype(f32)
    b8v = np.ascontiguousarray(b[0:8].reshape(8, 1))
    brow = np.ascontiguousarray(b.reshape(1, D))
    li = np.arange(128)
    mnegT = np.where(li[:, None] >= li[None, :], 0.0, NEG).astype(f32)
    idbf = np.eye(128, dtype=BF)
    idf32 = np.eye(128, dtype=f32)

    in_maps = []
    for core in range(8):
        bb, q = core // 4, core % 4
        t0 = q * 1024
        if q == 0:
            x_sl = np.concatenate([np.zeros((3, D), f32), x[bb, 0:1024]], axis=0)
        else:
            x_sl = np.ascontiguousarray(x[bb, t0 - 3: t0 + 1024])
        maskq = np.zeros((128, 4), f32)
        maskq[:, :q] = 1.0
        qsel = np.zeros((128, 4), f32)
        qsel[:, q] = 1.0
        in_maps.append({
            "x_sl": x_sl, "wiT_a": wiT_a, "wiT_g": wiT_g, "woT": woT,
            "wfT": wfT, "amat": amat, "bmat": bmat, "limes": limes,
            "limT": limT, "cwt": cwt, "cbt": cbt, "nwt": nwt,
            "alogc": alogc, "dtbc": dtbc, "dvecc": dvecc, "maskq": maskq,
            "qsel": qsel, "b8": b8v, "brow": brow, "mnegT": mnegT,
            "idbf": idbf, "idf32": idf32,
        })
    return in_maps


def kernel(**inputs):
    in_maps = prep_in_maps(inputs)
    nc = build_nc()
    res = run_bass_kernel_spmd(nc, in_maps, core_ids=list(range(8)))
    out = np.empty((2, 4096, D), np.float32)
    for core in range(8):
        bb, q = core // 4, core % 4
        out[bb, q * 1024:(q + 1) * 1024] = res.results[core]["out"]
    return out


# version 02
# speedup vs baseline: 1.0175x; 1.0175x over previous
import numpy as np
import ml_dtypes
from contextlib import ExitStack

import concourse.bass as bass
import concourse.mybir as mybir
from concourse.bass_utils import run_bass_kernel_spmd

F32 = mybir.dt.float32
BF16 = mybir.dt.bfloat16
AF = mybir.ActivationFunctionType
ALU = mybir.AluOpType
AX = mybir.AxisListType
BF = ml_dtypes.bfloat16

D = 4096
T = 1024
TH = 1027
HN = 128
PD = 64
G = 8
NCH = 8
INTER = 8192
FA = 10368
NKT = 32
NEG = -1.0e30
EPS_LN = 1e-5
EPS_RMS = 1e-5


class Ph:
    pass


def bc2(ap, n0, n1):
    return ap.unsqueeze(2).broadcast_to([ap.shape[0], n0, n1])


def build_nc(taps=(), upto=99):
    nc = bass.Bass("TRN2", target_bir_lowering=False)
    p = Ph()
    p.taps = set(taps)

    io = {}

    def inp(name, shape, dt=F32):
        io[name] = nc.dram_tensor(name, shape, dt, kind="ExternalInput").ap()

    inp("x_sl", [TH, D])
    inp("wiT_a", [D, FA])
    inp("wiT_g", [D, INTER])
    inp("woT", [INTER, D])
    inp("wfT", [D, D])
    inp("amat", [D, 32])
    inp("bmat", [32, D])
    inp("limes", [8, 32])
    inp("limT", [32, 8])
    inp("cwt", [128, 320])
    inp("cbt", [128, 80])
    inp("nwt", [128, 64])
    inp("alogc", [128, 1])
    inp("dtbc", [128, 1])
    inp("dvecc", [128, 1])
    inp("maskq", [128, 4])
    inp("qsel", [128, 4])
    inp("b8", [8, 1])
    inp("brow", [1, D])
    inp("mnegT", [128, 128])
    inp("idbf", [128, 128], BF16)
    inp("idf32", [128, 128])
    io["out"] = nc.dram_tensor("out", [T, D], F32, kind="ExternalOutput").ap()
    p.io = io

    p.xch_sp = nc.dram_tensor("xch_sp", [NCH, 128, INTER], BF16).ap()
    p.yfm_sp = nc.dram_tensor("yfm_sp", [64, 128, T], BF16).ap()
    p.gate_sp = nc.dram_tensor("gate_sp", [64, 128, T], BF16).ap()
    p.x1_sp = nc.dram_tensor("x1_sp", [T, D], F32).ap()
    p.x2p_sp = nc.dram_tensor("x2p_sp", [T, D], F32).ap()
    p.carry_dram = nc.dram_tensor("carry_dram", [128, INTER + 1], F32).ap()
    p.gath_dram = nc.dram_tensor("gath_dram", [512, INTER + 1], F32).ap()
    p.msc_dram = nc.dram_tensor("msc_dram", [2, 1], F32).ap()
    p.msc_out = nc.dram_tensor("msc_out", [2, 1], F32).ap()

    p.tap = {}

    def mktap(name, shape, dt):
        if name in p.taps:
            p.tap[name] = nc.dram_tensor(name, shape, dt, kind="ExternalOutput").ap()

    mktap("tap_ln1", [128, NKT * TH], BF16)
    mktap("tap_dt", [128, TH], F32)
    mktap("tap_acg", [128, T], F32)
    mktap("tap_bc", [128, 16 * T], BF16)
    mktap("tap_btok", [128, 64 * 128], BF16)
    mktap("tap_carry", [128, INTER], F32)
    mktap("tap_xch", [NCH, 128, INTER], BF16)
    mktap("tap_R", [128, INTER], F32)
    mktap("tap_y", [64, 128, T], BF16)
    mktap("tap_g3", [64, 128, T], BF16)
    mktap("tap_x1", [T, D], F32)
    mktap("tap_x2p", [T, D], F32)
    mktap("tap_hsl", [8, 512], F32)
    mktap("tap_dsl", [8, 512], F32)
    mktap("tap_z", [32, T], F32)
    mktap("tap_msc", [2, 1], F32)

    with ExitStack() as ctx:
        p.ctx = ctx
        p.nc = nc
        p.mksb = lambda n, s, d: ctx.enter_context(nc.sbuf_tensor(n, s, d))
        p.phase_sems = []

        def mksem(n):
            h = nc.alloc_semaphore(name=n)
            p.phase_sems.append(h)
            return h

        def free_sems():
            if p.phase_sems:
                nc.clear_and_free_semaphores(p.phase_sems)
                p.phase_sems = []
        p.mksem = mksem
        p.free_sems = free_sems
        p.vs = nc.alloc_semaphore(name="vself")
        p.ps_ = nc.alloc_semaphore(name="pself")
        p.vsn = 0
        p.psn = 0

        def vf(v, inst, sem=None, k=1):
            p.vsn += 1
            inst.then_inc(p.vs, 1)
            v.wait_ge(p.vs, p.vsn)
            if sem is not None:
                v.engine_nop().then_inc(sem, k)
            return inst

        def pf(g, inst, sem=None, k=1):
            p.psn += 1
            inst.then_inc(p.ps_, 1)
            g.wait_ge(p.ps_, p.psn)
            if sem is not None:
                g.engine_nop().then_inc(sem, k)
            return inst
        p.ss = nc.alloc_semaphore(name="sself")
        p.ssn = 0

        def sf(sc, inst, sem=None, k=1):
            p.ssn += 1
            inst.then_inc(p.ss, 1)
            sc.wait_ge(p.ss, p.ssn)
            if sem is not None:
                sc.nop().then_inc(sem, k)
            return inst
        p.vf = vf
        p.pf = pf
        p.sf = sf

        p.ones1 = p.mksb("ones1", [1, 128], F32)
        p.onescbf = p.mksb("onescbf", [128, 1], BF16)
        p.ones128 = p.mksb("ones128", [128, 128], F32)
        p.mneg = p.mksb("mneg_s", [128, 128], F32)
        p.idbf = p.mksb("idbf_s", [128, 128], BF16)
        p.idf32 = p.mksb("idf32_s", [128, 128], F32)
        p.cwt = p.mksb("cwt_s", [128, 320], F32)
        p.cbt = p.mksb("cbt_s", [128, 80], F32)
        p.nwt = p.mksb("nwt_s", [128, 64], F32)
        p.alog = p.mksb("alog_s", [128, 1], F32)
        p.aneg = p.mksb("aneg_s", [128, 1], F32)
        p.dtb = p.mksb("dtb_s", [128, 1], F32)
        p.dvec = p.mksb("dvec_s", [128, 1], F32)
        p.maskq = p.mksb("maskq_s", [128, 4], F32)
        p.qsel = p.mksb("qsel_s", [128, 4], F32)
        p.b8 = p.mksb("b8_s", [8, 1], F32)
        p.cst = {}
        for cv in (1e-5, -1.0, 2.0):
            ct = p.mksb(f"cst_{abs(cv):.0e}_{'n' if cv < 0 else 'p'}", [128, 1], F32)
            nc.const_aps.register(F32, cv, ct[:, 0:1])
            p.cst[cv] = ct

        p.scopeA = ExitStack()
        sbA = lambda n, s, d: p.scopeA.enter_context(nc.sbuf_tensor(n, s, d))
        p.scopeLn = ExitStack()
        p.ln1 = p.scopeLn.enter_context(
            nc.sbuf_tensor("ln1", [128, NKT * TH], BF16, side="right"))
        p.dtf = sbA("dtf", [128, TH], F32)
        p.acg = sbA("acg", [128, T], F32)
        p.bcfm = sbA("bcfm", [128, 16 * T], BF16)
        p.btok = sbA("btok", [128, 64 * 128], BF16)
        p.rbuf = sbA("rbuf", [128, INTER], F32)

        phase0(nc, p)
        p.free_sems()
        if upto >= 1:
            phase1(nc, p)
            tapblk(nc, p, [("tap_ln1", p.ln1)])
            p.free_sems()
        if upto >= 2:
            p.scope23 = ExitStack()
            p.dttok = p.scope23.enter_context(
                nc.sbuf_tensor("dttok", [128, NCH * 128], F32))
            p.wftok = p.scope23.enter_context(
                nc.sbuf_tensor("wftok", [128, NCH * 128], F32))
            phase2(nc, p)
            tapblk(nc, p, [("tap_dt", p.dtf), ("tap_acg", p.acg)])
            p.free_sems()
        if upto >= 3:
            phase3(nc, p)
            tapblk(nc, p, [("tap_bc", p.bcfm), ("tap_btok", p.btok),
                           ("tap_carry", p.rbuf)])
            p.free_sems()
        if upto >= 2:
            p.scope23.close()
        p.scopeLn.close()
        if upto >= 4:
            p.scopeB = ExitStack()
            p.rb = p.scopeB.enter_context(
                nc.sbuf_tensor("rb", [128, INTER], BF16))
            phase4(nc, p)
            tapblk(nc, p, [("tap_R", p.rbuf)])
            p.free_sems()
        if upto >= 5:
            phase5(nc, p)
            p.free_sems()
        if upto >= 4:
            p.scopeB.close()
        p.scopeA.close()
        if upto >= 6:
            p.scope67 = ExitStack()
            p.g3buf = p.scope67.enter_context(
                nc.sbuf_tensor("g3buf", [128, 64 * T], BF16))
            phase6(nc, p)
            tapblk(nc, p, [("tap_g3", p.g3buf)])
            p.free_sems()
        if upto >= 7:
            phase7(nc, p)
            p.free_sems()
        if upto >= 6:
            p.scope67.close()
        if upto >= 8:
            phase8(nc, p)
        finalphase(nc, p)
    return nc


def tapblk(nc, p, items):
    items = [(n, t) for (n, t) in items if n in p.tap]
    if not items:
        return
    sem = p.mksem("tap_" + items[0][0])
    with nc.Block() as blk:
        @blk.sync
        def _(s):
            n = 0
            for name, src in items:
                s.dma_start(out=p.tap[name], in_=src[:]).then_inc(sem, 16)
                n += 16
            s.wait_ge(sem, n)


def phase0(nc, p):
    io = p.io
    s_ld = p.mksem("p0_ld")
    with nc.Block() as blk:
        @blk.gpsimd
        def _(g):
            g.memset(p.ones1[:], 1.0)
            g.memset(p.onescbf[:], 1.0)
            g.memset(p.ones128[:], 1.0)
            for cv, ct in p.cst.items():
                g.memset(ct[:], cv)

        @blk.sync
        def _(s):
            n = 0
            for dst, src in [(p.mneg, "mnegT"), (p.idbf, "idbf"), (p.idf32, "idf32"),
                             (p.cwt, "cwt"), (p.cbt, "cbt"), (p.nwt, "nwt"),
                             (p.alog, "alogc"), (p.dtb, "dtbc"), (p.dvec, "dvecc"),
                             (p.maskq, "maskq"), (p.qsel, "qsel"),
                             (p.b8, "b8")]:
                s.dma_start(out=dst[:], in_=io[src]).then_inc(s_ld, 16)
                n += 16
            s.wait_ge(s_ld, n)
    with nc.Block() as blk:
        @blk.scalar
        def _(sc):
            p.sf(sc, sc.activation(p.aneg[:], p.alog[:], AF.Exp))
            sc.mul(p.aneg[:], p.aneg[:], -1.0)


def phase1(nc, p):
    io = p.io
    NT = 9
    with ExitStack() as c1:
        sb = lambda n, s, d: c1.enter_context(nc.sbuf_tensor(n, s, d))
        x0 = [sb(f"x0_{i}", [128, D], F32) for i in range(2)]
        sqs = p.bcfm[:, 0:2 * D].bitcast(F32)
        rbv = p.rbuf[:, 0:D].bitcast(BF16)
        xn = [rbv[:, 0:D], rbv[:, D:2 * D]]
        st = sb("lnst", [128, 8], F32)
        tp = [c1.enter_context(nc.psum_tensor(f"lntp_{i}", [128, 128], BF16))
              for i in range(2)]
        s_ld = [p.mksem("p1_lda"), p.mksem("p1_ldb")]
        s_sq = p.mksem("p1_sq")
        s_var = p.mksem("p1_var")
        s_sd = p.mksem("p1_sd")
        s_xn = p.mksem("p1_xn")
        s_t = p.mksem("p1_t")
        s_te = p.mksem("p1_te")

        with nc.Block() as blk:
            @blk.sync
            def _(s):
                for ti in range(NT):
                    rows = 128 if ti < 8 else 3
                    if ti >= 2:
                        s.wait_ge(s_xn, ti - 1)
                    s.dma_start(out=x0[ti % 2][0:rows, :],
                                in_=io["x_sl"][ti * 128: ti * 128 + rows, :]
                                ).then_inc(s_ld[ti % 2], 16)

            @blk.scalar
            def _(sc):
                for ti in range(NT):
                    sc.wait_ge(s_ld[ti % 2], (ti // 2 + 1) * 16)
                    sc.wait_ge(s_var, ti)
                    sc.activation(sqs, x0[ti % 2][:], AF.Square,
                                  accum_out=st[:, 1:2]).then_inc(s_sq, 1)
                    sc.wait_ge(s_var, ti + 1)
                    sc.activation(st[:, 6:7], st[:, 5:6], AF.Sqrt,
                                  bias=EPS_LN).then_inc(s_sd, 1)

            @blk.vector
            def _(v):
                for ti in range(NT):
                    v.wait_ge(s_ld[ti % 2], (ti // 2 + 1) * 16)
                    p.vf(v, v.reduce_sum(st[:, 0:1], x0[ti % 2][:], axis=AX.X))
                    v.wait_ge(s_sq, ti + 1)
                    v.tensor_scalar(st[:, 2:3], st[:, 0:1], 1.0 / D, None, ALU.mult)
                    p.vf(v, v.tensor_scalar(st[:, 3:4], st[:, 1:2], 1.0 / D,
                                            None, ALU.mult))
                    p.vf(v, v.tensor_mul(st[:, 4:5], st[:, 2:3], st[:, 2:3]))
                    v.tensor_sub(st[:, 5:6], st[:, 3:4], st[:, 4:5]).then_inc(s_var, 1)
                    v.wait_ge(s_sd, ti + 1)
                    p.vf(v, v.reciprocal(st[:, 7:8], st[:, 6:7]))
                    v.tensor_scalar(xn[ti % 2], x0[ti % 2][:], st[:, 2:3],
                                    st[:, 7:8], ALU.subtract,
                                    ALU.mult).then_inc(s_xn, 1)
                    cols = 128 if ti < 8 else 3
                    for k in range(NKT):
                        v.wait_ge(s_t, ti * NKT + k + 1)
                        v.tensor_copy(
                            p.ln1[:, k * TH + ti * 128: k * TH + ti * 128 + cols],
                            tp[k % 2][:, 0:cols]).then_inc(s_te, 1)

            @blk.tensor
            def _(t):
                for ti in range(NT):
                    t.wait_ge(s_xn, ti + 1)
                    for k in range(NKT):
                        idx = ti * NKT + k
                        if idx >= 2:
                            t.wait_ge(s_te, idx - 1)
                        t.transpose(tp[k % 2][:], xn[ti % 2][:, k * 128:(k + 1) * 128],
                                    p.idbf[:]).then_inc(s_t, 1)


def phase2(nc, p):
    io = p.io
    with ExitStack() as c2:
        sb = lambda n, s, d: c2.enter_context(nc.sbuf_tensor(n, s, d))
        wdt = [sb(f"wdt_{i}", [128, 4 * 128], BF16) for i in range(2)]
        dtraw = sb("dtraw", [128, TH], F32)
        t1 = sb("p2t1", [128, TH], F32)
        t2 = sb("p2t2", [128, TH], F32)
        wff = sb("wff", [128, T], F32)
        psdt = c2.enter_context(nc.psum_tensor("psdt", [128, 1536], F32))
        pst = c2.enter_context(nc.psum_tensor("p2pst", [128, 128], F32))
        s_w = [p.mksem("p2_wa"), p.mksem("p2_wb")]
        s_mm = p.mksem("p2_mm")
        s_dv = p.mksem("p2_dv")
        s_sc = p.mksem("p2_sc")
        s_t = p.mksem("p2_t")
        s_te = p.mksem("p2_te")

        with nc.Block() as blk:
            @blk.gpsimd
            def _(g):
                for kg in range(8):
                    if kg >= 2:
                        g.wait_ge(s_mm, kg - 1)
                    g.dma_start(
                        out=wdt[kg % 2][:],
                        in_=io["wiT_a"][kg * 512:(kg + 1) * 512, 10240:10368]
                        .rearrange("(k q) f -> q k f", q=128)
                        ).then_inc(s_w[kg % 2], 16)

            @blk.tensor
            def _(t):
                for kg in range(8):
                    t.wait_ge(s_w[kg % 2], (kg // 2 + 1) * 16)
                    for k4 in range(4):
                        k = kg * 4 + k4
                        lhs = wdt[kg % 2][:, k4 * 128:(k4 + 1) * 128]
                        st_, sp_ = (k == 0), (k == 31)
                        t.matmul(psdt[:, 0:512], lhs,
                                 p.ln1[:, k * TH: k * TH + 512], start=st_, stop=sp_)
                        t.matmul(psdt[:, 512:1024], lhs,
                                 p.ln1[:, k * TH + 512: k * TH + 1024],
                                 start=st_, stop=sp_)
                        mm = t.matmul(psdt[:, 1024:1027], lhs,
                                      p.ln1[:, k * TH + 1024: k * TH + 1027],
                                      start=st_, stop=sp_)
                    mm.then_inc(s_mm, 1)
                for c in range(NCH):
                    m = 2 * c
                    t.wait_ge(s_dv, 3)
                    if m >= 1:
                        t.wait_ge(s_te, m)
                    t.transpose(pst[:], p.dtf[:, 3 + c * 128: 3 + (c + 1) * 128],
                                p.idf32[:]).then_inc(s_t, 1)
                    t.wait_ge(s_sc, 5)
                    t.wait_ge(s_te, m + 1)
                    t.transpose(pst[:], wff[:, c * 128:(c + 1) * 128],
                                p.idf32[:]).then_inc(s_t, 1)

            @blk.scalar
            def _(sc):
                sc.wait_ge(s_mm, 8)
                p.sf(sc, sc.copy(dtraw[:], psdt[:, 0:1027]), s_sc, 1)
                p.sf(sc, sc.activation(t1[:], dtraw[:], AF.Abs, bias=p.dtb[:]))
                p.sf(sc, sc.activation(t1[:], t1[:], AF.Exp, scale=-1.0))
                sc.activation(t1[:], t1[:], AF.Ln, bias=1.0).then_inc(s_sc, 1)
                sc.wait_ge(s_dv, 4)
                sc.activation(wff[:], p.acg[:], AF.Exp, scale=-1.0,
                              bias=p.acg[:, T - 1:T]).then_inc(s_sc, 3)

            @blk.vector
            def _(v):
                v.wait_ge(s_sc, 2)
                p.vf(v, v.tensor_scalar(t2[:], dtraw[:], p.dtb[:], 0.0,
                                        ALU.add, ALU.max))
                p.vf(v, v.tensor_add(t2[:], t2[:], t1[:]))
                p.vf(v, v.tensor_scalar(p.dtf[:], t2[:], 100.0, None,
                                        ALU.min), s_dv, 3)
                p.vf(v, v.tensor_scalar(t1[:, 0:T], p.dtf[:, 3:TH], p.aneg[:],
                                        None, ALU.mult))
                v.tensor_tensor_scan(p.acg[:], t1[:, 0:T],
                                     p.ones128[:, 0:1].broadcast_to([128, T]),
                                     0.0, ALU.add, ALU.mult).then_inc(s_dv, 1)
                for c in range(NCH):
                    m = 2 * c
                    v.wait_ge(s_t, m + 1)
                    v.tensor_copy(p.dttok[:, c * 128:(c + 1) * 128],
                                  pst[:]).then_inc(s_te, 1)
                    v.wait_ge(s_t, m + 2)
                    v.tensor_copy(p.wftok[:, c * 128:(c + 1) * 128],
                                  pst[:]).then_inc(s_te, 1)


def phase3(nc, p):
    io = p.io
    with ExitStack() as c3:
        sb = lambda n, s, d: c3.enter_context(nc.sbuf_tensor(n, s, d))
        wbuf = [sb(f"p3w_{i}", [128, 4 * 256], BF16) for i in range(4)]
        raw = [sb(f"p3raw_{i}", [128, TH], F32) for i in range(2)]
        cacc = [sb(f"p3cv_{i}", [128, T], F32) for i in range(2)]
        hsb = [sb(f"p3hs_{i}", [128, T], BF16) for i in range(2)]
        xst = [sb(f"p3xs_{i}", [128, T], BF16) for i in range(2)]
        xds = [sb(f"p3xd_{i}", [128, 128], BF16) for i in range(2)]
        gst = [sb(f"p3gs_{i}", [128, T], BF16) for i in range(2)]
        psA = [c3.enter_context(nc.psum_tensor(f"p3psA_{i}", [128, 1536], F32))
               for i in range(2)]
        psT = c3.enter_context(nc.psum_tensor("p3psT", [128, 128], BF16))
        psC = c3.enter_context(nc.psum_tensor("p3psC", [128, 128], F32))
        s_w = [p.mksem(f"p3_w{i}") for i in range(4)]
        s_pc = p.mksem("p3_pc")
        s_mm = p.mksem("p3_mm")
        s_raw = p.mksem("p3_raw")
        s_cv = p.mksem("p3_cv")
        s_si = p.mksem("p3_si")
        s_t = p.mksem("p3_t")
        s_te = p.mksem("p3_te")
        s_xd = p.mksem("p3_xd")
        s_cy = p.mksem("p3_cy")
        s_ce = p.mksem("p3_ce")
        s_sp = [p.mksem("p3_spa"), p.mksem("p3_spb")]
        s_ge = p.mksem("p3_ge")
        s_gs = [p.mksem("p3_gsa"), p.mksem("p3_gsb")]

        tiles = ([("bc", i) for i in range(16)] + [("hs", i) for i in range(64)]
                 + [("gt", i) for i in range(64)])
        npair = 72

        def wsrc(kind, i):
            if kind == "gt":
                return io["wiT_g"], i * 128
            return io["wiT_a"], (8192 + i * 128) if kind == "bc" else i * 128

        def psum_free_wait(eng, ti):
            kind, i = tiles[ti]
            if kind == "gt":
                eng.wait_ge(s_ge, i + 1)
            else:
                eng.wait_ge(s_raw, ti + 1)

        with nc.Block() as blk:
            @blk.gpsimd
            def _(g):
                for pr in range(npair):
                    wt, base = wsrc(*tiles[2 * pr])
                    for kg in range(8):
                        pi = pr * 8 + kg
                        if pi >= 4:
                            g.wait_ge(s_pc, pi - 3)
                        g.dma_start(out=wbuf[pi % 4][:],
                                    in_=wt[kg * 512:(kg + 1) * 512,
                                           base:base + 256]
                                    .rearrange("(k q) f -> q k f", q=128)
                                    ).then_inc(s_w[pi % 4], 16)

            @blk.tensor
            def _(t):
                for pr in range(npair):
                    kind0 = tiles[2 * pr][0]
                    for kg in range(8):
                        pi = pr * 8 + kg
                        t.wait_ge(s_w[pi % 4], (pi // 4 + 1) * 16)
                        if kg == 0 and pr >= 1:
                            psum_free_wait(t, 2 * (pr - 1))
                            psum_free_wait(t, 2 * (pr - 1) + 1)
                        for k4 in range(4):
                            k = kg * 4 + k4
                            st_, sp_ = (k == 0), (k == 31)
                            for sub in range(2):
                                lhs = wbuf[pi % 4][:, k4 * 256 + sub * 128:
                                                   k4 * 256 + sub * 128 + 128]
                                if kind0 == "gt":
                                    t.matmul(psA[sub][:, 0:512], lhs,
                                             p.ln1[:, k * TH + 3: k * TH + 515],
                                             start=st_, stop=sp_)
                                    mm = t.matmul(psA[sub][:, 512:1024], lhs,
                                                  p.ln1[:, k * TH + 515: k * TH + 1027],
                                                  start=st_, stop=sp_)
                                else:
                                    t.matmul(psA[sub][:, 0:512], lhs,
                                             p.ln1[:, k * TH: k * TH + 512],
                                             start=st_, stop=sp_)
                                    t.matmul(psA[sub][:, 512:1024], lhs,
                                             p.ln1[:, k * TH + 512: k * TH + 1024],
                                             start=st_, stop=sp_)
                                    mm = t.matmul(psA[sub][:, 1024:1027], lhs,
                                                  p.ln1[:, k * TH + 1024: k * TH + 1027],
                                                  start=st_, stop=sp_)
                        mm.then_inc(s_pc, 1)
                    if pr == 7:
                        for c in range(NCH):
                            for gg in range(G):
                                j = c * 8 + gg
                                t.wait_ge(s_si, gg + 1)
                                if j >= 1:
                                    t.wait_ge(s_te, j)
                                t.transpose(
                                    psT[:],
                                    p.bcfm[:, gg * T + c * 128: gg * T + (c + 1) * 128],
                                    p.idbf[:]).then_inc(s_t, 1)
                    if 9 <= pr <= 40:
                        for ti in (2 * (pr - 1), 2 * (pr - 1) + 1):
                            kind, i = tiles[ti]
                            assert kind == "hs"
                            t.wait_ge(s_si, ti + 1)
                            for c in range(NCH):
                                j = 64 + i * 8 + c
                                t.wait_ge(s_te, j)
                                t.transpose(psT[:], hsb[i % 2][:, c * 128:(c + 1) * 128],
                                            p.idbf[:]).then_inc(s_t, 1)
                            for c in range(NCH):
                                t.wait_ge(s_xd, i * 8 + c + 1)
                                if c == 0:
                                    t.wait_ge(s_ce, i)
                                t.matmul(psC[:],
                                         p.btok[:, (c * 8 + i // 8) * 128:
                                                (c * 8 + i // 8 + 1) * 128],
                                         xds[(i * 8 + c) % 2][:],
                                         start=(c == 0), stop=(c == 7)
                                         ).then_inc(s_cy, 1)

            @blk.scalar
            def _(sc):
                for ti, (kind, i) in enumerate(tiles):
                    pr = ti // 2
                    sc.wait_ge(s_pc, (pr + 1) * 8)
                    if kind == "gt":
                        if i >= 2:
                            sc.wait_ge(s_gs[i % 2], (i // 2) * 16)
                        sc.copy(gst[i % 2][:], psA[ti % 2][:, 0:1024]
                                ).then_inc(s_ge, 1)
                        continue
                    if pr >= 1:
                        sc.wait_ge(s_cv, min(2 * pr, 80))
                    sc.copy(raw[ti % 2][:], psA[ti % 2][:, 0:1027]).then_inc(s_raw, 1)
                    sc.wait_ge(s_cv, ti + 1)
                    tci = (64 + i) if kind == "bc" else i
                    dst = (p.bcfm[:, i * T:(i + 1) * T] if kind == "bc"
                           else hsb[i % 2][:])
                    if kind == "hs" and i >= 2:
                        sc.wait_ge(s_t, 64 + (i - 1) * 8)
                    sc.activation(dst, cacc[ti % 2][:], AF.Silu,
                                  bias=p.cbt[:, tci: tci + 1]).then_inc(s_si, 1)

            @blk.vector
            def _(v):
                def conv(ti):
                    kind, i = tiles[ti]
                    v.wait_ge(s_raw, ti + 1)
                    if ti >= 2:
                        v.wait_ge(s_si, ti - 1)
                    tci = (64 + i) if kind == "bc" else i
                    r = raw[ti % 2]
                    ca = cacc[ti % 2]
                    p.vf(v, v.tensor_scalar(ca[:], r[:, 0:T],
                                            p.cwt[:, tci * 4: tci * 4 + 1],
                                            None, ALU.mult))
                    for k in range(1, 3):
                        p.vf(v, v.scalar_tensor_tensor(
                            ca[:], r[:, k:T + k],
                            p.cwt[:, tci * 4 + k: tci * 4 + k + 1],
                            ca[:], ALU.mult, ALU.add))
                    v.scalar_tensor_tensor(
                        ca[:], r[:, 3:T + 3],
                        p.cwt[:, tci * 4 + 3: tci * 4 + 4],
                        ca[:], ALU.mult, ALU.add).then_inc(s_cv, 1)

                for ti in range(16):
                    conv(ti)
                for c in range(NCH):
                    for gg in range(G):
                        j = c * 8 + gg
                        v.wait_ge(s_t, j + 1)
                        v.tensor_copy(p.btok[:, j * 128:(j + 1) * 128],
                                      psT[:]).then_inc(s_te, 1)
                conv(16)
                conv(17)
                for i in range(64):
                    if i + 2 < 64:
                        conv(16 + i + 2)
                    for c in range(NCH):
                        j = 64 + i * 8 + c
                        v.wait_ge(s_t, j + 1)
                        if i >= 2:
                            v.wait_ge(s_sp[i % 2], (i // 2) * 16)
                        v.tensor_tensor(
                            xst[i % 2][:, c * 128:(c + 1) * 128], psT[:],
                            bc2(p.dttok[:, c * 128 + 2 * i: c * 128 + 2 * i + 2], 2, 64),
                            ALU.mult).then_inc(s_te, 1)
                    for c in range(NCH):
                        idx = i * 8 + c
                        if idx >= 2:
                            v.wait_ge(s_cy, idx - 1)
                        v.tensor_tensor(
                            xds[idx % 2][:],
                            xst[i % 2][:, c * 128:(c + 1) * 128],
                            bc2(p.wftok[:, c * 128 + 2 * i: c * 128 + 2 * i + 2], 2, 64),
                            ALU.mult).then_inc(s_xd, 1)
                    v.wait_ge(s_cy, (i + 1) * 8)
                    v.tensor_copy(p.rbuf[:, i * 128:(i + 1) * 128],
                                  psC[:]).then_inc(s_ce, 1)

            @blk.sync
            def _(s):
                for i in range(64):
                    s.wait_ge(s_te, 64 + (i + 1) * 8)
                    s.dma_start(
                        out=p.xch_sp.rearrange("c q f -> q c f")
                        [:, :, i * 128:(i + 1) * 128],
                        in_=xst[i % 2][:].rearrange("q (c l) -> q c l", c=NCH)
                    ).then_inc(s_sp[i % 2], 16)
                for i in range(64):
                    s.wait_ge(s_ge, i + 1)
                    s.dma_start(out=p.gate_sp[i].opt(),
                                in_=gst[i % 2][:]).then_inc(s_gs[i % 2], 16)
                for sl in range(2):
                    s.wait_ge(s_sp[sl], 32 * 16)
                    s.wait_ge(s_gs[sl], 32 * 16)
                s.wait_ge(s_ce, 64)


def phase4(nc, p):
    s_a = p.mksem("p4_a")
    s_cc = p.mksem("p4_cc")
    s_g = p.mksem("p4_g")
    s_ga = p.mksem("p4_ga")
    s_m = p.mksem("p4_m")
    s_t = p.mksem("p4_t")
    s_e = p.mksem("p4_e")
    s_wb = p.mksem("p4_wb")
    with ExitStack() as c4:
        sb = lambda n, s, d: c4.enter_context(nc.sbuf_tensor(n, s, d))
        gbuf = sb("p4_gbuf", [128, INTER], F32)
        am = sb("p4_am", [128, 4], F32)
        sml = sb("p4_sml", [128, 24], F32)
        wmt = sb("p4_wmt", [128, 128], F32)
        pst = c4.enter_context(nc.psum_tensor("p4_pst", [128, 128], F32))

        with nc.Block() as blk:
            @blk.sync
            def _(s):
                s.dma_start(out=p.carry_dram[:, 0:INTER],
                            in_=p.rbuf[:]).then_inc(s_a, 16)
                s.dma_start(out=p.carry_dram[:, INTER:INTER + 1],
                            in_=p.acg[:, T - 1:T],
                            allow_slow_non_contiguous=True).then_inc(s_a, 16)
                s.wait_ge(s_a, 32)

        with nc.Block() as blk:
            @blk.gpsimd
            def _(g):
                g.collective_compute(
                    "AllGather", ALU.bypass,
                    replica_groups=[[0, 1, 2, 3], [4, 5, 6, 7]],
                    ins=[p.carry_dram.opt()], outs=[p.gath_dram.opt()],
                ).then_inc(s_cc)
                g.wait_ge(s_cc, 1)

        with nc.Block() as blk:
            @blk.sync
            def _(s):
                s.dma_start(out=am[:],
                            in_=p.gath_dram[:, INTER:INTER + 1]
                            .rearrange("(q h) o -> h (q o)", h=128),
                            allow_slow_non_contiguous=True).then_inc(s_ga, 16)
                for qp in range(4):
                    if qp >= 1:
                        s.wait_ge(s_m, qp)
                    s.dma_start(out=gbuf[:],
                                in_=p.gath_dram[qp * 128:(qp + 1) * 128, 0:INTER]
                                ).then_inc(s_g, 16)

            @blk.vector
            def _(v):
                v.wait_ge(s_ga, 16)
                p.vf(v, v.memset(sml[:, 0:1], 0.0))
                p.vf(v, v.tensor_copy(sml[:, 1:2], am[:, 0:1]))
                p.vf(v, v.tensor_add(sml[:, 2:3], sml[:, 1:2], am[:, 1:2]))
                p.vf(v, v.tensor_add(sml[:, 3:4], sml[:, 2:3], am[:, 2:3]))
                p.vf(v, v.tensor_add(sml[:, 4:8], sml[:, 0:4], am[:, 0:4]))
                p.vf(v, v.tensor_mul(sml[:, 12:16], p.qsel[:], sml[:, 0:4]))
                p.vf(v, v.reduce_sum(sml[:, 8:9], sml[:, 12:16], axis=AX.X))
                p.vf(v, v.tensor_scalar(sml[:, 12:16], sml[:, 4:8], sml[:, 8:9],
                                        None, ALU.subtract))
                v.tensor_scalar(sml[:, 16:20], sml[:, 12:16], -1.0, 0.0,
                                ALU.mult, ALU.min).then_inc(s_e, 1)
                v.wait_ge(s_e, 2)
                p.vf(v, v.tensor_mul(sml[:, 12:16], sml[:, 20:24], p.maskq[:]))
                v.memset(p.rbuf[:], 0.0)
                for qp in range(4):
                    v.tensor_scalar(wmt[:], p.ones128[:], sml[:, 12 + qp:13 + qp],
                                    None, ALU.mult).then_inc(s_wb, 1)
                    v.wait_ge(s_t, qp + 1)
                    v.wait_ge(s_g, (qp + 1) * 16)
                    p.vf(v, v.tensor_tensor(gbuf[:], gbuf[:], bc2(pst[:], 128, 64),
                                            ALU.mult))
                    v.tensor_add(p.rbuf[:], p.rbuf[:], gbuf[:]).then_inc(s_m, 1)

            @blk.scalar
            def _(sc):
                sc.wait_ge(s_e, 1)
                sc.activation(sml[:, 20:24], sml[:, 16:20], AF.Exp).then_inc(s_e, 1)

            @blk.tensor
            def _(t):
                for qp in range(4):
                    t.wait_ge(s_wb, qp + 1)
                    if qp >= 1:
                        t.wait_ge(s_m, qp)
                    t.transpose(pst[:], wmt[:], p.idf32[:]).then_inc(s_t, 1)


def phase5(nc, p):
    with ExitStack() as c5:
        sb = lambda n, s, d: c5.enter_context(nc.sbuf_tensor(n, s, d))
        xch = [sb(f"p5x_{i}", [128, INTER], BF16) for i in range(2)]
        pall = [sb(f"p5pa_{i}", [128, 2048], F32) for i in range(2)]
        lt = [sb(f"p5lt_{i}", [128, 128], F32) for i in range(16)]
        gt = [sb(f"p5gt_{i}", [128, 128], BF16) for i in range(16)]
        gts = [sb(f"p5gts_{i}", [128, 128], F32) for i in range(2)]
        tmp1 = sb("p5tmp1", [128, 512], F32)
        tmp2 = [sb(f"p5tmp2_{i}", [128, 512], F32) for i in range(2)]
        hsd = [sb(f"p5hsd_{i}", [128, 512], F32) for i in range(2)]
        ytk = [sb(f"p5ytk_{i}", [128, 1024], BF16) for i in range(2)]
        yst = [sb(f"p5yst_{i}", [128, 1024], BF16) for i in range(2)]
        xds = [sb(f"p5xd_{i}", [128, 1024], BF16) for i in range(2)]
        afm = sb("p5afm", [128, 128], F32)
        acum = sb("p5acum", [128, 128], F32)
        nacum = sb("p5nacum", [128, 128], F32)
        eex = sb("p5eex", [128, 128], F32)
        dfm = sb("p5dfm", [128, 128], F32)
        dtok = sb("p5dtok", [128, 128], F32)
        ezb = sb("p5ezb", [128, 128], F32)
        ezm = sb("p5ezm", [128, 128], F32)
        dtrec = sb("p5dtrec", [128, 128], F32)
        ddtf = sb("p5ddtf", [128, 128], F32)
        ddtm = sb("p5ddtm", [128, 128], F32)
        smlc = sb("p5smlc", [128, 4], F32)
        psb = [c5.enter_context(nc.psum_tensor(f"p5psb_{i}", [128, 128], F32))
               for i in range(2)]
        psa = psb[0]
        psbt = c5.enter_context(nc.psum_tensor("p5psbt", [128, 128], BF16))
        psg = c5.enter_context(nc.psum_tensor("p5psg", [128, 128], F32))
        psyd = [c5.enter_context(nc.psum_tensor(f"p5psyd_{i}", [128, 512], F32))
                for i in range(2)]
        psyo = [c5.enter_context(nc.psum_tensor(f"p5psyo_{i}", [128, 512], F32))
                for i in range(2)]
        s_xl = [p.mksem("p5_xla"), p.mksem("p5_xlb")]
        s_pv = p.mksem("p5_pv")
        s_pa5 = p.mksem("p5_pa5")
        s_pt = p.mksem("p5_pt")
        s_pp = p.mksem("p5_pp")
        s_bt = p.mksem("p5_bt")
        s_lt = p.mksem("p5_lt")
        s_m1 = p.mksem("p5_m1")
        s_ge = p.mksem("p5_ge")
        s_gt = p.mksem("p5_gt")
        s_ymm = p.mksem("p5_ymm")
        s_p1 = p.mksem("p5_p1")
        s_p2 = p.mksem("p5_p2")
        s_hd = p.mksem("p5_hd")
        s_ye = p.mksem("p5_ye")
        s_yt = p.mksem("p5_yt")
        s_ys = p.mksem("p5_ys")
        s_sp = [p.mksem("p5_spa"), p.mksem("p5_spb")]
        s_rb = p.mksem("p5_rb")
        s_rz = p.mksem("p5_rz")
        s_xd = p.mksem("p5_xd")
        s_sm = p.mksem("p5_sm")
        s_se = p.mksem("p5_se")

        NPV = 10

        with nc.Block() as blk:
            @blk.sync
            def _(s):
                for c in range(NCH):
                    if c >= 2:
                        s.wait_ge(s_se, (c - 1) * 8)
                    s.dma_start(out=xch[c % 2][:],
                                in_=p.xch_sp[c].opt()).then_inc(s_xl[c % 2], 16)
                    for gg in range(G):
                        cg = c * 8 + gg
                        s.wait_ge(s_ys, (cg + 1) * 8)
                        s.dma_start(
                            out=p.yfm_sp.rearrange("j q t -> q j t")
                            [:, gg * 8:(gg + 1) * 8, c * 128:(c + 1) * 128],
                            in_=yst[cg % 2][:].rearrange("q (j l) -> q j l", j=8)
                        ).then_inc(s_sp[cg % 2], 16)
                for sl in range(2):
                    s.wait_ge(s_sp[sl], 32 * 16)

            @blk.vector
            def _(v):
                for c in range(NCH):
                    cs, ce = c * 128, (c + 1) * 128
                    b = c * NPV
                    if c > 0:
                        v.wait_ge(s_se, c * 8)
                        v.wait_ge(s_ymm, c * 128)
                        v.wait_ge(s_xd, c * 8)
                        v.wait_ge(s_ye, c * 16)
                        v.wait_ge(s_pt, c * 4)
                    v.tensor_copy(p.rb[:], p.rbuf[:]).then_inc(s_rb, 1)
                    if c == 0:
                        p.vf(v, v.memset(smlc[:, 0:1], 0.0))
                    else:
                        p.vf(v, v.tensor_copy(smlc[:, 0:1], p.acg[:, cs - 1:cs]))
                    v.tensor_scalar(afm[:], p.acg[:, cs:ce], smlc[:, 0:1],
                                    None, ALU.subtract).then_inc(s_pv, 1)
                    v.wait_ge(s_pt, c * 4 + 1)
                    p.vf(v, v.tensor_copy(acum[:], psa[:]), s_pv, 1)
                    v.tensor_scalar(nacum[:], acum[:], -1.0, None,
                                    ALU.mult).then_inc(s_pv, 1)
                    v.tensor_scalar(smlc[:, 1:2], p.acg[:, ce - 1:ce], smlc[:, 0:1],
                                    None, ALU.subtract).then_inc(s_pv, 1)
                    v.wait_ge(s_pa5, c * 3 + 2)
                    v.tensor_scalar(ezb[:], p.ones128[:], smlc[:, 2:3],
                                    None, ALU.mult).then_inc(s_pv, 1)
                    v.wait_ge(s_pt, c * 4 + 2)
                    v.tensor_copy(dtok[:], psa[:]).then_inc(s_pv, 1)
                    v.wait_ge(s_pt, c * 4 + 3)
                    v.tensor_copy(ezm[:], psa[:]).then_inc(s_pv, 1)
                    p.vf(v, v.reciprocal(dtrec[:],
                                         p.dtf[:, 3 + cs: 3 + ce]), s_pv, 1)
                    v.tensor_scalar(ddtf[:], dtrec[:], p.dvec[:], None,
                                    ALU.mult).then_inc(s_pv, 1)
                    v.wait_ge(s_pt, c * 4 + 4)
                    v.tensor_copy(ddtm[:], psa[:]).then_inc(s_pv, 1)
                    def yst_evac(cg2):
                        for j in range(8):
                            v.wait_ge(s_yt, cg2 * 8 + j + 1)
                            if cg2 >= 2:
                                v.wait_ge(s_sp[cg2 % 2], ((cg2 - 2) // 2 + 1) * 16)
                            v.tensor_copy(
                                yst[cg2 % 2][:, j * 128:(j + 1) * 128],
                                psbt[:]).then_inc(s_ys, 1)

                    for gg in range(G):
                        cg = c * 8 + gg
                        if gg >= 1:
                            yst_evac(cg - 1)
                        for half in range(2):
                            hg = cg * 2 + half
                            v.wait_ge(s_ymm, (hg + 1) * 8)
                            if hg >= 2:
                                v.wait_ge(s_ye, hg - 1)
                            for hh8 in range(8):
                                h = gg * 16 + half * 8 + hh8
                                inst = v.tensor_scalar(
                                    tmp1[:, hh8 * 64:(hh8 + 1) * 64],
                                    psyo[hg % 2][:, hh8 * 64:(hh8 + 1) * 64],
                                    eex[:, h:h + 1], None, ALU.mult)
                            p.vf(v, inst)
                            p.vf(v, v.tensor_add(tmp2[hg % 2][:], tmp1[:],
                                                 psyd[hg % 2][:]), s_p2, 1)
                    yst_evac(c * 8 + 7)
                    v.wait_ge(s_rb, c + 1)
                    p.vf(v, v.tensor_tensor(p.rbuf[:], p.rbuf[:],
                                            bc2(ezm[:], 128, 64),
                                            ALU.mult), s_rz, 1)
                    for gg in range(G):
                        cg = c * 8 + gg
                        v.wait_ge(s_sm, (cg + 1) * 16)
                        v.tensor_add(p.rbuf[:, gg * 1024: gg * 1024 + 512],
                                     p.rbuf[:, gg * 1024: gg * 1024 + 512],
                                     psyd[gg % 2][:])
                        v.tensor_add(p.rbuf[:, gg * 1024 + 512:(gg + 1) * 1024],
                                     p.rbuf[:, gg * 1024 + 512:(gg + 1) * 1024],
                                     psyo[gg % 2][:]).then_inc(s_se, 1)

            @blk.scalar
            def _(sc):
                for c in range(NCH):
                    cs, ce = c * 128, (c + 1) * 128
                    b = c * NPV
                    if c > 0:
                        sc.wait_ge(s_pt, (c - 1) * 4 + 2)
                    sc.activation(dfm[:], p.acg[:, cs:ce], AF.Exp, scale=-1.0,
                                  bias=p.acg[:, ce - 1:ce]).then_inc(s_pa5, 1)
                    sc.wait_ge(s_pv, b + 4)
                    sc.activation(smlc[:, 2:3], smlc[:, 1:2],
                                  AF.Exp).then_inc(s_pa5, 1)
                    sc.wait_ge(s_pv, b + 2)
                    if c > 0:
                        sc.wait_ge(s_p2, c * 16)
                    sc.activation(eex[:], acum[:], AF.Exp).then_inc(s_pa5, 1)
                    for gg in range(G):
                        cg = c * 8 + gg
                        sc.wait_ge(s_m1, cg + 1)
                        if cg >= 2:
                            sc.wait_ge(s_gt, (cg - 1) * 16)
                        sc.copy(gts[cg % 2][:], psg[:]).then_inc(s_ge, 1)
                        for hh in range(16):
                            hidx = cg * 16 + hh
                            sc.wait_ge(s_bt, hidx + 1)
                            if cg >= 1:
                                sc.wait_ge(s_gt, (cg - 1) * 16 + hh + 1)
                            sc.activation(lt[hh][:], psb[hh % 2][:], AF.Exp,
                                          bias=nacum[:, gg * 16 + hh: gg * 16 + hh + 1]
                                          ).then_inc(s_lt, 1)

            @blk.gpsimd
            def _(g):
                for c in range(NCH):
                    b = c * NPV
                    for gg in range(G):
                        cg = c * 8 + gg
                        g.wait_ge(s_pv, b + 2)
                        if cg >= 2:
                            g.wait_ge(s_bt, (cg - 1) * 16)
                        g.tensor_tensor(
                            pall[cg % 2][:],
                            bc2(acum[:, gg * 16:(gg + 1) * 16], 16, 128),
                            p.mneg[:].unsqueeze(1).broadcast_to([128, 16, 128]),
                            ALU.add).then_inc(s_pp, 1)
                        for hh in range(16):
                            hidx = cg * 16 + hh
                            g.wait_ge(s_lt, hidx + 1)
                            g.wait_ge(s_ge, cg + 1)
                            if cg >= 1:
                                g.wait_ge(s_ymm, (cg - 1) * 16 + hh + 1)
                            g.tensor_tensor(gt[hh][:], gts[cg % 2][:],
                                            lt[hh][:], ALU.mult).then_inc(s_gt, 1)
                        for half in range(2):
                            hgp = cg * 2 + half
                            g.wait_ge(s_pv, b + 10)
                            g.wait_ge(s_xl[c % 2], (c // 2 + 1) * 16)
                            if hgp >= 2:
                                g.wait_ge(s_ye, hgp - 1)
                            sp = gg * 1024 + half * 512
                            h0 = gg * 16 + half * 8
                            p.pf(g, g.tensor_tensor(hsd[hgp % 2][:],
                                                    xch[c % 2][:, sp:sp + 512],
                                                    bc2(ddtm[:, h0:h0 + 8], 8, 64),
                                                    ALU.mult), s_hd, 1)
                            g.wait_ge(s_p2, hgp + 1)
                            if cg >= 2 and half == 0:
                                g.wait_ge(s_yt, (cg - 1) * 8)
                            g.tensor_tensor(ytk[cg % 2][:, half * 512:(half + 1) * 512],
                                            tmp2[hgp % 2][:], hsd[hgp % 2][:],
                                            ALU.add).then_inc(s_ye, 1)
                    for gg in range(G):
                        cg = c * 8 + gg
                        g.wait_ge(s_pv, b + 6)
                        if cg >= 2:
                            g.wait_ge(s_sm, (cg - 1) * 16)
                        g.tensor_tensor(xds[gg % 2][:],
                                        xch[c % 2][:, gg * 1024:(gg + 1) * 1024],
                                        bc2(dtok[:, gg * 16:(gg + 1) * 16], 16, 64),
                                        ALU.mult).then_inc(s_xd, 1)

            @blk.tensor
            def _(t):
                for c in range(NCH):
                    cs, ce = c * 128, (c + 1) * 128
                    b = c * NPV
                    t.wait_ge(s_pv, b + 1)
                    if c >= 1:
                        t.wait_ge(s_lt, c * 128)
                    t.transpose(psa[:], afm[:], p.idf32[:]).then_inc(s_pt, 1)
                    t.wait_ge(s_pa5, c * 3 + 1)
                    t.wait_ge(s_pv, b + 2)
                    t.transpose(psa[:], dfm[:], p.idf32[:]).then_inc(s_pt, 1)
                    t.wait_ge(s_pv, b + 6)
                    t.transpose(psa[:], ezb[:], p.idf32[:]).then_inc(s_pt, 1)
                    t.wait_ge(s_pv, b + 9)
                    t.transpose(psa[:], ddtf[:], p.idf32[:]).then_inc(s_pt, 1)
                    for gg in range(G):
                        cg = c * 8 + gg
                        if cg >= 1:
                            t.wait_ge(s_ge, cg)
                        t.matmul(psg[:], p.bcfm[:, gg * T + cs: gg * T + ce],
                                 p.bcfm[:, (8 + gg) * T + cs:(8 + gg) * T + ce],
                                 start=True, stop=True).then_inc(s_m1, 1)
                        for hh in range(16):
                            hidx = cg * 16 + hh
                            t.wait_ge(s_pp, cg + 1)
                            if gg == 0 and hh == 0:
                                t.wait_ge(s_pv, b + 10)
                            if hidx >= 2:
                                t.wait_ge(s_lt, hidx - 1)
                            t.transpose(psb[hh % 2][:],
                                        pall[cg % 2][:, hh * 128:(hh + 1) * 128],
                                        p.idf32[:]).then_inc(s_bt, 1)
                        for hh in range(16):
                            hidx = cg * 16 + hh
                            hg = cg * 2 + hh // 8
                            h = gg * 16 + hh
                            t.wait_ge(s_gt, hidx + 1)
                            t.wait_ge(s_xl[c % 2], (c // 2 + 1) * 16)
                            if hh % 8 == 0 and hg >= 2:
                                t.wait_ge(s_p2, hg - 1)
                            sl = slice((hh % 8) * 64, (hh % 8 + 1) * 64)
                            t.matmul(psyd[hg % 2][:, sl], gt[hh][:],
                                     xch[c % 2][:, h * 64:(h + 1) * 64],
                                     start=True, stop=True)
                            t.wait_ge(s_rb, c + 1)
                            t.matmul(psyo[hg % 2][:, sl],
                                     p.bcfm[:, (8 + gg) * T + cs:(8 + gg) * T + ce],
                                     p.rb[:, h * 64:(h + 1) * 64],
                                     start=True, stop=True).then_inc(s_ymm, 1)
                        for j in range(8):
                            cgj = cg * 8 + j
                            t.wait_ge(s_ye, cg * 2 + 1 + (1 if j >= 4 else 0))
                            if cgj >= 1:
                                t.wait_ge(s_ys, cgj)
                            t.transpose(psbt[:],
                                        ytk[cg % 2][:, j * 128:(j + 1) * 128],
                                        p.idbf[:]).then_inc(s_yt, 1)
                    for gg in range(G):
                        cg = c * 8 + gg
                        t.wait_ge(s_xd, cg + 1)
                        t.wait_ge(s_rz, c + 1)
                        t.wait_ge(s_p2, c * 16 + 16)
                        if cg >= 2:
                            t.wait_ge(s_se, cg - 1)
                        for hh in range(16):
                            dstp = psyd[gg % 2] if hh < 8 else psyo[gg % 2]
                            sl = slice((hh % 8) * 64, (hh % 8 + 1) * 64)
                            t.matmul(dstp[:, sl],
                                     p.btok[:, (c * 8 + gg) * 128:
                                            (c * 8 + gg + 1) * 128],
                                     xds[gg % 2][:, hh * 64:(hh + 1) * 64],
                                     start=True, stop=True).then_inc(s_sm, 1)

    if "tap_xch" in p.tap or "tap_y" in p.tap or "tap_R" in p.tap:
        sem = p.mksem("p5_tap")
        with nc.Block() as blk:
            @blk.sync
            def _(s):
                n = 0
                if "tap_xch" in p.tap:
                    s.dma_start(out=p.tap["tap_xch"], in_=p.xch_sp).then_inc(sem, 16)
                    n += 16
                if "tap_y" in p.tap:
                    s.dma_start(out=p.tap["tap_y"], in_=p.yfm_sp).then_inc(sem, 16)
                    n += 16
                if "tap_R" in p.tap:
                    s.dma_start(out=p.tap["tap_R"], in_=p.rbuf[:]).then_inc(sem, 16)
                    n += 16
                s.wait_ge(sem, n)


def phase6(nc, p):
    with ExitStack() as c6:
        sb = lambda n, s, d: c6.enter_context(nc.sbuf_tensor(n, s, d))
        gbx = [sb(f"p6gb_{i}", [128, T], BF16) for i in range(2)]
        sgb = [sb(f"p6sg_{i}", [128, T], BF16) for i in range(2)]
        yfm = [sb(f"p6yf_{i}", [128, T], BF16) for i in range(2)]
        sqb = [sb(f"p6sq_{i}", [128, T], BF16) for i in range(2)]
        rsm = sb("p6rs", [1, 2 * T], F32)
        psV = c6.enter_context(nc.psum_tensor("p6psV", [1, T], F32))
        psR = c6.enter_context(nc.psum_tensor("p6psR", [128, T], F32))
        s_gl = [p.mksem("p6_gla"), p.mksem("p6_glb")]
        s_yl = [p.mksem("p6_yla"), p.mksem("p6_ylb")]
        s_sg = p.mksem("p6_sg")
        s_g2 = p.mksem("p6_g2")
        s_sq = p.mksem("p6_sq")
        s_vm = p.mksem("p6_vm")
        s_vr = p.mksem("p6_vr")
        s_sd = p.mksem("p6_sd")
        s_rc = p.mksem("p6_rc")
        s_rm = p.mksem("p6_rm")
        s_sc6 = p.mksem("p6_sc")

        with nc.Block() as blk:
            @blk.sync
            def _(s):
                for ft in range(64):
                    if ft >= 2:
                        s.wait_ge(s_sg, ft - 1)
                    s.dma_start(out=gbx[ft % 2][:],
                                in_=p.gate_sp[ft].opt()).then_inc(s_gl[ft % 2], 16)
                    if ft >= 2:
                        s.wait_ge(s_g2, ft - 1)
                    s.dma_start(out=yfm[ft % 2][:],
                                in_=p.yfm_sp[ft].opt()).then_inc(s_yl[ft % 2], 16)

            @blk.scalar
            def _(sc):
                for ft in range(64):
                    sc.wait_ge(s_gl[ft % 2], (ft // 2 + 1) * 16)
                    if ft >= 2:
                        sc.wait_ge(s_g2, ft - 1)
                    sc.activation(sgb[ft % 2][:], gbx[ft % 2][:],
                                  AF.Silu).then_inc(s_sg, 1)
                    sc.wait_ge(s_g2, ft + 1)
                    if ft >= 2:
                        sc.wait_ge(s_vm, ft - 1)
                    sc.activation(sqb[ft % 2][:], p.g3buf[:, ft * T:(ft + 1) * T],
                                  AF.Square).then_inc(s_sq, 1)
                    if ft % 8 == 7:
                        grp = ft // 8
                        sc.wait_ge(s_vr, grp + 1)
                        sc.activation(rsm[:, 0:T], rsm[:, 0:T], AF.Sqrt,
                                      bias=EPS_RMS).then_inc(s_sd, 1)

            @blk.tensor
            def _(t):
                for ft in range(64):
                    st_, sp_ = (ft % 8 == 0), (ft % 8 == 7)
                    t.wait_ge(s_sq, ft + 1)
                    if st_:
                        t.wait_ge(s_vr, ft // 8)
                    t.matmul(psV[:, 0:512], p.onescbf[:], sqb[ft % 2][:, 0:512],
                             start=st_, stop=sp_)
                    t.matmul(psV[:, 512:1024], p.onescbf[:], sqb[ft % 2][:, 512:1024],
                             start=st_, stop=sp_).then_inc(s_vm, 1)
                    if sp_:
                        grp = ft // 8
                        t.wait_ge(s_rc, grp + 1)
                        if grp >= 1:
                            t.wait_ge(s_sc6, grp * 8)
                        t.matmul(psR[:, 0:512], p.ones1[:], rsm[:, T: T + 512],
                                 start=True, stop=True)
                        t.matmul(psR[:, 512:1024], p.ones1[:], rsm[:, T + 512: 2 * T],
                                 start=True, stop=True).then_inc(s_rm, 1)

            @blk.vector
            def _(v):
                for ft in range(64):
                    v.wait_ge(s_sg, ft + 1)
                    v.wait_ge(s_yl[ft % 2], (ft // 2 + 1) * 16)
                    v.tensor_tensor(p.g3buf[:, ft * T:(ft + 1) * T], yfm[ft % 2][:],
                                    sgb[ft % 2][:], ALU.mult).then_inc(s_g2, 1)
                    if ft % 8 == 7:
                        grp = ft // 8
                        v.wait_ge(s_vm, (grp + 1) * 8)
                        v.tensor_scalar(rsm[:, 0:T], psV[:], 1.0 / 1024.0, None,
                                        ALU.mult).then_inc(s_vr, 1)
                        v.wait_ge(s_sd, grp + 1)
                        v.reciprocal(rsm[:, T:2 * T], rsm[:, 0:T]).then_inc(s_rc, 1)
                        v.wait_ge(s_rm, grp + 1)
                        for ft2 in range(grp * 8, grp * 8 + 8):
                            v.scalar_tensor_tensor(
                                p.g3buf[:, ft2 * T:(ft2 + 1) * T],
                                p.g3buf[:, ft2 * T:(ft2 + 1) * T],
                                p.nwt[:, ft2:ft2 + 1], psR[:],
                                ALU.mult, ALU.mult).then_inc(s_sc6, 1)


def phase7(nc, p):
    io = p.io
    with ExitStack() as c7:
        sb = lambda n, s, d: c7.enter_context(nc.sbuf_tensor(n, s, d))
        wob = [sb(f"p7w_{i}", [128, 4 * 512], BF16) for i in range(4)]
        x0s = [sb(f"p7x0_{i}", [128, 512], F32) for i in range(2)]
        x1s = [sb(f"p7x1_{i}", [128, 512], F32) for i in range(2)]
        pso = [c7.enter_context(nc.psum_tensor(f"p7ps_{i}", [128, 512], F32))
               for i in range(8)]
        s_w = [p.mksem(f"p7_w{i}") for i in range(4)]
        s_pc = p.mksem("p7_pc")
        s_mm = p.mksem("p7_mm")
        s_x0 = [p.mksem("p7_x0a"), p.mksem("p7_x0b")]
        s_ev = p.mksem("p7_ev")
        s_st = [p.mksem("p7_sta"), p.mksem("p7_stb")]

        with nc.Block() as blk:
            @blk.gpsimd
            def _(g):
                for o in range(8):
                    for ig in range(16):
                        pi = o * 16 + ig
                        if pi >= 4:
                            g.wait_ge(s_pc, pi - 3)
                        g.dma_start(out=wob[pi % 4][:],
                                    in_=io["woT"][ig * 512:(ig + 1) * 512,
                                                  o * 512:(o + 1) * 512]
                                    .rearrange("(k q) f -> q k f", q=128)
                                    ).then_inc(s_w[pi % 4], 16)

            @blk.tensor
            def _(t):
                for o in range(8):
                    for ig in range(16):
                        pi = o * 16 + ig
                        t.wait_ge(s_w[pi % 4], (pi // 4 + 1) * 16)
                        if ig == 0 and o >= 1:
                            t.wait_ge(s_ev, o * 8)
                        for i4 in range(4):
                            i = ig * 4 + i4
                            st_, sp_ = (i == 0), (i == 63)
                            for tt in range(8):
                                mm = t.matmul(pso[tt][:],
                                              p.g3buf[:, i * T + tt * 128:
                                                      i * T + (tt + 1) * 128],
                                              wob[pi % 4][:, i4 * 512:(i4 + 1) * 512],
                                              start=st_, stop=sp_)
                        mm.then_inc(s_pc, 1)

            @blk.sync
            def _(s):
                for o in range(8):
                    for tt in range(8):
                        idx = o * 8 + tt
                        if idx >= 2:
                            s.wait_ge(s_ev, idx - 1)
                        s.dma_start(out=x0s[tt % 2][:],
                                    in_=io["x_sl"][3 + tt * 128: 3 + (tt + 1) * 128,
                                                   o * 512:(o + 1) * 512]
                                    ).then_inc(s_x0[idx % 2], 16)
                        s.wait_ge(s_ev, idx + 1)
                        s.dma_start(out=p.x1_sp[tt * 128:(tt + 1) * 128,
                                                o * 512:(o + 1) * 512],
                                    in_=x1s[tt % 2][:]).then_inc(s_st[idx % 2], 16)
                s.wait_ge(s_st[0], 32 * 16); s.wait_ge(s_st[1], 32 * 16)

            @blk.vector
            def _(v):
                for o in range(8):
                    v.wait_ge(s_pc, (o + 1) * 16)
                    for tt in range(8):
                        idx = o * 8 + tt
                        v.wait_ge(s_x0[idx % 2], (idx // 2 + 1) * 16)
                        if idx >= 2:
                            v.wait_ge(s_st[idx % 2], (idx // 2) * 16)
                        v.tensor_add(x1s[tt % 2][:], x0s[tt % 2][:],
                                     pso[tt][:]).then_inc(s_ev, 1)

    if "tap_x1" in p.tap:
        sem = p.mksem("p7_tap")
        with nc.Block() as blk:
            @blk.sync
            def _(s):
                s.dma_start(out=p.tap["tap_x1"], in_=p.x1_sp).then_inc(sem, 16)
                s.wait_ge(sem, 16)


def phase8(nc, p):
    io = p.io
    scope8 = ExitStack()
    sb8 = lambda n, s, d: scope8.enter_context(nc.sbuf_tensor(n, s, d))
    ln2 = sb8("ln2", [128, NKT * T], BF16)
    biasm = sb8("biasm", [128, D], F32)
    browb = sb8("browb", [1, D], F32)
    zsb = sb8("zsb", [32, T], F32)
    zbf = sb8("zbf", [32, T], BF16)
    hsl = sb8("hsl", [8, 512], F32)
    dsl = sb8("dsl", [8, 512], F32)

    with ExitStack() as c8:
        sb = lambda n, s, d: c8.enter_context(nc.sbuf_tensor(n, s, d))
        x1t = [sb(f"p8x1_{i}", [128, D], F32) for i in range(2)]
        sqs = sb("p8sq", [128, D], F32)
        xn = [sb(f"p8xn_{i}", [128, D], BF16) for i in range(2)]
        st = sb("p8st", [128, 8], F32)
        tp = [c8.enter_context(nc.psum_tensor(f"p8tp_{i}", [128, 128], BF16))
              for i in range(2)]
        s_ld = [p.mksem("p8_lda"), p.mksem("p8_ldb")]
        s_sq = p.mksem("p8_sq")
        s_var = p.mksem("p8_var")
        s_sd = p.mksem("p8_sd")
        s_xn = p.mksem("p8_xn")
        s_t = p.mksem("p8_t")
        s_te = p.mksem("p8_te")

        with nc.Block() as blk:
            @blk.sync
            def _(s):
                for ti in range(8):
                    if ti >= 2:
                        s.wait_ge(s_xn, ti - 1)
                    s.dma_start(out=x1t[ti % 2][:],
                                in_=p.x1_sp[ti * 128:(ti + 1) * 128, :]
                                ).then_inc(s_ld[ti % 2], 16)

            @blk.scalar
            def _(sc):
                for ti in range(8):
                    sc.wait_ge(s_ld[ti % 2], (ti // 2 + 1) * 16)
                    sc.wait_ge(s_var, ti)
                    sc.activation(sqs[:], x1t[ti % 2][:], AF.Square,
                                  accum_out=st[:, 1:2]).then_inc(s_sq, 1)
                    sc.wait_ge(s_var, ti + 1)
                    sc.activation(st[:, 6:7], st[:, 5:6], AF.Sqrt,
                                  bias=EPS_LN).then_inc(s_sd, 1)

            @blk.vector
            def _(v):
                for ti in range(8):
                    v.wait_ge(s_ld[ti % 2], (ti // 2 + 1) * 16)
                    p.vf(v, v.reduce_sum(st[:, 0:1], x1t[ti % 2][:], axis=AX.X))
                    v.wait_ge(s_sq, ti + 1)
                    v.tensor_scalar(st[:, 2:3], st[:, 0:1], 1.0 / D, None, ALU.mult)
                    p.vf(v, v.tensor_scalar(st[:, 3:4], st[:, 1:2], 1.0 / D,
                                            None, ALU.mult))
                    p.vf(v, v.tensor_mul(st[:, 4:5], st[:, 2:3], st[:, 2:3]))
                    v.tensor_sub(st[:, 5:6], st[:, 3:4], st[:, 4:5]).then_inc(s_var, 1)
                    v.wait_ge(s_sd, ti + 1)
                    p.vf(v, v.reciprocal(st[:, 7:8], st[:, 6:7]))
                    v.tensor_scalar(xn[ti % 2][:], x1t[ti % 2][:], st[:, 2:3],
                                    st[:, 7:8], ALU.subtract,
                                    ALU.mult).then_inc(s_xn, 1)
                    for k in range(NKT):
                        v.wait_ge(s_t, ti * NKT + k + 1)
                        v.tensor_copy(ln2[:, k * T + ti * 128: k * T + (ti + 1) * 128],
                                      tp[k % 2][:]).then_inc(s_te, 1)

            @blk.tensor
            def _(t):
                for ti in range(8):
                    t.wait_ge(s_xn, ti + 1)
                    for k in range(NKT):
                        idx = ti * NKT + k
                        if idx >= 2:
                            t.wait_ge(s_te, idx - 1)
                        t.transpose(tp[k % 2][:], xn[ti % 2][:, k * 128:(k + 1) * 128],
                                    p.idbf[:]).then_inc(s_t, 1)

    with ExitStack() as c8:
        sb = lambda n, s, d: c8.enter_context(nc.sbuf_tensor(n, s, d))
        abuf = sb("p8ab", [128, 32 * 32], BF16)
        wf8 = sb("p8wf8", [128, 32 * 8], BF16)
        limTb = sb("p8limT", [32, 8], BF16)
        mcol = sb("p8mcol", [8, 2], F32)
        msc = sb("p8msc", [2, 8], F32)
        psb = c8.enter_context(nc.psum_tensor("p8psb", [128, 512], F32))
        psz = c8.enter_context(nc.psum_tensor("p8psz", [32, T], F32))
        pshs = c8.enter_context(nc.psum_tensor("p8pshs", [8, 512], F32))
        psds = c8.enter_context(nc.psum_tensor("p8psds", [8, 512], F32))
        psmx = c8.enter_context(nc.psum_tensor("p8psmx", [2, 8], F32))
        s_ld = p.mksem("p8b_ld")
        s_bm = p.mksem("p8b_bm")
        s_be = p.mksem("p8b_be")
        s_z = p.mksem("p8b_z")
        s_ze = p.mksem("p8b_ze")
        s_h = p.mksem("p8b_h")
        s_he = p.mksem("p8b_he")
        s_ds = p.mksem("p8b_ds")
        s_mx = p.mksem("p8b_mx")
        s_st = p.mksem("p8b_st")

        with nc.Block() as blk:
            @blk.gpsimd
            def _(g):
                g.dma_start(out=browb[:], in_=io["brow"]).then_inc(s_ld, 16)
                g.dma_start(out=abuf[:],
                            in_=io["amat"].rearrange("(k q) r -> q k r", q=128)
                            ).then_inc(s_ld, 16)
                g.dma_start(out=wf8[:],
                            in_=io["wfT"][:, 0:8].rearrange("(k q) e -> q k e", q=128)
                            ).then_inc(s_ld, 16)
                g.dma_start(out=limTb[:], in_=io["limT"]).then_inc(s_ld, 16)

            @blk.tensor
            def _(t):
                t.wait_ge(s_ld, 64)
                for o in range(8):
                    if o >= 1:
                        t.wait_ge(s_be, o)
                    t.matmul(psb[:], p.ones1[:], browb[:, o * 512:(o + 1) * 512],
                             start=True, stop=True).then_inc(s_bm, 1)
                t.wait_ge(s_ld, 64)
                for k in range(NKT):
                    st_, sp_ = (k == 0), (k == NKT - 1)
                    t.matmul(psz[:, 0:512], abuf[:, k * 32:(k + 1) * 32],
                             ln2[:, k * T: k * T + 512], start=st_, stop=sp_)
                    t.matmul(psz[:, 512:1024], abuf[:, k * 32:(k + 1) * 32],
                             ln2[:, k * T + 512:(k + 1) * T],
                             start=st_, stop=sp_).then_inc(s_z, 1)
                t.wait_ge(s_ld, 64)
                for k in range(NKT):
                    st_, sp_ = (k == 0), (k == NKT - 1)
                    odd = ln2[:, k * T:(k + 1) * T].rearrange(
                        "q (a two) -> q two a", two=2)[:, 1, :]
                    t.matmul(pshs[:], wf8[:, k * 8:(k + 1) * 8], odd,
                             start=st_, stop=sp_).then_inc(s_h, 1)
                t.wait_ge(s_ld, 64)
                t.wait_ge(s_ze, 2)
                zodd = zbf[:].rearrange("q (a two) -> q two a", two=2)[:, 1, :]
                t.matmul(psds[:], limTb[:], zodd, start=True,
                         stop=True).then_inc(s_ds, 1)
                t.wait_ge(s_mx, 1)
                t.transpose(psmx[:], mcol[:], p.idf32[0:8, 0:8]).then_inc(s_mx, 1)

            @blk.vector
            def _(v):
                for o in range(8):
                    v.wait_ge(s_bm, o + 1)
                    v.tensor_copy(biasm[:, o * 512:(o + 1) * 512],
                                  psb[:]).then_inc(s_be, 1)
                v.wait_ge(s_z, NKT)
                v.tensor_copy(zsb[:], psz[:]).then_inc(s_ze, 1)
                v.tensor_copy(zbf[:], psz[:]).then_inc(s_ze, 1)
                v.wait_ge(s_h, NKT)
                p.vf(v, v.tensor_scalar(hsl[:], pshs[:], p.b8[:], None,
                                        ALU.add), s_he, 1)
                v.wait_ge(s_ds, 1)
                p.vf(v, v.tensor_copy(dsl[:], psds[:]))
                v.tensor_reduce(mcol[:, 0:1], hsl[:], AX.X, ALU.max,
                                apply_absolute_value=True)
                p.vf(v, v.tensor_reduce(mcol[:, 1:2], dsl[:], AX.X, ALU.max,
                                        apply_absolute_value=True), s_mx, 1)
                v.wait_ge(s_mx, 2)
                v.tensor_reduce(msc[:, 0:1], psmx[:], AX.X, ALU.max).then_inc(s_mx, 1)

            @blk.sync
            def _(s):
                s.wait_ge(s_mx, 3)
                s.dma_start(out=p.msc_dram, in_=msc[:, 0:1]).then_inc(s_st, 16)
                s.wait_ge(s_st, 16)

    with ExitStack() as c8:
        sb = lambda n, s, d: c8.enter_context(nc.sbuf_tensor(n, s, d))
        wfb = [sb(f"p8wf_{i}", [128, 4 * 512], BF16) for i in range(4)]
        x1c = [sb(f"p8x1c_{i}", [128, 512], F32) for i in range(2)]
        x2c = [sb(f"p8x2c_{i}", [128, 512], F32) for i in range(2)]
        tmpc = sb("p8tmpc", [128, 512], F32)
        psh = [c8.enter_context(nc.psum_tensor(f"p8psh_{i}", [128, 512], F32))
               for i in range(8)]
        s_cc = p.mksem("p8_cc")
        s_w = [p.mksem(f"p8d_w{i}") for i in range(4)]
        s_pc = p.mksem("p8d_pc")
        s_mm = p.mksem("p8d_mm")
        s_x1 = [p.mksem("p8d_x1a"), p.mksem("p8d_x1b")]
        s_ev = p.mksem("p8d_ev")
        s_st = [p.mksem("p8d_sta"), p.mksem("p8d_stb")]
        p.s8_cc = s_cc

        with nc.Block() as blk:
            @blk.gpsimd
            def _(g):
                g.collective_compute(
                    "AllReduce", ALU.max,
                    replica_groups=[[0, 1, 2, 3, 4, 5, 6, 7]],
                    ins=[p.msc_dram.opt()], outs=[p.msc_out.opt()],
                ).then_inc(s_cc)
                for o in range(8):
                    for kg in range(8):
                        pi = o * 8 + kg
                        if pi >= 4:
                            g.wait_ge(s_pc, pi - 3)
                        g.dma_start(out=wfb[pi % 4][:],
                                    in_=io["wfT"][kg * 512:(kg + 1) * 512,
                                                  o * 512:(o + 1) * 512]
                                    .rearrange("(k q) f -> q k f", q=128)
                                    ).then_inc(s_w[pi % 4], 16)

            @blk.tensor
            def _(t):
                for o in range(8):
                    for kg in range(8):
                        pi = o * 8 + kg
                        t.wait_ge(s_w[pi % 4], (pi // 4 + 1) * 16)
                        if kg == 0 and o >= 1:
                            t.wait_ge(s_ev, o * 8)
                        for k4 in range(4):
                            k = kg * 4 + k4
                            st_, sp_ = (k == 0), (k == NKT - 1)
                            for tt in range(8):
                                mm = t.matmul(psh[tt][:],
                                              ln2[:, k * T + tt * 128:
                                                  k * T + (tt + 1) * 128],
                                              wfb[pi % 4][:, k4 * 512:(k4 + 1) * 512],
                                              start=st_, stop=sp_)
                        mm.then_inc(s_pc, 1)

            @blk.sync
            def _(s):
                for o in range(8):
                    for tt in range(8):
                        idx = o * 8 + tt
                        if idx >= 2:
                            s.wait_ge(s_ev, idx - 1)
                        s.dma_start(out=x1c[tt % 2][:],
                                    in_=p.x1_sp[tt * 128:(tt + 1) * 128,
                                                o * 512:(o + 1) * 512]
                                    ).then_inc(s_x1[idx % 2], 16)
                        s.wait_ge(s_ev, idx + 1)
                        s.dma_start(out=p.x2p_sp[tt * 128:(tt + 1) * 128,
                                                 o * 512:(o + 1) * 512],
                                    in_=x2c[tt % 2][:]).then_inc(s_st[idx % 2], 16)
                s.wait_ge(s_st[0], 32 * 16); s.wait_ge(s_st[1], 32 * 16)

            @blk.vector
            def _(v):
                for o in range(8):
                    v.wait_ge(s_pc, (o + 1) * 8)
                    for tt in range(8):
                        idx = o * 8 + tt
                        v.wait_ge(s_x1[idx % 2], (idx // 2 + 1) * 16)
                        if idx >= 2:
                            v.wait_ge(s_st[idx % 2], (idx // 2) * 16)
                        p.vf(v, v.tensor_add(tmpc[:], x1c[tt % 2][:],
                                             biasm[:, o * 512:(o + 1) * 512]))
                        v.tensor_add(x2c[tt % 2][:], tmpc[:],
                                     psh[tt][:]).then_inc(s_ev, 1)

    with ExitStack() as c8:
        sb = lambda n, s, d: c8.enter_context(nc.sbuf_tensor(n, s, d))
        mscs = sb("p8mscs", [2, 1], F32)
        rsc1 = sb("p8rsc1", [1, 2], F32)
        rscb = sb("p8rscb", [128, 2], F32)
        htok = sb("p8htok", [128, 4 * 8], F32)
        dtok8 = sb("p8dtok", [128, 4 * 8], F32)
        rtr = sb("p8rtr", [128, 8 * 8], F32)
        top8 = sb("p8top8", [128, 8], F32)
        wnb = sb("p8wnb", [128, 4 * 8], BF16)
        wTb = sb("p8wTb", [8, 4 * 128], BF16)
        limb = sb("p8limb", [8, 32], BF16)
        pmix = sb("p8pmix", [32, 512], F32)
        zpb = sb("p8zpb", [32, T], BF16)
        bmb = sb("p8bmb", [32, D], BF16)
        x2s = [sb(f"p8x2s_{i}", [128, 512], F32) for i in range(2)]
        outs = [sb(f"p8outs_{i}", [128, 512], F32) for i in range(2)]
        ps1 = c8.enter_context(nc.psum_tensor("p8e_ps1", [128, 512], F32))
        ps2 = c8.enter_context(nc.psum_tensor("p8e_ps2", [128, 512], F32))
        psw = c8.enter_context(nc.psum_tensor("p8e_psw", [8, 128], BF16))
        s_ld = p.mksem("p8e_ld")
        s_ldg = p.mksem("p8e_ldg")
        s_stp = p.mksem("p8e_stp")
        s_x2 = [p.mksem("p8e_x2a"), p.mksem("p8e_x2b")]
        s_ev = p.mksem("p8e_ev")
        s_mm = p.mksem("p8e_mm")
        s_st = [p.mksem("p8e_sta"), p.mksem("p8e_stb")]

        with nc.Block() as blk:
            @blk.sync
            def _(s):
                s.wait_ge(p.s8_cc, 1)
                s.dma_start(out=mscs[:], in_=p.msc_out).then_inc(s_ld, 16)
                s.wait_ge(s_ld, 16)
                s.wait_ge(s_ldg, 32)

            @blk.gpsimd
            def _(g):
                g.dma_start(out=limb[:], in_=io["limes"]).then_inc(s_ldg, 16)
                g.dma_start(out=bmb[:], in_=io["bmat"]).then_inc(s_ldg, 16)

        with nc.Block() as blk:
            @blk.vector
            def _(v):
                p.vf(v, v.tensor_scalar(mscs[:], mscs[:], 1e-6, None, ALU.max))
                p.vf(v, v.reciprocal(mscs[:], mscs[:]))
                v.tensor_scalar(mscs[:], mscs[:], 0.5, None, ALU.mult)

        with nc.Block() as blk:
            @blk.tensor
            def _(t):
                t.transpose(ps1[0:1, 0:2], mscs[:], p.idf32[0:2, 0:2]).then_inc(s_stp, 1)

            @blk.vector
            def _(v):
                v.wait_ge(s_stp, 1)
                v.tensor_copy(rsc1[:], ps1[0:1, 0:2])
        with nc.Block() as blk:
            @blk.tensor
            def _(t):
                t.matmul(ps1[:, 0:2], p.ones1[:], rsc1[:], start=True,
                         stop=True).then_inc(s_stp, 1)

            @blk.vector
            def _(v):
                v.wait_ge(s_stp, 2)
                v.tensor_copy(rscb[:], ps1[:, 0:2])

        s_t4 = p.mksem("p8e_t4")
        s_v4 = p.mksem("p8e_v4")
        with nc.Block() as blk:
            @blk.tensor
            def _(t):
                for j in range(4):
                    if j >= 1:
                        t.wait_ge(s_v4, j)
                    t.transpose(ps1[:, 0:8], hsl[:, j * 128:(j + 1) * 128],
                                p.idf32[0:8, 0:8])
                    t.transpose(ps2[:, 0:8], dsl[:, j * 128:(j + 1) * 128],
                                p.idf32[0:8, 0:8]).then_inc(s_t4, 1)

            @blk.vector
            def _(v):
                for j in range(4):
                    v.wait_ge(s_t4, j + 1)
                    v.tensor_copy(htok[:, j * 8:(j + 1) * 8], ps1[:, 0:8])
                    v.tensor_copy(dtok8[:, j * 8:(j + 1) * 8],
                                  ps2[:, 0:8]).then_inc(s_v4, 1)

        for j in range(4):
            with nc.Block() as blk:
                @blk.vector
                def _(v):
                    hj = htok[:, j * 8:(j + 1) * 8]
                    dj = dtok8[:, j * 8:(j + 1) * 8]
                    p.vf(v, v.tensor_scalar(rtr[:, 8:16], dj, rscb[:, 1:2],
                                            None, ALU.mult))
                    v.scalar_tensor_tensor(rtr[:, 0:8], hj, rscb[:, 0:1],
                                           rtr[:, 8:16], ALU.mult, ALU.add)
            with nc.Block() as blk:
                @blk.scalar
                def _(sc):
                    sc.activation(rtr[:, 16:24], rtr[:, 0:8], AF.Exp)
            with nc.Block() as blk:
                @blk.vector
                def _(v):
                    p.vf(v, v.reduce_sum(rtr[:, 24:25], rtr[:, 16:24], axis=AX.X))
                    p.vf(v, v.reciprocal(rtr[:, 25:26], rtr[:, 24:25]))
                    p.vf(v, v.tensor_scalar(rtr[:, 32:40], rtr[:, 16:24],
                                            rtr[:, 25:26], None, ALU.mult))
                    p.vf(v, v.max(top8[:], rtr[:, 32:40]))
                    v.tensor_scalar(rtr[:, 26:27], top8[:, 1:2], -2.0, None,
                                    ALU.mult)
            with nc.Block() as blk:
                @blk.scalar
                def _(sc):
                    sc.activation(rtr[:, 40:48], rtr[:, 32:40], AF.Sigmoid,
                                  scale=2.0, bias=rtr[:, 26:27])
            with nc.Block() as blk:
                @blk.vector
                def _(v):
                    p.vf(v, v.tensor_mul(rtr[:, 48:56], rtr[:, 32:40],
                                         rtr[:, 40:48]))
                    p.vf(v, v.reduce_sum(rtr[:, 24:25], rtr[:, 48:56], axis=AX.X))
                    p.vf(v, v.tensor_scalar(rtr[:, 24:25], rtr[:, 24:25], 1e-9,
                                            None, ALU.add))
                    p.vf(v, v.reciprocal(rtr[:, 25:26], rtr[:, 24:25]))
                    v.tensor_scalar(wnb[:, j * 8:(j + 1) * 8], rtr[:, 48:56],
                                    rtr[:, 25:26], None, ALU.mult)

        s_t6 = p.mksem("p8e_t6")
        s_v6 = p.mksem("p8e_v6")
        with nc.Block() as blk:
            @blk.tensor
            def _(t):
                for j in range(4):
                    if j >= 1:
                        t.wait_ge(s_v6, j)
                    t.transpose(psw[:], wnb[:, j * 8:(j + 1) * 8],
                                p.idbf[:]).then_inc(s_t6, 1)

            @blk.vector
            def _(v):
                for j in range(4):
                    v.wait_ge(s_t6, j + 1)
                    v.tensor_copy(wTb[:, j * 128:(j + 1) * 128],
                                  psw[:]).then_inc(s_v6, 1)
        with nc.Block() as blk:
            @blk.tensor
            def _(t):
                t.matmul(ps2[0:32, 0:512], limb[:], wTb[:], start=True,
                         stop=True).then_inc(s_t6, 1)

            @blk.vector
            def _(v):
                v.wait_ge(s_t6, 5)
                p.vf(v, v.tensor_copy(pmix[:], ps2[0:32, 0:512]))
                v.tensor_tensor(zpb[:], zbf[:], bc2(pmix[:], 512, 2), ALU.mult)

        with nc.Block() as blk:
            @blk.tensor
            def _(t):
                for tt in range(8):
                    for o in range(8):
                        idx = tt * 8 + o
                        dstp = ps1 if idx % 2 == 0 else ps2
                        if idx >= 2:
                            t.wait_ge(s_ev, idx - 1)
                        t.matmul(dstp[:], zpb[:, tt * 128:(tt + 1) * 128],
                                 bmb[:, o * 512:(o + 1) * 512],
                                 start=True, stop=True).then_inc(s_mm, 1)

            @blk.vector
            def _(v):
                for tt in range(8):
                    for o in range(8):
                        idx = tt * 8 + o
                        v.wait_ge(s_mm, idx + 1)
                        v.wait_ge(s_x2[idx % 2], (idx // 2 + 1) * 16)
                        if idx >= 2:
                            v.wait_ge(s_st[idx % 2], (idx // 2) * 16)
                        v.scalar_tensor_tensor(outs[idx % 2][:],
                                               (ps1 if idx % 2 == 0 else ps2)[:],
                                               0.5, x2s[idx % 2][:],
                                               ALU.mult, ALU.add).then_inc(s_ev, 1)

            @blk.sync
            def _(s):
                for tt in range(8):
                    for o in range(8):
                        idx = tt * 8 + o
                        if idx >= 2:
                            s.wait_ge(s_ev, idx - 1)
                        s.dma_start(out=x2s[idx % 2][:],
                                    in_=p.x2p_sp[tt * 128:(tt + 1) * 128,
                                                 o * 512:(o + 1) * 512]
                                    ).then_inc(s_x2[idx % 2], 16)
                        s.wait_ge(s_ev, idx + 1)
                        s.dma_start(out=io["out"][tt * 128:(tt + 1) * 128,
                                                  o * 512:(o + 1) * 512],
                                    in_=outs[idx % 2][:]).then_inc(s_st[idx % 2], 16)
                s.wait_ge(s_st[0], 32 * 16)
                s.wait_ge(s_st[1], 32 * 16)

    scope8.close()

    if "tap_x2p" in p.tap or "tap_z" in p.tap or "tap_hsl" in p.tap:
        sem = p.mksem("p8_tap")
        with nc.Block() as blk:
            @blk.sync
            def _(s):
                n = 0
                if "tap_x2p" in p.tap:
                    s.dma_start(out=p.tap["tap_x2p"], in_=p.x2p_sp).then_inc(sem, 16)
                    n += 16
                if "tap_msc" in p.tap:
                    s.dma_start(out=p.tap["tap_msc"], in_=p.msc_out).then_inc(sem, 16)
                    n += 16
                s.wait_ge(sem, n)


def finalphase(nc, p):
    pass


def prep_in_maps(inputs):
    f32 = np.float32
    x = np.ascontiguousarray(inputs["L0_attn_x"], dtype=f32)
    wiT = np.ascontiguousarray(inputs["L0_attn_in_proj_w"].T, dtype=f32)
    wiT_g = np.ascontiguousarray(wiT[:, 0:INTER])
    wiT_a = np.ascontiguousarray(wiT[:, INTER:])
    woT = np.ascontiguousarray(inputs["L0_attn_out_proj_w"].T, dtype=f32)
    wfT = np.ascontiguousarray(inputs["L0_ffn_W"].T, dtype=f32)
    conv_w = np.ascontiguousarray(inputs["L0_attn_conv_w"], dtype=f32).reshape(10240, 4)
    cwt = np.ascontiguousarray(conv_w.reshape(80, 128, 4).transpose(1, 0, 2)
                               .reshape(128, 320))
    cbt = np.ascontiguousarray(
        inputs["L0_attn_conv_b"].astype(f32).reshape(80, 128).T)
    nwt = np.ascontiguousarray(
        inputs["L0_attn_norm_w"].astype(f32).reshape(64, 128).T)
    alogc = np.ascontiguousarray(inputs["L0_attn_A_log"].astype(f32).reshape(128, 1))
    dtbc = np.ascontiguousarray(inputs["L0_attn_dt_bias"].astype(f32).reshape(128, 1))
    dvecc = np.ascontiguousarray(inputs["L0_attn_D"].astype(f32).reshape(128, 1))
    amat = np.ascontiguousarray(inputs["L0_ffn_A"], dtype=f32)
    bmat = np.ascontiguousarray(inputs["L0_ffn_Bmat"], dtype=f32)
    limes = np.ascontiguousarray(inputs["L0_ffn_LiMEs"], dtype=f32)
    limT = np.ascontiguousarray(limes.T)
    b = inputs["L0_ffn_b"].astype(f32)
    b8v = np.ascontiguousarray(b[0:8].reshape(8, 1))
    brow = np.ascontiguousarray(b.reshape(1, D))
    li = np.arange(128)
    mnegT = np.where(li[:, None] >= li[None, :], 0.0, NEG).astype(f32)
    idbf = np.eye(128, dtype=BF)
    idf32 = np.eye(128, dtype=f32)

    in_maps = []
    for core in range(8):
        bb, q = core // 4, core % 4
        t0 = q * 1024
        if q == 0:
            x_sl = np.concatenate([np.zeros((3, D), f32), x[bb, 0:1024]], axis=0)
        else:
            x_sl = np.ascontiguousarray(x[bb, t0 - 3: t0 + 1024])
        maskq = np.zeros((128, 4), f32)
        maskq[:, :q] = 1.0
        qsel = np.zeros((128, 4), f32)
        qsel[:, q] = 1.0
        in_maps.append({
            "x_sl": x_sl, "wiT_a": wiT_a, "wiT_g": wiT_g, "woT": woT,
            "wfT": wfT, "amat": amat, "bmat": bmat, "limes": limes,
            "limT": limT, "cwt": cwt, "cbt": cbt, "nwt": nwt,
            "alogc": alogc, "dtbc": dtbc, "dvecc": dvecc, "maskq": maskq,
            "qsel": qsel, "b8": b8v, "brow": brow, "mnegT": mnegT,
            "idbf": idbf, "idf32": idf32,
        })
    return in_maps


def kernel(**inputs):
    in_maps = prep_in_maps(inputs)
    nc = build_nc()
    res = run_bass_kernel_spmd(nc, in_maps, core_ids=list(range(8)))
    out = np.empty((2, 4096, D), np.float32)
    for core in range(8):
        bb, q = core // 4, core % 4
        out[bb, q * 1024:(q + 1) * 1024] = res.results[core]["out"]
    return out


# version 03 (final)
# speedup vs baseline: 1.0212x; 1.0036x over previous
import numpy as np
import ml_dtypes
from contextlib import ExitStack

import concourse.bass as bass
import concourse.mybir as mybir
from concourse.bass_utils import run_bass_kernel_spmd

F32 = mybir.dt.float32
BF16 = mybir.dt.bfloat16
AF = mybir.ActivationFunctionType
ALU = mybir.AluOpType
AX = mybir.AxisListType
BF = ml_dtypes.bfloat16

D = 4096
T = 1024
TH = 1027
HN = 128
PD = 64
G = 8
NCH = 8
INTER = 8192
FA = 10368
NKT = 32
NEG = -1.0e30
EPS_LN = 1e-5
EPS_RMS = 1e-5


class Ph:
    pass


def bc2(ap, n0, n1):
    return ap.unsqueeze(2).broadcast_to([ap.shape[0], n0, n1])


def build_nc(taps=(), upto=99):
    nc = bass.Bass("TRN2", target_bir_lowering=False)
    p = Ph()
    p.taps = set(taps)

    io = {}

    def inp(name, shape, dt=F32):
        io[name] = nc.dram_tensor(name, shape, dt, kind="ExternalInput").ap()

    inp("x_sl", [TH, D])
    inp("wiT_a", [D, FA])
    inp("wiT_g", [D, INTER])
    inp("woT", [INTER, D])
    inp("wfT", [D, D])
    inp("amat", [D, 32])
    inp("bmat", [32, D])
    inp("limes", [8, 32])
    inp("limT", [32, 8])
    inp("cwt", [128, 320])
    inp("cbt", [128, 80])
    inp("nwt", [128, 64])
    inp("alogc", [128, 1])
    inp("dtbc", [128, 1])
    inp("dvecc", [128, 1])
    inp("maskq", [128, 4])
    inp("qsel", [128, 4])
    inp("b8", [8, 1])
    inp("brow", [1, D])
    inp("mnegT", [128, 128])
    inp("idbf", [128, 128], BF16)
    inp("idf32", [128, 128])
    io["out"] = nc.dram_tensor("out", [T, D], F32, kind="ExternalOutput").ap()
    p.io = io

    p.xch_sp = nc.dram_tensor("xch_sp", [NCH, 128, INTER], BF16).ap()
    p.yfm_sp = nc.dram_tensor("yfm_sp", [64, 128, T], BF16).ap()
    p.gate_sp = nc.dram_tensor("gate_sp", [64, 128, T], BF16).ap()
    p.x1_sp = nc.dram_tensor("x1_sp", [T, D], F32).ap()
    p.x2p_sp = nc.dram_tensor("x2p_sp", [T, D], F32).ap()
    p.carry_dram = nc.dram_tensor("carry_dram", [128, INTER + 1], F32).ap()
    p.gath_dram = nc.dram_tensor("gath_dram", [512, INTER + 1], F32).ap()
    p.msc_dram = nc.dram_tensor("msc_dram", [2, 1], F32).ap()
    p.msc_out = nc.dram_tensor("msc_out", [2, 1], F32).ap()

    p.tap = {}

    def mktap(name, shape, dt):
        if name in p.taps:
            p.tap[name] = nc.dram_tensor(name, shape, dt, kind="ExternalOutput").ap()

    mktap("tap_ln1", [128, NKT * TH], BF16)
    mktap("tap_dt", [128, TH], F32)
    mktap("tap_acg", [128, T], F32)
    mktap("tap_bc", [128, 16 * T], BF16)
    mktap("tap_btok", [128, 64 * 128], BF16)
    mktap("tap_carry", [128, INTER], F32)
    mktap("tap_xch", [NCH, 128, INTER], BF16)
    mktap("tap_R", [128, INTER], F32)
    mktap("tap_y", [64, 128, T], BF16)
    mktap("tap_g3", [64, 128, T], BF16)
    mktap("tap_x1", [T, D], F32)
    mktap("tap_x2p", [T, D], F32)
    mktap("tap_hsl", [8, 512], F32)
    mktap("tap_dsl", [8, 512], F32)
    mktap("tap_z", [32, T], F32)
    mktap("tap_msc", [2, 1], F32)

    with ExitStack() as ctx:
        p.ctx = ctx
        p.nc = nc
        p.mksb = lambda n, s, d: ctx.enter_context(nc.sbuf_tensor(n, s, d))
        p.phase_sems = []

        def mksem(n):
            h = nc.alloc_semaphore(name=n)
            p.phase_sems.append(h)
            return h

        def free_sems():
            if p.phase_sems:
                nc.clear_and_free_semaphores(p.phase_sems)
                p.phase_sems = []
        p.mksem = mksem
        p.free_sems = free_sems
        p.vs = nc.alloc_semaphore(name="vself")
        p.ps_ = nc.alloc_semaphore(name="pself")
        p.vsn = 0
        p.psn = 0

        def vf(v, inst, sem=None, k=1):
            p.vsn += 1
            inst.then_inc(p.vs, 1)
            v.wait_ge(p.vs, p.vsn)
            if sem is not None:
                v.engine_nop().then_inc(sem, k)
            return inst

        def pf(g, inst, sem=None, k=1):
            p.psn += 1
            inst.then_inc(p.ps_, 1)
            g.wait_ge(p.ps_, p.psn)
            if sem is not None:
                g.engine_nop().then_inc(sem, k)
            return inst
        p.ss = nc.alloc_semaphore(name="sself")
        p.ssn = 0

        def sf(sc, inst, sem=None, k=1):
            p.ssn += 1
            inst.then_inc(p.ss, 1)
            sc.wait_ge(p.ss, p.ssn)
            if sem is not None:
                sc.nop().then_inc(sem, k)
            return inst
        p.vf = vf
        p.pf = pf
        p.sf = sf

        p.ones1 = p.mksb("ones1", [1, 128], F32)
        p.onescbf = p.mksb("onescbf", [128, 1], BF16)
        p.ones128 = p.mksb("ones128", [128, 128], F32)
        p.mneg = p.mksb("mneg_s", [128, 128], F32)
        p.idbf = p.mksb("idbf_s", [128, 128], BF16)
        p.idf32 = p.mksb("idf32_s", [128, 128], F32)
        p.cwt = p.mksb("cwt_s", [128, 320], F32)
        p.cbt = p.mksb("cbt_s", [128, 80], F32)
        p.nwt = p.mksb("nwt_s", [128, 64], F32)
        p.alog = p.mksb("alog_s", [128, 1], F32)
        p.aneg = p.mksb("aneg_s", [128, 1], F32)
        p.dtb = p.mksb("dtb_s", [128, 1], F32)
        p.dvec = p.mksb("dvec_s", [128, 1], F32)
        p.maskq = p.mksb("maskq_s", [128, 4], F32)
        p.qsel = p.mksb("qsel_s", [128, 4], F32)
        p.b8 = p.mksb("b8_s", [8, 1], F32)
        p.cst = {}
        for cv in (1e-5, -1.0, 2.0):
            ct = p.mksb(f"cst_{abs(cv):.0e}_{'n' if cv < 0 else 'p'}", [128, 1], F32)
            nc.const_aps.register(F32, cv, ct[:, 0:1])
            p.cst[cv] = ct

        p.scopeA = ExitStack()
        sbA = lambda n, s, d: p.scopeA.enter_context(nc.sbuf_tensor(n, s, d))
        p.scopeLn = ExitStack()
        p.ln1 = p.scopeLn.enter_context(
            nc.sbuf_tensor("ln1", [128, NKT * TH], BF16, side="right"))
        p.dtf = sbA("dtf", [128, TH], F32)
        p.acg = sbA("acg", [128, T], F32)
        p.bcfm = sbA("bcfm", [128, 16 * T], BF16)
        p.btok = sbA("btok", [128, 64 * 128], BF16)
        p.rbuf = sbA("rbuf", [128, INTER], F32)

        phase0(nc, p)
        p.free_sems()
        if upto >= 1:
            phase1(nc, p)
            tapblk(nc, p, [("tap_ln1", p.ln1)])
            p.free_sems()
        if upto >= 2:
            p.scope23 = ExitStack()
            p.dttok = p.scope23.enter_context(
                nc.sbuf_tensor("dttok", [128, NCH * 128], F32))
            p.wftok = p.scope23.enter_context(
                nc.sbuf_tensor("wftok", [128, NCH * 128], F32))
            phase2(nc, p)
            tapblk(nc, p, [("tap_dt", p.dtf), ("tap_acg", p.acg)])
            p.free_sems()
        if upto >= 3:
            phase3(nc, p)
            tapblk(nc, p, [("tap_bc", p.bcfm), ("tap_btok", p.btok),
                           ("tap_carry", p.rbuf)])
            p.free_sems()
        if upto >= 2:
            p.scope23.close()
        p.scopeLn.close()
        if upto >= 4:
            p.scopeB = ExitStack()
            p.rb = p.scopeB.enter_context(
                nc.sbuf_tensor("rb", [128, INTER], BF16))
            phase4(nc, p)
            tapblk(nc, p, [("tap_R", p.rbuf)])
            p.free_sems()
        if upto >= 5:
            phase5(nc, p)
            p.free_sems()
        if upto >= 4:
            p.scopeB.close()
        p.scopeA.close()
        if upto >= 6:
            p.scope67 = ExitStack()
            p.g3buf = p.scope67.enter_context(
                nc.sbuf_tensor("g3buf", [128, 64 * T], BF16))
            phase6(nc, p)
            tapblk(nc, p, [("tap_g3", p.g3buf)])
            p.free_sems()
        if upto >= 7:
            phase7(nc, p)
            p.free_sems()
        if upto >= 6:
            p.scope67.close()
        if upto >= 8:
            phase8(nc, p)
        finalphase(nc, p)
    return nc


def tapblk(nc, p, items):
    items = [(n, t) for (n, t) in items if n in p.tap]
    if not items:
        return
    sem = p.mksem("tap_" + items[0][0])
    with nc.Block() as blk:
        @blk.sync
        def _(s):
            n = 0
            for name, src in items:
                s.dma_start(out=p.tap[name], in_=src[:]).then_inc(sem, 16)
                n += 16
            s.wait_ge(sem, n)


def phase0(nc, p):
    io = p.io
    s_ld = p.mksem("p0_ld")
    with nc.Block() as blk:
        @blk.gpsimd
        def _(g):
            g.memset(p.ones1[:], 1.0)
            g.memset(p.onescbf[:], 1.0)
            g.memset(p.ones128[:], 1.0)
            for cv, ct in p.cst.items():
                g.memset(ct[:], cv)

        @blk.sync
        def _(s):
            n = 0
            for dst, src in [(p.mneg, "mnegT"), (p.idbf, "idbf"), (p.idf32, "idf32"),
                             (p.cwt, "cwt"), (p.cbt, "cbt"), (p.nwt, "nwt"),
                             (p.alog, "alogc"), (p.dtb, "dtbc"), (p.dvec, "dvecc"),
                             (p.maskq, "maskq"), (p.qsel, "qsel"),
                             (p.b8, "b8")]:
                s.dma_start(out=dst[:], in_=io[src]).then_inc(s_ld, 16)
                n += 16
            s.wait_ge(s_ld, n)
    with nc.Block() as blk:
        @blk.scalar
        def _(sc):
            p.sf(sc, sc.activation(p.aneg[:], p.alog[:], AF.Exp))
            sc.mul(p.aneg[:], p.aneg[:], -1.0)


def phase1(nc, p):
    io = p.io
    NT = 9
    with ExitStack() as c1:
        sb = lambda n, s, d: c1.enter_context(nc.sbuf_tensor(n, s, d))
        x0 = [sb(f"x0_{i}", [128, D], F32) for i in range(2)]
        sqs = p.bcfm[:, 0:2 * D].bitcast(F32)
        rbv = p.rbuf[:, 0:D].bitcast(BF16)
        xn = [rbv[:, 0:D], rbv[:, D:2 * D]]
        st = sb("lnst", [128, 8], F32)
        tp = [c1.enter_context(nc.psum_tensor(f"lntp_{i}", [128, 128], BF16))
              for i in range(2)]
        s_ld = [p.mksem("p1_lda"), p.mksem("p1_ldb")]
        s_sq = p.mksem("p1_sq")
        s_var = p.mksem("p1_var")
        s_sd = p.mksem("p1_sd")
        s_xn = p.mksem("p1_xn")
        s_t = p.mksem("p1_t")
        s_te = p.mksem("p1_te")

        with nc.Block() as blk:
            @blk.sync
            def _(s):
                for ti in range(NT):
                    rows = 128 if ti < 8 else 3
                    if ti >= 2:
                        s.wait_ge(s_xn, ti - 1)
                    s.dma_start(out=x0[ti % 2][0:rows, :],
                                in_=io["x_sl"][ti * 128: ti * 128 + rows, :]
                                ).then_inc(s_ld[ti % 2], 16)

            @blk.scalar
            def _(sc):
                for ti in range(NT):
                    sc.wait_ge(s_ld[ti % 2], (ti // 2 + 1) * 16)
                    sc.wait_ge(s_var, ti)
                    sc.activation(sqs, x0[ti % 2][:], AF.Square,
                                  accum_out=st[:, 1:2]).then_inc(s_sq, 1)
                    sc.wait_ge(s_var, ti + 1)
                    sc.activation(st[:, 6:7], st[:, 5:6], AF.Sqrt,
                                  bias=EPS_LN).then_inc(s_sd, 1)

            @blk.vector
            def _(v):
                for ti in range(NT):
                    v.wait_ge(s_ld[ti % 2], (ti // 2 + 1) * 16)
                    p.vf(v, v.reduce_sum(st[:, 0:1], x0[ti % 2][:], axis=AX.X))
                    v.wait_ge(s_sq, ti + 1)
                    v.tensor_scalar(st[:, 2:3], st[:, 0:1], 1.0 / D, None, ALU.mult)
                    p.vf(v, v.tensor_scalar(st[:, 3:4], st[:, 1:2], 1.0 / D,
                                            None, ALU.mult))
                    p.vf(v, v.tensor_mul(st[:, 4:5], st[:, 2:3], st[:, 2:3]))
                    v.tensor_sub(st[:, 5:6], st[:, 3:4], st[:, 4:5]).then_inc(s_var, 1)
                    v.wait_ge(s_sd, ti + 1)
                    p.vf(v, v.reciprocal(st[:, 7:8], st[:, 6:7]))
                    v.tensor_scalar(xn[ti % 2], x0[ti % 2][:], st[:, 2:3],
                                    st[:, 7:8], ALU.subtract,
                                    ALU.mult).then_inc(s_xn, 1)
                    cols = 128 if ti < 8 else 3
                    for k in range(NKT):
                        v.wait_ge(s_t, ti * NKT + k + 1)
                        v.tensor_copy(
                            p.ln1[:, k * TH + ti * 128: k * TH + ti * 128 + cols],
                            tp[k % 2][:, 0:cols]).then_inc(s_te, 1)

            @blk.tensor
            def _(t):
                for ti in range(NT):
                    t.wait_ge(s_xn, ti + 1)
                    for k in range(NKT):
                        idx = ti * NKT + k
                        if idx >= 2:
                            t.wait_ge(s_te, idx - 1)
                        t.transpose(tp[k % 2][:], xn[ti % 2][:, k * 128:(k + 1) * 128],
                                    p.idbf[:]).then_inc(s_t, 1)


def phase2(nc, p):
    io = p.io
    with ExitStack() as c2:
        sb = lambda n, s, d: c2.enter_context(nc.sbuf_tensor(n, s, d))
        wdt = [sb(f"wdt_{i}", [128, 4 * 128], BF16) for i in range(2)]
        dtraw = sb("dtraw", [128, TH], F32)
        t1 = sb("p2t1", [128, TH], F32)
        t2 = sb("p2t2", [128, TH], F32)
        wff = sb("wff", [128, T], F32)
        psdt = c2.enter_context(nc.psum_tensor("psdt", [128, 1536], F32))
        pst = c2.enter_context(nc.psum_tensor("p2pst", [128, 128], F32))
        s_w = [p.mksem("p2_wa"), p.mksem("p2_wb")]
        s_mm = p.mksem("p2_mm")
        s_dv = p.mksem("p2_dv")
        s_sc = p.mksem("p2_sc")
        s_t = p.mksem("p2_t")
        s_te = p.mksem("p2_te")

        with nc.Block() as blk:
            @blk.gpsimd
            def _(g):
                for kg in range(8):
                    if kg >= 2:
                        g.wait_ge(s_mm, kg - 1)
                    g.dma_start(
                        out=wdt[kg % 2][:],
                        in_=io["wiT_a"][kg * 512:(kg + 1) * 512, 10240:10368]
                        .rearrange("(k q) f -> q k f", q=128)
                        ).then_inc(s_w[kg % 2], 16)

            @blk.tensor
            def _(t):
                for kg in range(8):
                    t.wait_ge(s_w[kg % 2], (kg // 2 + 1) * 16)
                    for k4 in range(4):
                        k = kg * 4 + k4
                        lhs = wdt[kg % 2][:, k4 * 128:(k4 + 1) * 128]
                        st_, sp_ = (k == 0), (k == 31)
                        t.matmul(psdt[:, 0:512], lhs,
                                 p.ln1[:, k * TH: k * TH + 512], start=st_, stop=sp_)
                        t.matmul(psdt[:, 512:1024], lhs,
                                 p.ln1[:, k * TH + 512: k * TH + 1024],
                                 start=st_, stop=sp_)
                        mm = t.matmul(psdt[:, 1024:1027], lhs,
                                      p.ln1[:, k * TH + 1024: k * TH + 1027],
                                      start=st_, stop=sp_)
                    mm.then_inc(s_mm, 1)
                for c in range(NCH):
                    m = 2 * c
                    t.wait_ge(s_dv, 3)
                    if m >= 1:
                        t.wait_ge(s_te, m)
                    t.transpose(pst[:], p.dtf[:, 3 + c * 128: 3 + (c + 1) * 128],
                                p.idf32[:]).then_inc(s_t, 1)
                    t.wait_ge(s_sc, 5)
                    t.wait_ge(s_te, m + 1)
                    t.transpose(pst[:], wff[:, c * 128:(c + 1) * 128],
                                p.idf32[:]).then_inc(s_t, 1)

            @blk.scalar
            def _(sc):
                sc.wait_ge(s_mm, 8)
                p.sf(sc, sc.copy(dtraw[:], psdt[:, 0:1027]), s_sc, 1)
                p.sf(sc, sc.activation(t1[:], dtraw[:], AF.Abs, bias=p.dtb[:]))
                p.sf(sc, sc.activation(t1[:], t1[:], AF.Exp, scale=-1.0))
                sc.activation(t1[:], t1[:], AF.Ln, bias=1.0).then_inc(s_sc, 1)
                sc.wait_ge(s_dv, 4)
                sc.activation(wff[:], p.acg[:], AF.Exp, scale=-1.0,
                              bias=p.acg[:, T - 1:T]).then_inc(s_sc, 3)

            @blk.vector
            def _(v):
                v.wait_ge(s_sc, 2)
                p.vf(v, v.tensor_scalar(t2[:], dtraw[:], p.dtb[:], 0.0,
                                        ALU.add, ALU.max))
                p.vf(v, v.tensor_add(t2[:], t2[:], t1[:]))
                p.vf(v, v.tensor_scalar(p.dtf[:], t2[:], 100.0, None,
                                        ALU.min), s_dv, 3)
                p.vf(v, v.tensor_scalar(t1[:, 0:T], p.dtf[:, 3:TH], p.aneg[:],
                                        None, ALU.mult))
                v.tensor_tensor_scan(p.acg[:], t1[:, 0:T],
                                     p.ones128[:, 0:1].broadcast_to([128, T]),
                                     0.0, ALU.add, ALU.mult).then_inc(s_dv, 1)
                for c in range(NCH):
                    m = 2 * c
                    v.wait_ge(s_t, m + 1)
                    v.tensor_copy(p.dttok[:, c * 128:(c + 1) * 128],
                                  pst[:]).then_inc(s_te, 1)
                    v.wait_ge(s_t, m + 2)
                    v.tensor_copy(p.wftok[:, c * 128:(c + 1) * 128],
                                  pst[:]).then_inc(s_te, 1)


def phase3(nc, p):
    io = p.io
    with ExitStack() as c3:
        sb = lambda n, s, d: c3.enter_context(nc.sbuf_tensor(n, s, d))
        wbuf = [sb(f"p3w_{i}", [128, 4 * 256], BF16) for i in range(4)]
        raw = [sb(f"p3raw_{i}", [128, TH], F32) for i in range(2)]
        cacc = [sb(f"p3cv_{i}", [128, T], F32) for i in range(2)]
        hsb = [sb(f"p3hs_{i}", [128, T], BF16) for i in range(2)]
        xst = [sb(f"p3xs_{i}", [128, T], BF16) for i in range(2)]
        xds = [sb(f"p3xd_{i}", [128, 128], BF16) for i in range(2)]
        gst = [sb(f"p3gs_{i}", [128, T], BF16) for i in range(2)]
        psA = [c3.enter_context(nc.psum_tensor(f"p3psA_{i}", [128, 1536], F32))
               for i in range(2)]
        psT = c3.enter_context(nc.psum_tensor("p3psT", [128, 128], BF16))
        psC = c3.enter_context(nc.psum_tensor("p3psC", [128, 128], F32))
        s_w = [p.mksem(f"p3_w{i}") for i in range(4)]
        s_pc = p.mksem("p3_pc")
        s_mm = p.mksem("p3_mm")
        s_raw = p.mksem("p3_raw")
        s_cv = p.mksem("p3_cv")
        s_si = p.mksem("p3_si")
        s_t = p.mksem("p3_t")
        s_te = p.mksem("p3_te")
        s_xd = p.mksem("p3_xd")
        s_cy = p.mksem("p3_cy")
        s_ce = p.mksem("p3_ce")
        s_sp = [p.mksem("p3_spa"), p.mksem("p3_spb")]
        s_ge = p.mksem("p3_ge")
        s_gs = [p.mksem("p3_gsa"), p.mksem("p3_gsb")]

        tiles = ([("bc", i) for i in range(16)] + [("hs", i) for i in range(64)]
                 + [("gt", i) for i in range(64)])
        npair = 72

        def wsrc(kind, i):
            if kind == "gt":
                return io["wiT_g"], i * 128
            return io["wiT_a"], (8192 + i * 128) if kind == "bc" else i * 128

        def psum_free_wait(eng, ti):
            kind, i = tiles[ti]
            if kind == "gt":
                eng.wait_ge(s_ge, i + 1)
            else:
                eng.wait_ge(s_raw, ti + 1)

        with nc.Block() as blk:
            @blk.gpsimd
            def _(g):
                for pr in range(npair):
                    wt, base = wsrc(*tiles[2 * pr])
                    for kg in range(8):
                        pi = pr * 8 + kg
                        if pi >= 4:
                            g.wait_ge(s_pc, pi - 3)
                        g.dma_start(out=wbuf[pi % 4][:],
                                    in_=wt[kg * 512:(kg + 1) * 512,
                                           base:base + 256]
                                    .rearrange("(k q) f -> q k f", q=128)
                                    ).then_inc(s_w[pi % 4], 16)

            @blk.tensor
            def _(t):
                for pr in range(npair):
                    kind0 = tiles[2 * pr][0]
                    for kg in range(8):
                        pi = pr * 8 + kg
                        t.wait_ge(s_w[pi % 4], (pi // 4 + 1) * 16)
                        if kg == 0 and pr >= 1:
                            psum_free_wait(t, 2 * (pr - 1))
                            psum_free_wait(t, 2 * (pr - 1) + 1)
                        for k4 in range(4):
                            k = kg * 4 + k4
                            st_, sp_ = (k == 0), (k == 31)
                            for sub in range(2):
                                lhs = wbuf[pi % 4][:, k4 * 256 + sub * 128:
                                                   k4 * 256 + sub * 128 + 128]
                                if kind0 == "gt":
                                    t.matmul(psA[sub][:, 0:512], lhs,
                                             p.ln1[:, k * TH + 3: k * TH + 515],
                                             start=st_, stop=sp_)
                                    mm = t.matmul(psA[sub][:, 512:1024], lhs,
                                                  p.ln1[:, k * TH + 515: k * TH + 1027],
                                                  start=st_, stop=sp_)
                                else:
                                    t.matmul(psA[sub][:, 0:512], lhs,
                                             p.ln1[:, k * TH: k * TH + 512],
                                             start=st_, stop=sp_)
                                    t.matmul(psA[sub][:, 512:1024], lhs,
                                             p.ln1[:, k * TH + 512: k * TH + 1024],
                                             start=st_, stop=sp_)
                                    mm = t.matmul(psA[sub][:, 1024:1027], lhs,
                                                  p.ln1[:, k * TH + 1024: k * TH + 1027],
                                                  start=st_, stop=sp_)
                        mm.then_inc(s_pc, 1)
                    if pr == 7:
                        for c in range(NCH):
                            for gg in range(G):
                                j = c * 8 + gg
                                t.wait_ge(s_si, gg + 1)
                                if j >= 1:
                                    t.wait_ge(s_te, j)
                                t.transpose(
                                    psT[:],
                                    p.bcfm[:, gg * T + c * 128: gg * T + (c + 1) * 128],
                                    p.idbf[:]).then_inc(s_t, 1)
                    if 9 <= pr <= 40:
                        for ti in (2 * (pr - 1), 2 * (pr - 1) + 1):
                            kind, i = tiles[ti]
                            assert kind == "hs"
                            t.wait_ge(s_si, ti + 1)
                            for c in range(NCH):
                                j = 64 + i * 8 + c
                                t.wait_ge(s_te, j)
                                t.transpose(psT[:], hsb[i % 2][:, c * 128:(c + 1) * 128],
                                            p.idbf[:]).then_inc(s_t, 1)
                            for c in range(NCH):
                                t.wait_ge(s_xd, i * 8 + c + 1)
                                if c == 0:
                                    t.wait_ge(s_ce, i)
                                t.matmul(psC[:],
                                         p.btok[:, (c * 8 + i // 8) * 128:
                                                (c * 8 + i // 8 + 1) * 128],
                                         xds[(i * 8 + c) % 2][:],
                                         start=(c == 0), stop=(c == 7)
                                         ).then_inc(s_cy, 1)

            @blk.scalar
            def _(sc):
                for ti, (kind, i) in enumerate(tiles):
                    pr = ti // 2
                    sc.wait_ge(s_pc, (pr + 1) * 8)
                    if kind == "gt":
                        if i >= 2:
                            sc.wait_ge(s_gs[i % 2], (i // 2) * 16)
                        sc.copy(gst[i % 2][:], psA[ti % 2][:, 0:1024]
                                ).then_inc(s_ge, 1)
                        continue
                    if pr >= 1:
                        sc.wait_ge(s_cv, min(2 * pr, 80))
                    sc.copy(raw[ti % 2][:], psA[ti % 2][:, 0:1027]).then_inc(s_raw, 1)
                    sc.wait_ge(s_cv, ti + 1)
                    tci = (64 + i) if kind == "bc" else i
                    dst = (p.bcfm[:, i * T:(i + 1) * T] if kind == "bc"
                           else hsb[i % 2][:])
                    if kind == "hs" and i >= 2:
                        sc.wait_ge(s_t, 64 + (i - 1) * 8)
                    sc.activation(dst, cacc[ti % 2][:], AF.Silu,
                                  bias=p.cbt[:, tci: tci + 1]).then_inc(s_si, 1)

            @blk.vector
            def _(v):
                def conv(ti):
                    kind, i = tiles[ti]
                    v.wait_ge(s_raw, ti + 1)
                    if ti >= 2:
                        v.wait_ge(s_si, ti - 1)
                    tci = (64 + i) if kind == "bc" else i
                    r = raw[ti % 2]
                    ca = cacc[ti % 2]
                    p.vf(v, v.tensor_scalar(ca[:], r[:, 0:T],
                                            p.cwt[:, tci * 4: tci * 4 + 1],
                                            None, ALU.mult))
                    for k in range(1, 3):
                        p.vf(v, v.scalar_tensor_tensor(
                            ca[:], r[:, k:T + k],
                            p.cwt[:, tci * 4 + k: tci * 4 + k + 1],
                            ca[:], ALU.mult, ALU.add))
                    v.scalar_tensor_tensor(
                        ca[:], r[:, 3:T + 3],
                        p.cwt[:, tci * 4 + 3: tci * 4 + 4],
                        ca[:], ALU.mult, ALU.add).then_inc(s_cv, 1)

                for ti in range(16):
                    conv(ti)
                for c in range(NCH):
                    for gg in range(G):
                        j = c * 8 + gg
                        v.wait_ge(s_t, j + 1)
                        v.tensor_copy(p.btok[:, j * 128:(j + 1) * 128],
                                      psT[:]).then_inc(s_te, 1)
                conv(16)
                conv(17)
                for i in range(64):
                    if i + 2 < 64:
                        conv(16 + i + 2)
                    for c in range(NCH):
                        j = 64 + i * 8 + c
                        v.wait_ge(s_t, j + 1)
                        if i >= 2:
                            v.wait_ge(s_sp[i % 2], (i // 2) * 16)
                        v.tensor_tensor(
                            xst[i % 2][:, c * 128:(c + 1) * 128], psT[:],
                            bc2(p.dttok[:, c * 128 + 2 * i: c * 128 + 2 * i + 2], 2, 64),
                            ALU.mult).then_inc(s_te, 1)
                    for c in range(NCH):
                        idx = i * 8 + c
                        if idx >= 2:
                            v.wait_ge(s_cy, idx - 1)
                        v.tensor_tensor(
                            xds[idx % 2][:],
                            xst[i % 2][:, c * 128:(c + 1) * 128],
                            bc2(p.wftok[:, c * 128 + 2 * i: c * 128 + 2 * i + 2], 2, 64),
                            ALU.mult).then_inc(s_xd, 1)
                    v.wait_ge(s_cy, (i + 1) * 8)
                    v.tensor_copy(p.rbuf[:, i * 128:(i + 1) * 128],
                                  psC[:]).then_inc(s_ce, 1)

            @blk.sync
            def _(s):
                for i in range(64):
                    s.wait_ge(s_te, 64 + (i + 1) * 8)
                    s.dma_start(
                        out=p.xch_sp.rearrange("c q f -> q c f")
                        [:, :, i * 128:(i + 1) * 128],
                        in_=xst[i % 2][:].rearrange("q (c l) -> q c l", c=NCH)
                    ).then_inc(s_sp[i % 2], 16)
                for i in range(64):
                    s.wait_ge(s_ge, i + 1)
                    s.dma_start(out=p.gate_sp[i].opt(),
                                in_=gst[i % 2][:]).then_inc(s_gs[i % 2], 16)
                for sl in range(2):
                    s.wait_ge(s_sp[sl], 32 * 16)
                    s.wait_ge(s_gs[sl], 32 * 16)
                s.wait_ge(s_ce, 64)


def phase4(nc, p):
    s_a = p.mksem("p4_a")
    s_cc = p.mksem("p4_cc")
    s_g = p.mksem("p4_g")
    s_ga = p.mksem("p4_ga")
    s_m = p.mksem("p4_m")
    s_t = p.mksem("p4_t")
    s_e = p.mksem("p4_e")
    s_wb = p.mksem("p4_wb")
    with ExitStack() as c4:
        sb = lambda n, s, d: c4.enter_context(nc.sbuf_tensor(n, s, d))
        gbuf = sb("p4_gbuf", [128, INTER], F32)
        am = sb("p4_am", [128, 4], F32)
        sml = sb("p4_sml", [128, 24], F32)
        wmt = sb("p4_wmt", [128, 128], F32)
        pst = c4.enter_context(nc.psum_tensor("p4_pst", [128, 128], F32))

        with nc.Block() as blk:
            @blk.sync
            def _(s):
                s.dma_start(out=p.carry_dram[:, 0:INTER],
                            in_=p.rbuf[:]).then_inc(s_a, 16)
                s.dma_start(out=p.carry_dram[:, INTER:INTER + 1],
                            in_=p.acg[:, T - 1:T],
                            allow_slow_non_contiguous=True).then_inc(s_a, 16)
                s.wait_ge(s_a, 32)

        with nc.Block() as blk:
            @blk.gpsimd
            def _(g):
                g.collective_compute(
                    "AllGather", ALU.bypass,
                    replica_groups=[[0, 1, 2, 3], [4, 5, 6, 7]],
                    ins=[p.carry_dram.opt()], outs=[p.gath_dram.opt()],
                ).then_inc(s_cc)
                g.wait_ge(s_cc, 1)

        with nc.Block() as blk:
            @blk.sync
            def _(s):
                s.dma_start(out=am[:],
                            in_=p.gath_dram[:, INTER:INTER + 1]
                            .rearrange("(q h) o -> h (q o)", h=128),
                            allow_slow_non_contiguous=True).then_inc(s_ga, 16)
                for qp in range(4):
                    if qp >= 1:
                        s.wait_ge(s_m, qp)
                    s.dma_start(out=gbuf[:],
                                in_=p.gath_dram[qp * 128:(qp + 1) * 128, 0:INTER]
                                ).then_inc(s_g, 16)

            @blk.vector
            def _(v):
                v.wait_ge(s_ga, 16)
                p.vf(v, v.memset(sml[:, 0:1], 0.0))
                p.vf(v, v.tensor_copy(sml[:, 1:2], am[:, 0:1]))
                p.vf(v, v.tensor_add(sml[:, 2:3], sml[:, 1:2], am[:, 1:2]))
                p.vf(v, v.tensor_add(sml[:, 3:4], sml[:, 2:3], am[:, 2:3]))
                p.vf(v, v.tensor_add(sml[:, 4:8], sml[:, 0:4], am[:, 0:4]))
                p.vf(v, v.tensor_mul(sml[:, 12:16], p.qsel[:], sml[:, 0:4]))
                p.vf(v, v.reduce_sum(sml[:, 8:9], sml[:, 12:16], axis=AX.X))
                p.vf(v, v.tensor_scalar(sml[:, 12:16], sml[:, 4:8], sml[:, 8:9],
                                        None, ALU.subtract))
                v.tensor_scalar(sml[:, 16:20], sml[:, 12:16], -1.0, 0.0,
                                ALU.mult, ALU.min).then_inc(s_e, 1)
                v.wait_ge(s_e, 2)
                p.vf(v, v.tensor_mul(sml[:, 12:16], sml[:, 20:24], p.maskq[:]))
                v.memset(p.rbuf[:], 0.0)
                for qp in range(4):
                    v.tensor_scalar(wmt[:], p.ones128[:], sml[:, 12 + qp:13 + qp],
                                    None, ALU.mult).then_inc(s_wb, 1)
                    v.wait_ge(s_t, qp + 1)
                    v.wait_ge(s_g, (qp + 1) * 16)
                    p.vf(v, v.tensor_tensor(gbuf[:], gbuf[:], bc2(pst[:], 128, 64),
                                            ALU.mult))
                    v.tensor_add(p.rbuf[:], p.rbuf[:], gbuf[:]).then_inc(s_m, 1)

            @blk.scalar
            def _(sc):
                sc.wait_ge(s_e, 1)
                sc.activation(sml[:, 20:24], sml[:, 16:20], AF.Exp).then_inc(s_e, 1)

            @blk.tensor
            def _(t):
                for qp in range(4):
                    t.wait_ge(s_wb, qp + 1)
                    if qp >= 1:
                        t.wait_ge(s_m, qp)
                    t.transpose(pst[:], wmt[:], p.idf32[:]).then_inc(s_t, 1)


def phase5(nc, p):
    with ExitStack() as c5:
        sb = lambda n, s, d: c5.enter_context(nc.sbuf_tensor(n, s, d))
        xch = [sb(f"p5x_{i}", [128, INTER], BF16) for i in range(2)]
        pall = [sb(f"p5pa_{i}", [128, 2048], F32) for i in range(2)]
        lt = [sb(f"p5lt_{i}", [128, 128], F32) for i in range(16)]
        gt = [sb(f"p5gt_{i}", [128, 128], BF16) for i in range(16)]
        gts = [sb(f"p5gts_{i}", [128, 128], F32) for i in range(2)]
        tmp1 = sb("p5tmp1", [128, 512], F32)
        tmp2 = [sb(f"p5tmp2_{i}", [128, 512], F32) for i in range(2)]
        hsd = [sb(f"p5hsd_{i}", [128, 512], F32) for i in range(2)]
        ytk = [sb(f"p5ytk_{i}", [128, 1024], BF16) for i in range(2)]
        yst = [sb(f"p5yst_{i}", [128, 1024], BF16) for i in range(2)]
        xds = [sb(f"p5xd_{i}", [128, 1024], BF16) for i in range(2)]
        afm = sb("p5afm", [128, 128], F32)
        acum = sb("p5acum", [128, 128], F32)
        nacum = sb("p5nacum", [128, 128], F32)
        eex = sb("p5eex", [128, 128], F32)
        dfm = sb("p5dfm", [128, 128], F32)
        dtok = sb("p5dtok", [128, 128], F32)
        ezb = sb("p5ezb", [128, 128], F32)
        ezm = sb("p5ezm", [128, 128], F32)
        dtrec = sb("p5dtrec", [128, 128], F32)
        ddtf = sb("p5ddtf", [128, 128], F32)
        ddtm = sb("p5ddtm", [128, 128], F32)
        smlc = sb("p5smlc", [128, 4], F32)
        psb = [c5.enter_context(nc.psum_tensor(f"p5psb_{i}", [128, 128], F32))
               for i in range(2)]
        psa = psb[0]
        psbt = c5.enter_context(nc.psum_tensor("p5psbt", [128, 128], BF16))
        psg = c5.enter_context(nc.psum_tensor("p5psg", [128, 128], F32))
        psyd = [c5.enter_context(nc.psum_tensor(f"p5psyd_{i}", [128, 512], F32))
                for i in range(2)]
        psyo = [c5.enter_context(nc.psum_tensor(f"p5psyo_{i}", [128, 512], F32))
                for i in range(2)]
        s_xl = [p.mksem("p5_xla"), p.mksem("p5_xlb")]
        s_pv = p.mksem("p5_pv")
        s_pa5 = p.mksem("p5_pa5")
        s_pt = p.mksem("p5_pt")
        s_pp = p.mksem("p5_pp")
        s_bt = p.mksem("p5_bt")
        s_lt = p.mksem("p5_lt")
        s_m1 = p.mksem("p5_m1")
        s_ge = p.mksem("p5_ge")
        s_gt = p.mksem("p5_gt")
        s_ymm = p.mksem("p5_ymm")
        s_p1 = p.mksem("p5_p1")
        s_p2 = p.mksem("p5_p2")
        s_hd = p.mksem("p5_hd")
        s_ye = p.mksem("p5_ye")
        s_yt = p.mksem("p5_yt")
        s_ys = p.mksem("p5_ys")
        s_sp = [p.mksem("p5_spa"), p.mksem("p5_spb")]
        s_rb = p.mksem("p5_rb")
        s_rz = p.mksem("p5_rz")
        s_xd = p.mksem("p5_xd")
        s_sm = p.mksem("p5_sm")
        s_se = p.mksem("p5_se")

        NPV = 10

        with nc.Block() as blk:
            @blk.sync
            def _(s):
                for c in range(NCH):
                    if c >= 2:
                        s.wait_ge(s_se, (c - 1) * 8)
                    s.dma_start(out=xch[c % 2][:],
                                in_=p.xch_sp[c].opt()).then_inc(s_xl[c % 2], 16)
                    for gg in range(G):
                        cg = c * 8 + gg
                        s.wait_ge(s_ys, (cg + 1) * 8)
                        s.dma_start(
                            out=p.yfm_sp.rearrange("j q t -> q j t")
                            [:, gg * 8:(gg + 1) * 8, c * 128:(c + 1) * 128],
                            in_=yst[cg % 2][:].rearrange("q (j l) -> q j l", j=8)
                        ).then_inc(s_sp[cg % 2], 16)
                for sl in range(2):
                    s.wait_ge(s_sp[sl], 32 * 16)

            @blk.vector
            def _(v):
                for c in range(NCH):
                    cs, ce = c * 128, (c + 1) * 128
                    b = c * NPV
                    if c > 0:
                        v.wait_ge(s_se, c * 8)
                        v.wait_ge(s_ymm, c * 128)
                        v.wait_ge(s_xd, c * 8)
                        v.wait_ge(s_ye, c * 16)
                        v.wait_ge(s_pt, c * 4)
                    v.tensor_copy(p.rb[:], p.rbuf[:]).then_inc(s_rb, 1)
                    if c == 0:
                        p.vf(v, v.memset(smlc[:, 0:1], 0.0))
                    else:
                        p.vf(v, v.tensor_copy(smlc[:, 0:1], p.acg[:, cs - 1:cs]))
                    v.tensor_scalar(afm[:], p.acg[:, cs:ce], smlc[:, 0:1],
                                    None, ALU.subtract).then_inc(s_pv, 1)
                    v.wait_ge(s_pt, c * 4 + 1)
                    p.vf(v, v.tensor_copy(acum[:], psa[:]), s_pv, 1)
                    v.tensor_scalar(nacum[:], acum[:], -1.0, None,
                                    ALU.mult).then_inc(s_pv, 1)
                    v.tensor_scalar(smlc[:, 1:2], p.acg[:, ce - 1:ce], smlc[:, 0:1],
                                    None, ALU.subtract).then_inc(s_pv, 1)
                    v.wait_ge(s_pa5, c * 3 + 2)
                    v.tensor_scalar(ezb[:], p.ones128[:], smlc[:, 2:3],
                                    None, ALU.mult).then_inc(s_pv, 1)
                    v.wait_ge(s_pt, c * 4 + 2)
                    v.tensor_copy(dtok[:], psa[:]).then_inc(s_pv, 1)
                    v.wait_ge(s_pt, c * 4 + 3)
                    v.tensor_copy(ezm[:], psa[:]).then_inc(s_pv, 1)
                    p.vf(v, v.reciprocal(dtrec[:],
                                         p.dtf[:, 3 + cs: 3 + ce]), s_pv, 1)
                    v.tensor_scalar(ddtf[:], dtrec[:], p.dvec[:], None,
                                    ALU.mult).then_inc(s_pv, 1)
                    v.wait_ge(s_pt, c * 4 + 4)
                    v.tensor_copy(ddtm[:], psa[:]).then_inc(s_pv, 1)
                    def yst_evac(cg2):
                        for j in range(8):
                            v.wait_ge(s_yt, cg2 * 8 + j + 1)
                            if cg2 >= 2:
                                v.wait_ge(s_sp[cg2 % 2], ((cg2 - 2) // 2 + 1) * 16)
                            v.tensor_copy(
                                yst[cg2 % 2][:, j * 128:(j + 1) * 128],
                                psbt[:]).then_inc(s_ys, 1)

                    for gg in range(G):
                        cg = c * 8 + gg
                        if gg >= 1:
                            yst_evac(cg - 1)
                        for half in range(2):
                            hg = cg * 2 + half
                            v.wait_ge(s_ymm, (hg + 1) * 8)
                            if hg >= 2:
                                v.wait_ge(s_ye, hg - 1)
                            for hh8 in range(8):
                                h = gg * 16 + half * 8 + hh8
                                inst = v.tensor_scalar(
                                    tmp1[:, hh8 * 64:(hh8 + 1) * 64],
                                    psyo[hg % 2][:, hh8 * 64:(hh8 + 1) * 64],
                                    eex[:, h:h + 1], None, ALU.mult)
                            p.vf(v, inst)
                            p.vf(v, v.tensor_add(tmp2[hg % 2][:], tmp1[:],
                                                 psyd[hg % 2][:]), s_p2, 1)
                    yst_evac(c * 8 + 7)
                    v.wait_ge(s_rb, c + 1)
                    p.vf(v, v.tensor_tensor(p.rbuf[:], p.rbuf[:],
                                            bc2(ezm[:], 128, 64),
                                            ALU.mult), s_rz, 1)
                    for gg in range(G):
                        cg = c * 8 + gg
                        v.wait_ge(s_sm, (cg + 1) * 16)
                        v.tensor_add(p.rbuf[:, gg * 1024: gg * 1024 + 512],
                                     p.rbuf[:, gg * 1024: gg * 1024 + 512],
                                     psyd[gg % 2][:])
                        v.tensor_add(p.rbuf[:, gg * 1024 + 512:(gg + 1) * 1024],
                                     p.rbuf[:, gg * 1024 + 512:(gg + 1) * 1024],
                                     psyo[gg % 2][:]).then_inc(s_se, 1)

            @blk.scalar
            def _(sc):
                for c in range(NCH):
                    cs, ce = c * 128, (c + 1) * 128
                    b = c * NPV
                    if c > 0:
                        sc.wait_ge(s_pt, (c - 1) * 4 + 2)
                    sc.activation(dfm[:], p.acg[:, cs:ce], AF.Exp, scale=-1.0,
                                  bias=p.acg[:, ce - 1:ce]).then_inc(s_pa5, 1)
                    sc.wait_ge(s_pv, b + 4)
                    sc.activation(smlc[:, 2:3], smlc[:, 1:2],
                                  AF.Exp).then_inc(s_pa5, 1)
                    sc.wait_ge(s_pv, b + 2)
                    if c > 0:
                        sc.wait_ge(s_p2, c * 16)
                    sc.activation(eex[:], acum[:], AF.Exp).then_inc(s_pa5, 1)
                    for gg in range(G):
                        cg = c * 8 + gg
                        sc.wait_ge(s_m1, cg + 1)
                        if cg >= 2:
                            sc.wait_ge(s_gt, (cg - 1) * 16)
                        sc.copy(gts[cg % 2][:], psg[:]).then_inc(s_ge, 1)
                        for hh in range(16):
                            hidx = cg * 16 + hh
                            sc.wait_ge(s_bt, hidx + 1)
                            if cg >= 1:
                                sc.wait_ge(s_gt, (cg - 1) * 16 + hh + 1)
                            sc.activation(lt[hh][:], psb[hh % 2][:], AF.Exp,
                                          bias=nacum[:, gg * 16 + hh: gg * 16 + hh + 1]
                                          ).then_inc(s_lt, 1)

            @blk.gpsimd
            def _(g):
                for c in range(NCH):
                    b = c * NPV
                    for gg in range(G):
                        cg = c * 8 + gg
                        g.wait_ge(s_pv, b + 2)
                        if cg >= 2:
                            g.wait_ge(s_bt, (cg - 1) * 16)
                        g.tensor_tensor(
                            pall[cg % 2][:],
                            bc2(acum[:, gg * 16:(gg + 1) * 16], 16, 128),
                            p.mneg[:].unsqueeze(1).broadcast_to([128, 16, 128]),
                            ALU.add).then_inc(s_pp, 1)
                        for hh in range(16):
                            hidx = cg * 16 + hh
                            g.wait_ge(s_lt, hidx + 1)
                            g.wait_ge(s_ge, cg + 1)
                            if cg >= 1:
                                g.wait_ge(s_ymm, (cg - 1) * 16 + hh + 1)
                            g.tensor_tensor(gt[hh][:], gts[cg % 2][:],
                                            lt[hh][:], ALU.mult).then_inc(s_gt, 1)
                        for half in range(2):
                            hgp = cg * 2 + half
                            g.wait_ge(s_pv, b + 10)
                            g.wait_ge(s_xl[c % 2], (c // 2 + 1) * 16)
                            if hgp >= 2:
                                g.wait_ge(s_ye, hgp - 1)
                            sp = gg * 1024 + half * 512
                            h0 = gg * 16 + half * 8
                            p.pf(g, g.tensor_tensor(hsd[hgp % 2][:],
                                                    xch[c % 2][:, sp:sp + 512],
                                                    bc2(ddtm[:, h0:h0 + 8], 8, 64),
                                                    ALU.mult), s_hd, 1)
                            g.wait_ge(s_p2, hgp + 1)
                            if cg >= 2 and half == 0:
                                g.wait_ge(s_yt, (cg - 1) * 8)
                            g.tensor_tensor(ytk[cg % 2][:, half * 512:(half + 1) * 512],
                                            tmp2[hgp % 2][:], hsd[hgp % 2][:],
                                            ALU.add).then_inc(s_ye, 1)
                    for gg in range(G):
                        cg = c * 8 + gg
                        g.wait_ge(s_pv, b + 6)
                        if cg >= 2:
                            g.wait_ge(s_sm, (cg - 1) * 16)
                        g.tensor_tensor(xds[gg % 2][:],
                                        xch[c % 2][:, gg * 1024:(gg + 1) * 1024],
                                        bc2(dtok[:, gg * 16:(gg + 1) * 16], 16, 64),
                                        ALU.mult).then_inc(s_xd, 1)

            @blk.tensor
            def _(t):
                for c in range(NCH):
                    cs, ce = c * 128, (c + 1) * 128
                    b = c * NPV
                    t.wait_ge(s_pv, b + 1)
                    if c >= 1:
                        t.wait_ge(s_lt, c * 128)
                    t.transpose(psa[:], afm[:], p.idf32[:]).then_inc(s_pt, 1)
                    t.wait_ge(s_pa5, c * 3 + 1)
                    t.wait_ge(s_pv, b + 2)
                    t.transpose(psa[:], dfm[:], p.idf32[:]).then_inc(s_pt, 1)
                    t.wait_ge(s_pv, b + 6)
                    t.transpose(psa[:], ezb[:], p.idf32[:]).then_inc(s_pt, 1)
                    t.wait_ge(s_pv, b + 9)
                    t.transpose(psa[:], ddtf[:], p.idf32[:]).then_inc(s_pt, 1)
                    for gg in range(G):
                        cg = c * 8 + gg
                        if cg >= 1:
                            t.wait_ge(s_ge, cg)
                        t.matmul(psg[:], p.bcfm[:, gg * T + cs: gg * T + ce],
                                 p.bcfm[:, (8 + gg) * T + cs:(8 + gg) * T + ce],
                                 start=True, stop=True).then_inc(s_m1, 1)
                        for hh in range(16):
                            hidx = cg * 16 + hh
                            t.wait_ge(s_pp, cg + 1)
                            if gg == 0 and hh == 0:
                                t.wait_ge(s_pv, b + 10)
                            if hidx >= 2:
                                t.wait_ge(s_lt, hidx - 1)
                            t.transpose(psb[hh % 2][:],
                                        pall[cg % 2][:, hh * 128:(hh + 1) * 128],
                                        p.idf32[:]).then_inc(s_bt, 1)
                        for hh in range(16):
                            hidx = cg * 16 + hh
                            hg = cg * 2 + hh // 8
                            h = gg * 16 + hh
                            t.wait_ge(s_gt, hidx + 1)
                            t.wait_ge(s_xl[c % 2], (c // 2 + 1) * 16)
                            if hh % 8 == 0 and hg >= 2:
                                t.wait_ge(s_p2, hg - 1)
                            sl = slice((hh % 8) * 64, (hh % 8 + 1) * 64)
                            t.matmul(psyd[hg % 2][:, sl], gt[hh][:],
                                     xch[c % 2][:, h * 64:(h + 1) * 64],
                                     start=True, stop=True)
                            t.wait_ge(s_rb, c + 1)
                            t.matmul(psyo[hg % 2][:, sl],
                                     p.bcfm[:, (8 + gg) * T + cs:(8 + gg) * T + ce],
                                     p.rb[:, h * 64:(h + 1) * 64],
                                     start=True, stop=True).then_inc(s_ymm, 1)
                        for j in range(8):
                            cgj = cg * 8 + j
                            t.wait_ge(s_ye, cg * 2 + 1 + (1 if j >= 4 else 0))
                            if cgj >= 1:
                                t.wait_ge(s_ys, cgj)
                            t.transpose(psbt[:],
                                        ytk[cg % 2][:, j * 128:(j + 1) * 128],
                                        p.idbf[:]).then_inc(s_yt, 1)
                    for gg in range(G):
                        cg = c * 8 + gg
                        t.wait_ge(s_xd, cg + 1)
                        t.wait_ge(s_rz, c + 1)
                        t.wait_ge(s_p2, c * 16 + 16)
                        if cg >= 2:
                            t.wait_ge(s_se, cg - 1)
                        for hh in range(16):
                            dstp = psyd[gg % 2] if hh < 8 else psyo[gg % 2]
                            sl = slice((hh % 8) * 64, (hh % 8 + 1) * 64)
                            t.matmul(dstp[:, sl],
                                     p.btok[:, (c * 8 + gg) * 128:
                                            (c * 8 + gg + 1) * 128],
                                     xds[gg % 2][:, hh * 64:(hh + 1) * 64],
                                     start=True, stop=True).then_inc(s_sm, 1)

    if "tap_xch" in p.tap or "tap_y" in p.tap or "tap_R" in p.tap:
        sem = p.mksem("p5_tap")
        with nc.Block() as blk:
            @blk.sync
            def _(s):
                n = 0
                if "tap_xch" in p.tap:
                    s.dma_start(out=p.tap["tap_xch"], in_=p.xch_sp).then_inc(sem, 16)
                    n += 16
                if "tap_y" in p.tap:
                    s.dma_start(out=p.tap["tap_y"], in_=p.yfm_sp).then_inc(sem, 16)
                    n += 16
                if "tap_R" in p.tap:
                    s.dma_start(out=p.tap["tap_R"], in_=p.rbuf[:]).then_inc(sem, 16)
                    n += 16
                s.wait_ge(sem, n)


def phase6(nc, p):
    with ExitStack() as c6:
        sb = lambda n, s, d: c6.enter_context(nc.sbuf_tensor(n, s, d))
        gbx = [sb(f"p6gb_{i}", [128, T], BF16) for i in range(4)]
        sgb = [sb(f"p6sg_{i}", [128, T], BF16) for i in range(2)]
        yfm = [sb(f"p6yf_{i}", [128, T], BF16) for i in range(4)]
        sqb = [sb(f"p6sq_{i}", [128, T], BF16) for i in range(2)]
        rsm = sb("p6rs", [1, 2 * T], F32)
        psV = c6.enter_context(nc.psum_tensor("p6psV", [1, T], F32))
        psR = c6.enter_context(nc.psum_tensor("p6psR", [128, T], F32))
        s_gl = [p.mksem(f"p6_gl{i}") for i in range(4)]
        s_yl = [p.mksem(f"p6_yl{i}") for i in range(4)]
        s_sg = p.mksem("p6_sg")
        s_g2 = p.mksem("p6_g2")
        s_sq = p.mksem("p6_sq")
        s_vm = p.mksem("p6_vm")
        s_vr = p.mksem("p6_vr")
        s_sd = p.mksem("p6_sd")
        s_rc = p.mksem("p6_rc")
        s_rm = p.mksem("p6_rm")
        s_sc6 = p.mksem("p6_sc")

        with nc.Block() as blk:
            @blk.sync
            def _(s):
                for ft in range(64):
                    if ft >= 4:
                        s.wait_ge(s_sg, ft - 3)
                    s.dma_start(out=gbx[ft % 4][:],
                                in_=p.gate_sp[ft].opt()).then_inc(s_gl[ft % 4], 16)
                    if ft >= 4:
                        s.wait_ge(s_g2, ft - 3)
                    s.dma_start(out=yfm[ft % 4][:],
                                in_=p.yfm_sp[ft].opt()).then_inc(s_yl[ft % 4], 16)

            @blk.scalar
            def _(sc):
                for ft in range(64):
                    sc.wait_ge(s_gl[ft % 4], (ft // 4 + 1) * 16)
                    if ft >= 2:
                        sc.wait_ge(s_g2, ft - 1)
                    sc.activation(sgb[ft % 2][:], gbx[ft % 4][:],
                                  AF.Silu).then_inc(s_sg, 1)
                    sc.wait_ge(s_g2, ft + 1)
                    if ft >= 2:
                        sc.wait_ge(s_vm, ft - 1)
                    sc.activation(sqb[ft % 2][:], p.g3buf[:, ft * T:(ft + 1) * T],
                                  AF.Square).then_inc(s_sq, 1)
                    if ft % 8 == 7:
                        grp = ft // 8
                        sc.wait_ge(s_vr, grp + 1)
                        sc.activation(rsm[:, 0:T], rsm[:, 0:T], AF.Sqrt,
                                      bias=EPS_RMS).then_inc(s_sd, 1)

            @blk.tensor
            def _(t):
                for ft in range(64):
                    st_, sp_ = (ft % 8 == 0), (ft % 8 == 7)
                    t.wait_ge(s_sq, ft + 1)
                    if st_:
                        t.wait_ge(s_vr, ft // 8)
                    t.matmul(psV[:, 0:512], p.onescbf[:], sqb[ft % 2][:, 0:512],
                             start=st_, stop=sp_)
                    t.matmul(psV[:, 512:1024], p.onescbf[:], sqb[ft % 2][:, 512:1024],
                             start=st_, stop=sp_).then_inc(s_vm, 1)
                    if sp_:
                        grp = ft // 8
                        t.wait_ge(s_rc, grp + 1)
                        if grp >= 1:
                            t.wait_ge(s_sc6, grp * 8)
                        t.matmul(psR[:, 0:512], p.ones1[:], rsm[:, T: T + 512],
                                 start=True, stop=True)
                        t.matmul(psR[:, 512:1024], p.ones1[:], rsm[:, T + 512: 2 * T],
                                 start=True, stop=True).then_inc(s_rm, 1)

            @blk.vector
            def _(v):
                for ft in range(64):
                    v.wait_ge(s_sg, ft + 1)
                    v.wait_ge(s_yl[ft % 4], (ft // 4 + 1) * 16)
                    v.tensor_tensor(p.g3buf[:, ft * T:(ft + 1) * T], yfm[ft % 4][:],
                                    sgb[ft % 2][:], ALU.mult).then_inc(s_g2, 1)
                    if ft % 8 == 7:
                        grp = ft // 8
                        v.wait_ge(s_vm, (grp + 1) * 8)
                        v.tensor_scalar(rsm[:, 0:T], psV[:], 1.0 / 1024.0, None,
                                        ALU.mult).then_inc(s_vr, 1)
                        v.wait_ge(s_sd, grp + 1)
                        v.reciprocal(rsm[:, T:2 * T], rsm[:, 0:T]).then_inc(s_rc, 1)
                        v.wait_ge(s_rm, grp + 1)
                        for ft2 in range(grp * 8, grp * 8 + 8):
                            v.scalar_tensor_tensor(
                                p.g3buf[:, ft2 * T:(ft2 + 1) * T],
                                p.g3buf[:, ft2 * T:(ft2 + 1) * T],
                                p.nwt[:, ft2:ft2 + 1], psR[:],
                                ALU.mult, ALU.mult).then_inc(s_sc6, 1)


def phase7(nc, p):
    io = p.io
    with ExitStack() as c7:
        sb = lambda n, s, d: c7.enter_context(nc.sbuf_tensor(n, s, d))
        wob = [sb(f"p7w_{i}", [128, 4 * 512], BF16) for i in range(4)]
        x0s = [sb(f"p7x0_{i}", [128, 512], F32) for i in range(4)]
        x1s = [sb(f"p7x1_{i}", [128, 512], F32) for i in range(2)]
        pso = [c7.enter_context(nc.psum_tensor(f"p7ps_{i}", [128, 512], F32))
               for i in range(8)]
        s_w = [p.mksem(f"p7_w{i}") for i in range(4)]
        s_pc = p.mksem("p7_pc")
        s_mm = p.mksem("p7_mm")
        s_x0 = [p.mksem(f"p7_x0{i}") for i in range(4)]
        s_ev = p.mksem("p7_ev")
        s_st = [p.mksem("p7_sta"), p.mksem("p7_stb")]

        with nc.Block() as blk:
            @blk.gpsimd
            def _(g):
                for o in range(8):
                    for ig in range(16):
                        pi = o * 16 + ig
                        if pi >= 4:
                            g.wait_ge(s_pc, pi - 3)
                        g.dma_start(out=wob[pi % 4][:],
                                    in_=io["woT"][ig * 512:(ig + 1) * 512,
                                                  o * 512:(o + 1) * 512]
                                    .rearrange("(k q) f -> q k f", q=128)
                                    ).then_inc(s_w[pi % 4], 16)

            @blk.tensor
            def _(t):
                for o in range(8):
                    for ig in range(16):
                        pi = o * 16 + ig
                        t.wait_ge(s_w[pi % 4], (pi // 4 + 1) * 16)
                        if ig == 0 and o >= 1:
                            t.wait_ge(s_ev, o * 8)
                        for i4 in range(4):
                            i = ig * 4 + i4
                            st_, sp_ = (i == 0), (i == 63)
                            for tt in range(8):
                                mm = t.matmul(pso[tt][:],
                                              p.g3buf[:, i * T + tt * 128:
                                                      i * T + (tt + 1) * 128],
                                              wob[pi % 4][:, i4 * 512:(i4 + 1) * 512],
                                              start=st_, stop=sp_)
                        mm.then_inc(s_pc, 1)

            @blk.sync
            def _(s):
                for o in range(8):
                    for tt in range(8):
                        idx = o * 8 + tt
                        if idx >= 4:
                            s.wait_ge(s_ev, idx - 3)
                        s.dma_start(out=x0s[idx % 4][:],
                                    in_=io["x_sl"][3 + tt * 128: 3 + (tt + 1) * 128,
                                                   o * 512:(o + 1) * 512]
                                    ).then_inc(s_x0[idx % 4], 16)
                        s.wait_ge(s_ev, idx + 1)
                        s.dma_start(out=p.x1_sp[tt * 128:(tt + 1) * 128,
                                                o * 512:(o + 1) * 512],
                                    in_=x1s[tt % 2][:]).then_inc(s_st[idx % 2], 16)
                s.wait_ge(s_st[0], 32 * 16); s.wait_ge(s_st[1], 32 * 16)

            @blk.vector
            def _(v):
                for o in range(8):
                    v.wait_ge(s_pc, (o + 1) * 16)
                    for tt in range(8):
                        idx = o * 8 + tt
                        v.wait_ge(s_x0[idx % 4], (idx // 4 + 1) * 16)
                        if idx >= 2:
                            v.wait_ge(s_st[idx % 2], (idx // 2) * 16)
                        v.tensor_add(x1s[tt % 2][:], x0s[idx % 4][:],
                                     pso[tt][:]).then_inc(s_ev, 1)

    if "tap_x1" in p.tap:
        sem = p.mksem("p7_tap")
        with nc.Block() as blk:
            @blk.sync
            def _(s):
                s.dma_start(out=p.tap["tap_x1"], in_=p.x1_sp).then_inc(sem, 16)
                s.wait_ge(sem, 16)


def phase8(nc, p):
    io = p.io
    scope8 = ExitStack()
    sb8 = lambda n, s, d: scope8.enter_context(nc.sbuf_tensor(n, s, d))
    ln2 = sb8("ln2", [128, NKT * T], BF16)
    biasm = sb8("biasm", [128, D], F32)
    browb = sb8("browb", [1, D], F32)
    zsb = sb8("zsb", [32, T], F32)
    zbf = sb8("zbf", [32, T], BF16)
    hsl = sb8("hsl", [8, 512], F32)
    dsl = sb8("dsl", [8, 512], F32)

    with ExitStack() as c8:
        sb = lambda n, s, d: c8.enter_context(nc.sbuf_tensor(n, s, d))
        x1t = [sb(f"p8x1_{i}", [128, D], F32) for i in range(2)]
        sqs = sb("p8sq", [128, D], F32)
        xn = [sb(f"p8xn_{i}", [128, D], BF16) for i in range(2)]
        st = sb("p8st", [128, 8], F32)
        tp = [c8.enter_context(nc.psum_tensor(f"p8tp_{i}", [128, 128], BF16))
              for i in range(2)]
        s_ld = [p.mksem("p8_lda"), p.mksem("p8_ldb")]
        s_sq = p.mksem("p8_sq")
        s_var = p.mksem("p8_var")
        s_sd = p.mksem("p8_sd")
        s_xn = p.mksem("p8_xn")
        s_t = p.mksem("p8_t")
        s_te = p.mksem("p8_te")

        with nc.Block() as blk:
            @blk.sync
            def _(s):
                for ti in range(8):
                    if ti >= 2:
                        s.wait_ge(s_xn, ti - 1)
                    s.dma_start(out=x1t[ti % 2][:],
                                in_=p.x1_sp[ti * 128:(ti + 1) * 128, :]
                                ).then_inc(s_ld[ti % 2], 16)

            @blk.scalar
            def _(sc):
                for ti in range(8):
                    sc.wait_ge(s_ld[ti % 2], (ti // 2 + 1) * 16)
                    sc.wait_ge(s_var, ti)
                    sc.activation(sqs[:], x1t[ti % 2][:], AF.Square,
                                  accum_out=st[:, 1:2]).then_inc(s_sq, 1)
                    sc.wait_ge(s_var, ti + 1)
                    sc.activation(st[:, 6:7], st[:, 5:6], AF.Sqrt,
                                  bias=EPS_LN).then_inc(s_sd, 1)

            @blk.vector
            def _(v):
                for ti in range(8):
                    v.wait_ge(s_ld[ti % 2], (ti // 2 + 1) * 16)
                    p.vf(v, v.reduce_sum(st[:, 0:1], x1t[ti % 2][:], axis=AX.X))
                    v.wait_ge(s_sq, ti + 1)
                    v.tensor_scalar(st[:, 2:3], st[:, 0:1], 1.0 / D, None, ALU.mult)
                    p.vf(v, v.tensor_scalar(st[:, 3:4], st[:, 1:2], 1.0 / D,
                                            None, ALU.mult))
                    p.vf(v, v.tensor_mul(st[:, 4:5], st[:, 2:3], st[:, 2:3]))
                    v.tensor_sub(st[:, 5:6], st[:, 3:4], st[:, 4:5]).then_inc(s_var, 1)
                    v.wait_ge(s_sd, ti + 1)
                    p.vf(v, v.reciprocal(st[:, 7:8], st[:, 6:7]))
                    v.tensor_scalar(xn[ti % 2][:], x1t[ti % 2][:], st[:, 2:3],
                                    st[:, 7:8], ALU.subtract,
                                    ALU.mult).then_inc(s_xn, 1)
                    for k in range(NKT):
                        v.wait_ge(s_t, ti * NKT + k + 1)
                        v.tensor_copy(ln2[:, k * T + ti * 128: k * T + (ti + 1) * 128],
                                      tp[k % 2][:]).then_inc(s_te, 1)

            @blk.tensor
            def _(t):
                for ti in range(8):
                    t.wait_ge(s_xn, ti + 1)
                    for k in range(NKT):
                        idx = ti * NKT + k
                        if idx >= 2:
                            t.wait_ge(s_te, idx - 1)
                        t.transpose(tp[k % 2][:], xn[ti % 2][:, k * 128:(k + 1) * 128],
                                    p.idbf[:]).then_inc(s_t, 1)

    with ExitStack() as c8:
        sb = lambda n, s, d: c8.enter_context(nc.sbuf_tensor(n, s, d))
        abuf = sb("p8ab", [128, 32 * 32], BF16)
        wf8 = sb("p8wf8", [128, 32 * 8], BF16)
        limTb = sb("p8limT", [32, 8], BF16)
        mcol = sb("p8mcol", [8, 2], F32)
        msc = sb("p8msc", [2, 8], F32)
        psb = c8.enter_context(nc.psum_tensor("p8psb", [128, 512], F32))
        psz = c8.enter_context(nc.psum_tensor("p8psz", [32, T], F32))
        pshs = c8.enter_context(nc.psum_tensor("p8pshs", [8, 512], F32))
        psds = c8.enter_context(nc.psum_tensor("p8psds", [8, 512], F32))
        psmx = c8.enter_context(nc.psum_tensor("p8psmx", [2, 8], F32))
        s_ld = p.mksem("p8b_ld")
        s_bm = p.mksem("p8b_bm")
        s_be = p.mksem("p8b_be")
        s_z = p.mksem("p8b_z")
        s_ze = p.mksem("p8b_ze")
        s_h = p.mksem("p8b_h")
        s_he = p.mksem("p8b_he")
        s_ds = p.mksem("p8b_ds")
        s_mx = p.mksem("p8b_mx")
        s_st = p.mksem("p8b_st")

        with nc.Block() as blk:
            @blk.gpsimd
            def _(g):
                g.dma_start(out=browb[:], in_=io["brow"]).then_inc(s_ld, 16)
                g.dma_start(out=abuf[:],
                            in_=io["amat"].rearrange("(k q) r -> q k r", q=128)
                            ).then_inc(s_ld, 16)
                g.dma_start(out=wf8[:],
                            in_=io["wfT"][:, 0:8].rearrange("(k q) e -> q k e", q=128)
                            ).then_inc(s_ld, 16)
                g.dma_start(out=limTb[:], in_=io["limT"]).then_inc(s_ld, 16)

            @blk.tensor
            def _(t):
                t.wait_ge(s_ld, 64)
                for o in range(8):
                    if o >= 1:
                        t.wait_ge(s_be, o)
                    t.matmul(psb[:], p.ones1[:], browb[:, o * 512:(o + 1) * 512],
                             start=True, stop=True).then_inc(s_bm, 1)
                t.wait_ge(s_ld, 64)
                for k in range(NKT):
                    st_, sp_ = (k == 0), (k == NKT - 1)
                    t.matmul(psz[:, 0:512], abuf[:, k * 32:(k + 1) * 32],
                             ln2[:, k * T: k * T + 512], start=st_, stop=sp_)
                    t.matmul(psz[:, 512:1024], abuf[:, k * 32:(k + 1) * 32],
                             ln2[:, k * T + 512:(k + 1) * T],
                             start=st_, stop=sp_).then_inc(s_z, 1)
                t.wait_ge(s_ld, 64)
                for k in range(NKT):
                    st_, sp_ = (k == 0), (k == NKT - 1)
                    odd = ln2[:, k * T:(k + 1) * T].rearrange(
                        "q (a two) -> q two a", two=2)[:, 1, :]
                    t.matmul(pshs[:], wf8[:, k * 8:(k + 1) * 8], odd,
                             start=st_, stop=sp_).then_inc(s_h, 1)
                t.wait_ge(s_ld, 64)
                t.wait_ge(s_ze, 2)
                zodd = zbf[:].rearrange("q (a two) -> q two a", two=2)[:, 1, :]
                t.matmul(psds[:], limTb[:], zodd, start=True,
                         stop=True).then_inc(s_ds, 1)
                t.wait_ge(s_mx, 1)
                t.transpose(psmx[:], mcol[:], p.idf32[0:8, 0:8]).then_inc(s_mx, 1)

            @blk.vector
            def _(v):
                for o in range(8):
                    v.wait_ge(s_bm, o + 1)
                    v.tensor_copy(biasm[:, o * 512:(o + 1) * 512],
                                  psb[:]).then_inc(s_be, 1)
                v.wait_ge(s_z, NKT)
                v.tensor_copy(zsb[:], psz[:]).then_inc(s_ze, 1)
                v.tensor_copy(zbf[:], psz[:]).then_inc(s_ze, 1)
                v.wait_ge(s_h, NKT)
                p.vf(v, v.tensor_scalar(hsl[:], pshs[:], p.b8[:], None,
                                        ALU.add), s_he, 1)
                v.wait_ge(s_ds, 1)
                p.vf(v, v.tensor_copy(dsl[:], psds[:]))
                v.tensor_reduce(mcol[:, 0:1], hsl[:], AX.X, ALU.max,
                                apply_absolute_value=True)
                p.vf(v, v.tensor_reduce(mcol[:, 1:2], dsl[:], AX.X, ALU.max,
                                        apply_absolute_value=True), s_mx, 1)
                v.wait_ge(s_mx, 2)
                v.tensor_reduce(msc[:, 0:1], psmx[:], AX.X, ALU.max).then_inc(s_mx, 1)

            @blk.sync
            def _(s):
                s.wait_ge(s_mx, 3)
                s.dma_start(out=p.msc_dram, in_=msc[:, 0:1]).then_inc(s_st, 16)
                s.wait_ge(s_st, 16)

    with ExitStack() as c8:
        sb = lambda n, s, d: c8.enter_context(nc.sbuf_tensor(n, s, d))
        wfb = [sb(f"p8wf_{i}", [128, 4 * 512], BF16) for i in range(4)]
        x1c = [sb(f"p8x1c_{i}", [128, 512], F32) for i in range(4)]
        x2c = [sb(f"p8x2c_{i}", [128, 512], F32) for i in range(2)]
        tmpc = sb("p8tmpc", [128, 512], F32)
        psh = [c8.enter_context(nc.psum_tensor(f"p8psh_{i}", [128, 512], F32))
               for i in range(8)]
        s_cc = p.mksem("p8_cc")
        s_w = [p.mksem(f"p8d_w{i}") for i in range(4)]
        s_pc = p.mksem("p8d_pc")
        s_mm = p.mksem("p8d_mm")
        s_x1 = [p.mksem(f"p8d_x1{i}") for i in range(4)]
        s_ev = p.mksem("p8d_ev")
        s_st = [p.mksem("p8d_sta"), p.mksem("p8d_stb")]
        p.s8_cc = s_cc

        with nc.Block() as blk:
            @blk.gpsimd
            def _(g):
                g.collective_compute(
                    "AllReduce", ALU.max,
                    replica_groups=[[0, 1, 2, 3, 4, 5, 6, 7]],
                    ins=[p.msc_dram.opt()], outs=[p.msc_out.opt()],
                ).then_inc(s_cc)
                for o in range(8):
                    for kg in range(8):
                        pi = o * 8 + kg
                        if pi >= 4:
                            g.wait_ge(s_pc, pi - 3)
                        g.dma_start(out=wfb[pi % 4][:],
                                    in_=io["wfT"][kg * 512:(kg + 1) * 512,
                                                  o * 512:(o + 1) * 512]
                                    .rearrange("(k q) f -> q k f", q=128)
                                    ).then_inc(s_w[pi % 4], 16)

            @blk.tensor
            def _(t):
                for o in range(8):
                    for kg in range(8):
                        pi = o * 8 + kg
                        t.wait_ge(s_w[pi % 4], (pi // 4 + 1) * 16)
                        if kg == 0 and o >= 1:
                            t.wait_ge(s_ev, o * 8)
                        for k4 in range(4):
                            k = kg * 4 + k4
                            st_, sp_ = (k == 0), (k == NKT - 1)
                            for tt in range(8):
                                mm = t.matmul(psh[tt][:],
                                              ln2[:, k * T + tt * 128:
                                                  k * T + (tt + 1) * 128],
                                              wfb[pi % 4][:, k4 * 512:(k4 + 1) * 512],
                                              start=st_, stop=sp_)
                        mm.then_inc(s_pc, 1)

            @blk.sync
            def _(s):
                for o in range(8):
                    for tt in range(8):
                        idx = o * 8 + tt
                        if idx >= 4:
                            s.wait_ge(s_ev, idx - 3)
                        s.dma_start(out=x1c[idx % 4][:],
                                    in_=p.x1_sp[tt * 128:(tt + 1) * 128,
                                                o * 512:(o + 1) * 512]
                                    ).then_inc(s_x1[idx % 4], 16)
                        s.wait_ge(s_ev, idx + 1)
                        s.dma_start(out=p.x2p_sp[tt * 128:(tt + 1) * 128,
                                                 o * 512:(o + 1) * 512],
                                    in_=x2c[tt % 2][:]).then_inc(s_st[idx % 2], 16)
                s.wait_ge(s_st[0], 32 * 16); s.wait_ge(s_st[1], 32 * 16)

            @blk.vector
            def _(v):
                for o in range(8):
                    v.wait_ge(s_pc, (o + 1) * 8)
                    for tt in range(8):
                        idx = o * 8 + tt
                        v.wait_ge(s_x1[idx % 4], (idx // 4 + 1) * 16)
                        if idx >= 2:
                            v.wait_ge(s_st[idx % 2], (idx // 2) * 16)
                        p.vf(v, v.tensor_add(tmpc[:], x1c[idx % 4][:],
                                             biasm[:, o * 512:(o + 1) * 512]))
                        v.tensor_add(x2c[tt % 2][:], tmpc[:],
                                     psh[tt][:]).then_inc(s_ev, 1)

    with ExitStack() as c8:
        sb = lambda n, s, d: c8.enter_context(nc.sbuf_tensor(n, s, d))
        mscs = sb("p8mscs", [2, 1], F32)
        rsc1 = sb("p8rsc1", [1, 2], F32)
        rscb = sb("p8rscb", [128, 2], F32)
        htok = sb("p8htok", [128, 4 * 8], F32)
        dtok8 = sb("p8dtok", [128, 4 * 8], F32)
        rtr = sb("p8rtr", [128, 8 * 8], F32)
        top8 = sb("p8top8", [128, 8], F32)
        wnb = sb("p8wnb", [128, 4 * 8], BF16)
        wTb = sb("p8wTb", [8, 4 * 128], BF16)
        limb = sb("p8limb", [8, 32], BF16)
        pmix = sb("p8pmix", [32, 512], F32)
        zpb = sb("p8zpb", [32, T], BF16)
        bmb = sb("p8bmb", [32, D], BF16)
        x2s = [sb(f"p8x2s_{i}", [128, 512], F32) for i in range(4)]
        outs = [sb(f"p8outs_{i}", [128, 512], F32) for i in range(2)]
        ps1 = c8.enter_context(nc.psum_tensor("p8e_ps1", [128, 512], F32))
        ps2 = c8.enter_context(nc.psum_tensor("p8e_ps2", [128, 512], F32))
        psw = c8.enter_context(nc.psum_tensor("p8e_psw", [8, 128], BF16))
        s_ld = p.mksem("p8e_ld")
        s_ldg = p.mksem("p8e_ldg")
        s_stp = p.mksem("p8e_stp")
        s_x2 = [p.mksem(f"p8e_x2{i}") for i in range(4)]
        s_ev = p.mksem("p8e_ev")
        s_mm = p.mksem("p8e_mm")
        s_st = [p.mksem("p8e_sta"), p.mksem("p8e_stb")]

        with nc.Block() as blk:
            @blk.sync
            def _(s):
                s.wait_ge(p.s8_cc, 1)
                s.dma_start(out=mscs[:], in_=p.msc_out).then_inc(s_ld, 16)
                s.wait_ge(s_ld, 16)
                s.wait_ge(s_ldg, 32)

            @blk.gpsimd
            def _(g):
                g.dma_start(out=limb[:], in_=io["limes"]).then_inc(s_ldg, 16)
                g.dma_start(out=bmb[:], in_=io["bmat"]).then_inc(s_ldg, 16)

        with nc.Block() as blk:
            @blk.vector
            def _(v):
                p.vf(v, v.tensor_scalar(mscs[:], mscs[:], 1e-6, None, ALU.max))
                p.vf(v, v.reciprocal(mscs[:], mscs[:]))
                v.tensor_scalar(mscs[:], mscs[:], 0.5, None, ALU.mult)

        with nc.Block() as blk:
            @blk.tensor
            def _(t):
                t.transpose(ps1[0:1, 0:2], mscs[:], p.idf32[0:2, 0:2]).then_inc(s_stp, 1)

            @blk.vector
            def _(v):
                v.wait_ge(s_stp, 1)
                v.tensor_copy(rsc1[:], ps1[0:1, 0:2])
        with nc.Block() as blk:
            @blk.tensor
            def _(t):
                t.matmul(ps1[:, 0:2], p.ones1[:], rsc1[:], start=True,
                         stop=True).then_inc(s_stp, 1)

            @blk.vector
            def _(v):
                v.wait_ge(s_stp, 2)
                v.tensor_copy(rscb[:], ps1[:, 0:2])

        s_t4 = p.mksem("p8e_t4")
        s_v4 = p.mksem("p8e_v4")
        with nc.Block() as blk:
            @blk.tensor
            def _(t):
                for j in range(4):
                    if j >= 1:
                        t.wait_ge(s_v4, j)
                    t.transpose(ps1[:, 0:8], hsl[:, j * 128:(j + 1) * 128],
                                p.idf32[0:8, 0:8])
                    t.transpose(ps2[:, 0:8], dsl[:, j * 128:(j + 1) * 128],
                                p.idf32[0:8, 0:8]).then_inc(s_t4, 1)

            @blk.vector
            def _(v):
                for j in range(4):
                    v.wait_ge(s_t4, j + 1)
                    v.tensor_copy(htok[:, j * 8:(j + 1) * 8], ps1[:, 0:8])
                    v.tensor_copy(dtok8[:, j * 8:(j + 1) * 8],
                                  ps2[:, 0:8]).then_inc(s_v4, 1)

        for j in range(4):
            with nc.Block() as blk:
                @blk.vector
                def _(v):
                    hj = htok[:, j * 8:(j + 1) * 8]
                    dj = dtok8[:, j * 8:(j + 1) * 8]
                    p.vf(v, v.tensor_scalar(rtr[:, 8:16], dj, rscb[:, 1:2],
                                            None, ALU.mult))
                    v.scalar_tensor_tensor(rtr[:, 0:8], hj, rscb[:, 0:1],
                                           rtr[:, 8:16], ALU.mult, ALU.add)
            with nc.Block() as blk:
                @blk.scalar
                def _(sc):
                    sc.activation(rtr[:, 16:24], rtr[:, 0:8], AF.Exp)
            with nc.Block() as blk:
                @blk.vector
                def _(v):
                    p.vf(v, v.reduce_sum(rtr[:, 24:25], rtr[:, 16:24], axis=AX.X))
                    p.vf(v, v.reciprocal(rtr[:, 25:26], rtr[:, 24:25]))
                    p.vf(v, v.tensor_scalar(rtr[:, 32:40], rtr[:, 16:24],
                                            rtr[:, 25:26], None, ALU.mult))
                    p.vf(v, v.max(top8[:], rtr[:, 32:40]))
                    v.tensor_scalar(rtr[:, 26:27], top8[:, 1:2], -2.0, None,
                                    ALU.mult)
            with nc.Block() as blk:
                @blk.scalar
                def _(sc):
                    sc.activation(rtr[:, 40:48], rtr[:, 32:40], AF.Sigmoid,
                                  scale=2.0, bias=rtr[:, 26:27])
            with nc.Block() as blk:
                @blk.vector
                def _(v):
                    p.vf(v, v.tensor_mul(rtr[:, 48:56], rtr[:, 32:40],
                                         rtr[:, 40:48]))
                    p.vf(v, v.reduce_sum(rtr[:, 24:25], rtr[:, 48:56], axis=AX.X))
                    p.vf(v, v.tensor_scalar(rtr[:, 24:25], rtr[:, 24:25], 1e-9,
                                            None, ALU.add))
                    p.vf(v, v.reciprocal(rtr[:, 25:26], rtr[:, 24:25]))
                    v.tensor_scalar(wnb[:, j * 8:(j + 1) * 8], rtr[:, 48:56],
                                    rtr[:, 25:26], None, ALU.mult)

        s_t6 = p.mksem("p8e_t6")
        s_v6 = p.mksem("p8e_v6")
        with nc.Block() as blk:
            @blk.tensor
            def _(t):
                for j in range(4):
                    if j >= 1:
                        t.wait_ge(s_v6, j)
                    t.transpose(psw[:], wnb[:, j * 8:(j + 1) * 8],
                                p.idbf[:]).then_inc(s_t6, 1)

            @blk.vector
            def _(v):
                for j in range(4):
                    v.wait_ge(s_t6, j + 1)
                    v.tensor_copy(wTb[:, j * 128:(j + 1) * 128],
                                  psw[:]).then_inc(s_v6, 1)
        with nc.Block() as blk:
            @blk.tensor
            def _(t):
                t.matmul(ps2[0:32, 0:512], limb[:], wTb[:], start=True,
                         stop=True).then_inc(s_t6, 1)

            @blk.vector
            def _(v):
                v.wait_ge(s_t6, 5)
                p.vf(v, v.tensor_copy(pmix[:], ps2[0:32, 0:512]))
                v.tensor_tensor(zpb[:], zbf[:], bc2(pmix[:], 512, 2), ALU.mult)

        with nc.Block() as blk:
            @blk.tensor
            def _(t):
                for tt in range(8):
                    for o in range(8):
                        idx = tt * 8 + o
                        dstp = ps1 if idx % 2 == 0 else ps2
                        if idx >= 2:
                            t.wait_ge(s_ev, idx - 1)
                        t.matmul(dstp[:], zpb[:, tt * 128:(tt + 1) * 128],
                                 bmb[:, o * 512:(o + 1) * 512],
                                 start=True, stop=True).then_inc(s_mm, 1)

            @blk.vector
            def _(v):
                for tt in range(8):
                    for o in range(8):
                        idx = tt * 8 + o
                        v.wait_ge(s_mm, idx + 1)
                        v.wait_ge(s_x2[idx % 4], (idx // 4 + 1) * 16)
                        if idx >= 2:
                            v.wait_ge(s_st[idx % 2], (idx // 2) * 16)
                        v.scalar_tensor_tensor(outs[idx % 2][:],
                                               (ps1 if idx % 2 == 0 else ps2)[:],
                                               0.5, x2s[idx % 4][:],
                                               ALU.mult, ALU.add).then_inc(s_ev, 1)

            @blk.sync
            def _(s):
                for tt in range(8):
                    for o in range(8):
                        idx = tt * 8 + o
                        if idx >= 4:
                            s.wait_ge(s_ev, idx - 3)
                        s.dma_start(out=x2s[idx % 4][:],
                                    in_=p.x2p_sp[tt * 128:(tt + 1) * 128,
                                                 o * 512:(o + 1) * 512]
                                    ).then_inc(s_x2[idx % 4], 16)
                        s.wait_ge(s_ev, idx + 1)
                        s.dma_start(out=io["out"][tt * 128:(tt + 1) * 128,
                                                  o * 512:(o + 1) * 512],
                                    in_=outs[idx % 2][:]).then_inc(s_st[idx % 2], 16)
                s.wait_ge(s_st[0], 32 * 16)
                s.wait_ge(s_st[1], 32 * 16)

    scope8.close()

    if "tap_x2p" in p.tap or "tap_z" in p.tap or "tap_hsl" in p.tap:
        sem = p.mksem("p8_tap")
        with nc.Block() as blk:
            @blk.sync
            def _(s):
                n = 0
                if "tap_x2p" in p.tap:
                    s.dma_start(out=p.tap["tap_x2p"], in_=p.x2p_sp).then_inc(sem, 16)
                    n += 16
                if "tap_msc" in p.tap:
                    s.dma_start(out=p.tap["tap_msc"], in_=p.msc_out).then_inc(sem, 16)
                    n += 16
                s.wait_ge(sem, n)


def finalphase(nc, p):
    pass


def prep_in_maps(inputs):
    f32 = np.float32
    x = np.ascontiguousarray(inputs["L0_attn_x"], dtype=f32)
    wiT = np.ascontiguousarray(inputs["L0_attn_in_proj_w"].T, dtype=f32)
    wiT_g = np.ascontiguousarray(wiT[:, 0:INTER])
    wiT_a = np.ascontiguousarray(wiT[:, INTER:])
    woT = np.ascontiguousarray(inputs["L0_attn_out_proj_w"].T, dtype=f32)
    wfT = np.ascontiguousarray(inputs["L0_ffn_W"].T, dtype=f32)
    conv_w = np.ascontiguousarray(inputs["L0_attn_conv_w"], dtype=f32).reshape(10240, 4)
    cwt = np.ascontiguousarray(conv_w.reshape(80, 128, 4).transpose(1, 0, 2)
                               .reshape(128, 320))
    cbt = np.ascontiguousarray(
        inputs["L0_attn_conv_b"].astype(f32).reshape(80, 128).T)
    nwt = np.ascontiguousarray(
        inputs["L0_attn_norm_w"].astype(f32).reshape(64, 128).T)
    alogc = np.ascontiguousarray(inputs["L0_attn_A_log"].astype(f32).reshape(128, 1))
    dtbc = np.ascontiguousarray(inputs["L0_attn_dt_bias"].astype(f32).reshape(128, 1))
    dvecc = np.ascontiguousarray(inputs["L0_attn_D"].astype(f32).reshape(128, 1))
    amat = np.ascontiguousarray(inputs["L0_ffn_A"], dtype=f32)
    bmat = np.ascontiguousarray(inputs["L0_ffn_Bmat"], dtype=f32)
    limes = np.ascontiguousarray(inputs["L0_ffn_LiMEs"], dtype=f32)
    limT = np.ascontiguousarray(limes.T)
    b = inputs["L0_ffn_b"].astype(f32)
    b8v = np.ascontiguousarray(b[0:8].reshape(8, 1))
    brow = np.ascontiguousarray(b.reshape(1, D))
    li = np.arange(128)
    mnegT = np.where(li[:, None] >= li[None, :], 0.0, NEG).astype(f32)
    idbf = np.eye(128, dtype=BF)
    idf32 = np.eye(128, dtype=f32)

    in_maps = []
    for core in range(8):
        bb, q = core // 4, core % 4
        t0 = q * 1024
        if q == 0:
            x_sl = np.concatenate([np.zeros((3, D), f32), x[bb, 0:1024]], axis=0)
        else:
            x_sl = np.ascontiguousarray(x[bb, t0 - 3: t0 + 1024])
        maskq = np.zeros((128, 4), f32)
        maskq[:, :q] = 1.0
        qsel = np.zeros((128, 4), f32)
        qsel[:, q] = 1.0
        in_maps.append({
            "x_sl": x_sl, "wiT_a": wiT_a, "wiT_g": wiT_g, "woT": woT,
            "wfT": wfT, "amat": amat, "bmat": bmat, "limes": limes,
            "limT": limT, "cwt": cwt, "cbt": cbt, "nwt": nwt,
            "alogc": alogc, "dtbc": dtbc, "dvecc": dvecc, "maskq": maskq,
            "qsel": qsel, "b8": b8v, "brow": brow, "mnegT": mnegT,
            "idbf": idbf, "idf32": idf32,
        })
    return in_maps


def kernel(**inputs):
    in_maps = prep_in_maps(inputs)
    nc = build_nc()
    res = run_bass_kernel_spmd(nc, in_maps, core_ids=list(range(8)))
    out = np.empty((2, 4096, D), np.float32)
    for core in range(8):
        bb, q = core // 4, core % 4
        out[bb, q * 1024:(q + 1) * 1024] = res.results[core]["out"]
    return out
